# version 08
# baseline (speedup 1.0000x reference)
import numpy as np
import ml_dtypes
import concourse.bass as bass
import concourse.mybir as mybir
from concourse.bass_utils import run_bass_kernel_spmd

F32 = mybir.dt.float32
BF16 = mybir.dt.bfloat16
AF = mybir.ActivationFunctionType
ALU = mybir.AluOpType
AX = mybir.AxisListType

NCORES = 8
D = 2048
H = 16
HD = 128
ROPE_D = 64
WIN = 512
SEQ = 8192
S_OWN = 1024
S_TOT = 1536
NST = 12
NSTO = 8
NDT = 16
HID = 5461
HIDP = 5504
NHT = 43
EPS = 1e-5
SM = float(1.0 / np.sqrt(HD))
NW_ELEM = HID * D

DEBUG = False
SCRATCH = 12288
NCOLSG = 54048
RACEDET = False
NDEV = NCORES = 8
PHASES = 9
P1LIM = 99
BARENG = "ytvsp"
STRIP = 0
Q2 = 63
ENGS = ("sync", "tensor", "vector", "scalar", "gpsimd")
ESEM = {"sync": "eY", "tensor": "eT", "vector": "eV", "scalar": "eS", "gpsimd": "eP"}


class Plan:
    def __init__(self, nc):
        self.nc = nc
        self.steps = {e: [] for e in ENGS}
        self.sems = {}
        self.counts = {}

    def sem(self, name):
        if name not in self.sems:
            self.sems[name] = self.nc.alloc_semaphore(name)
            self.counts[name] = 0
        return self.sems[name]

    def cnt(self, name):
        return self.counts[name]

    def step(self, eng, fn=None, waits=(), inc=None, incn=1, nochain=False):
        waits = list(waits)
        for s, v in waits:
            assert v <= self.counts[s], f"wait on future {s}:{v}>{self.counts[s]}"
        if inc is not None:
            self.sem(inc)
            self.counts[inc] += incn
        sems = self.sems

        def run(engine, fn=fn, waits=tuple(waits), inc=inc, incn=incn):
            for s, v in waits:
                if v > 0:
                    engine.wait_ge(sems[s], v)
            inst = fn(engine) if fn is not None else None
            if inc is not None:
                if inst is None:
                    engine.sem_inc(sems[inc], incn)
                else:
                    inst.then_inc(sems[inc], incn)
        self.steps[eng].append(run)
        return self

    def barrier(self, tiny=None):
        t = tiny or self.tinyops
        for e in ENGS:
            self.step(e, t.get(e), inc=ESEM[e])
        tot = [(ESEM[e], self.cnt(ESEM[e])) for e in ENGS]
        for e in ENGS:
            self.step(e, None, waits=tot)


def build_program():
    nc = bass.Bass("TRN2", target_bir_lowering=False, num_devices=NDEV,
                   dynamic_dma_scratch_size=SCRATCH, detect_race_conditions=RACEDET)

    x_d = nc.dram_tensor("x", [S_TOT, D], F32, kind="ExternalInput").ap()
    wq_d = nc.dram_tensor("wq", [D, D], F32, kind="ExternalInput").ap()
    wk_d = nc.dram_tensor("wk", [D, HD], F32, kind="ExternalInput").ap()
    wv_d = nc.dram_tensor("wv", [D, HD], F32, kind="ExternalInput").ap()
    wo_d = nc.dram_tensor("wo", [D, D], F32, kind="ExternalInput").ap()
    bo_d = nc.dram_tensor("bo", [128, D], F32, kind="ExternalInput").ap()
    rc_d = nc.dram_tensor("ropec", [S_TOT, 256], F32, kind="ExternalInput").ap()
    rs_d = nc.dram_tensor("ropes", [S_TOT, 256], F32, kind="ExternalInput").ap()
    ncr_d = nc.dram_tensor("ncr", [2, 2048], BF16, kind="ExternalInput").ap()
    w1_d = nc.dram_tensor("w1", [HIDP, D], F32, kind="ExternalInput").ap()
    w3_d = nc.dram_tensor("w3", [HIDP, D], F32, kind="ExternalInput").ap()
    w2t_d = nc.dram_tensor("w2t", [HIDP, D], F32, kind="ExternalInput").ap()
    out_d = nc.dram_tensor("out", [S_OWN, D], F32, kind="ExternalOutput").ap()
    x2_d = nc.dram_tensor("x2spill", [S_OWN, D], F32, kind="Internal").ap()
    if DEBUG:
        dbg_kps = nc.dram_tensor("dbg_kps", [128, 256], F32, kind="ExternalOutput").ap()
        dbg_ex = nc.dram_tensor("dbg_ex", [128, 1280], BF16, kind="ExternalOutput").ap()
        dbg_dn = nc.dram_tensor("dbg_dn", [128, 256], F32, kind="ExternalOutput").ap()
        dbg_av = nc.dram_tensor("dbg_av", [128, 256], F32, kind="ExternalOutput").ap()
        dbg_wk = nc.dram_tensor("dbg_wk", [128, NDT * HD], BF16, kind="ExternalOutput").ap()
        dbg_kb = nc.dram_tensor("dbg_kb", [128, HD], BF16, kind="ExternalOutput").ap()
        dbg_qt = nc.dram_tensor("dbg_qt", [128, H * S_OWN], BF16, kind="ExternalOutput").ap()
        dbg_kt = nc.dram_tensor("dbg_kt", [128, S_TOT], BF16, kind="ExternalOutput").ap()
        dbg_v = nc.dram_tensor("dbg_v", [128, NST * HD], BF16, kind="ExternalOutput").ap()
        dbg_avt = nc.dram_tensor("dbg_avt", [NSTO, 128 * H * 128], BF16, kind="ExternalOutput").ap()
        dbg_y = nc.dram_tensor("dbg_y", [NSTO, 128 * D], F32, kind="ExternalOutput").ap()
    w2q_d = nc.dram_tensor("w2q", [NHT, 128 * D], BF16, kind="Internal").ap()

    rr = np.arange(128)
    ident_t = nc.inline_tensor(np.eye(128, dtype=ml_dtypes.bfloat16), name="identc")
    ones_t = nc.inline_tensor(np.ones((128, 128), dtype=ml_dtypes.bfloat16), name="onesc")
    onesk1_t = nc.inline_tensor(np.ones((1, 128), dtype=ml_dtypes.bfloat16), name="onesk1c")
    onesk1f_t = nc.inline_tensor(np.ones((1, 128), dtype=np.float32), name="onesk1fc")
    onescol_t = nc.inline_tensor(np.ones((128, 1), dtype=np.float32), name="onescolc")
    mfar_t = nc.inline_tensor((rr[:, None] >= rr[None, :] + 1).astype(ml_dtypes.bfloat16), name="mfarc")
    mdia_t = nc.inline_tensor((rr[:, None] <= rr[None, :]).astype(ml_dtypes.bfloat16), name="mdiac")
    constvals = [0.0, 1.0, EPS, -0.5, SM, 0.125, 256.0, 1.0 / 2048.0]
    ct_t = nc.inline_tensor(np.tile(np.array(constvals, np.float32)[None, :], (128, 1)), name="ctensc")

    NCOLS = NCOLSG
    arena = nc.alloc_sbuf_tensor("arena", [128, NCOLS], F32)
    arenab = arena.bitcast(BF16)

    def f32(off, n, rows=128):
        assert off >= 0 and (off + n <= NCOLS or STRIP)
        return arena[0:rows, off:off + n] if off + n <= NCOLS else arena[0:rows, 0:n]

    def bf(off, n, rows=128):
        assert off >= 0 and (off + n <= NCOLS or STRIP)
        return (arenab[0:rows, 2 * off:2 * (off + n)] if off + n <= NCOLS
                else arenab[0:rows, 0:2 * n])

    csb = f32(0, 8)
    onescolb = f32(8, 1)
    thparts = f32(16, 3)
    thsum = f32(24, 3, rows=1)
    thsb = f32(32, 8)
    ngs = f32(40, 3)
    absacc = f32(48, 132)
    statb = f32(184, 24)
    mvb = f32(208, 8)
    rstdb = f32(216, 8)
    onesk1fb = f32(224, 128, rows=1)
    idsb = bf(352, 64)
    onesb = bf(416, 64)
    onesk1b = bf(480, 64, rows=1)
    mfarb = bf(544, 64)
    mdiab = bf(608, 64)
    ncrb0 = bf(672, 1024, rows=1)
    ncrb1 = bf(1696, 1024, rows=1)
    w2slb = lambda p: bf(2720 + p * 256, 256)
    knatb = lambda p: bf(3232 + p * 64, 64)
    SM_END = 3360

    WB = SM_END
    wbig_half = bf(WB, 8192)
    wbig = bf(WB, 16384)
    wtinA = lambda p: f32(WB + 8192 + p * 2048, 2048)
    q_wt = lambda p: f32(WB + p * 2048, 2048)
    q_b1 = bf(WB + 4096, 1024)
    q_b2 = bf(WB + 5120, 1024)
    q_tn1 = lambda p: bf(WB + 6144 + p * 1024, 1024)
    q_tn3 = lambda p: bf(WB + 8192 + p * 1024, 1024)
    q_tn2 = lambda p: bf(WB + 10240 + p * 1024, 1024)
    silu_t = f32(WB + 12288, 1024)
    osb = f32(WB, 16384)
    WB_END = WB + 16384

    A2 = WB_END
    xn1t = bf(A2, 12288)
    xn2t = bf(A2, 8192)
    stage1 = lambda p: bf(A2 + 8192 + p * 1024, 1024)
    stage3 = lambda p: bf(A2 + 10240 + p * 1024, 1024)
    x2tb = f32(A2 + 8192, 2048)
    xn2tmp = bf(A2 + 10240, 1024)
    x2re = lambda p: f32(A2 + p * 2048, 2048)
    x3tb = lambda p: f32(A2 + 4096 + p * 2048, 2048)
    A2_END = A2 + 12288

    A3 = A2_END
    qt_all = bf(A3, 8192)
    ktb = bf(A3 + 8192, 768)
    vsb = bf(A3 + 8960, 768)
    rcsb = f32(A3 + 9728, 3072)
    rssb = f32(A3 + 12800, 3072)
    wksb = bf(A3 + 15872, 1024)
    wvsb = bf(A3 + 16896, 1024)
    exptb = lambda p: bf(A3 + 17920 + p * 640, 640)
    recb = lambda p: f32(A3 + 19200 + p * 256, 256)
    avtb = bf(A3 + 19712, 1024)
    ytile = f32(A3 + 9728, 2048)
    xreb = lambda p: f32(A3 + 11776, 2048) if p == 0 else f32(A3 + 13824, 2048)
    bosb = f32(A3 + 15872, 2048)
    xinb = lambda p: f32(A3 + p * 2048, 2048)
    xn1tmpb = lambda p: bf(A3 + 4096 + p * 1024, 1024)
    qnatb = lambda p: bf(A3 + 17920 + p * 512, 512)
    rtmp = f32(A3 + 18944, 1024)
    gt = bf(A3, 22016)
    A3_END = A3 + 22016
    assert A3_END <= NCOLS or STRIP & 128, A3_END

    ps = nc.alloc_psum_tensor("ps", [128, 4096], F32)
    psb = ps.bitcast(BF16)

    if STRIP & 512:
        mm_a = bf(A2, 256)
        mm_ev = f32(A3, 24 * 512)
        dsem = nc.alloc_semaphore("dsem")
        tsem = nc.alloc_semaphore("tsem")
        ssem = nc.alloc_semaphore("ssem")
        osem = nc.alloc_semaphore("osem")
        with nc.Block() as block:
            @block.gpsimd
            def _(g):
                g.dma_start(out=mm_a, in_=wq_d[0:128, 0:512]).then_inc(dsem, 16)

            @block.tensor
            def _(t):
                t.wait_ge(dsem, 16)
                for it in range(24):
                    par = it % 2
                    reg = ps[:, par * 512:(par + 1) * 512]
                    if it >= 2:
                        t.wait_ge(ssem, it - 1)
                    last = None
                    for k in range(16):
                        t.ldweights(mm_a[:, 0:128])
                        last = t.matmul(reg, mm_a[:, 0:128], mm_a[:, 0:512],
                                        start=(k == 0), stop=(k == 15),
                                        skip_group_check=True)
                    last.then_inc(tsem, 1)

            @block.scalar
            def _(s):
                for it in range(24):
                    par = it % 2
                    s.wait_ge(tsem, it + 1)
                    s.activation(mm_ev[:, it * 512:(it + 1) * 512],
                                 ps[:, par * 512:(par + 1) * 512],
                                 AF.Copy).then_inc(ssem, 1)

            @block.sync
            def _(sy):
                sy.wait_ge(ssem, 24)
                sy.dma_start(out=out_d[0:128, 0:512], in_=mm_ev[:, 0:512]).then_inc(osem, 16)
                sy.wait_ge(osem, 16)
        return nc

    P = Plan(nc)
    for s in ("dC", "dB", "dD", "dX0", "dX1", "dA0", "dA1", "dR0", "dR1", "dP",
              "dW0", "dW1", "dQ", "dL0", "dL1", "dE0", "dE1", "dG", "dO",
              "eY", "eT", "eV", "eS", "eP"):
        P.sem(s)
    P.tinyops = {
        "tensor": lambda e: e.ldweights(idsb),
        "vector": lambda e: e.tensor_copy(statb[:, 0:1], csb[:, 0:1]),
        "scalar": lambda e: e.activation(statb[:, 1:2], csb[:, 0:1], AF.Copy),
        "gpsimd": lambda e: e.tensor_copy(statb[:, 2:3], csb[:, 0:1]),
    }

    if STRIP & 1024:
        if STRIP & 8192:
            for i, v in enumerate(constvals):
                nc.const_aps.register(F32, v, csb[:, i:i + 1])
        mm_a = bf(A2, 256)
        mm_ev = f32(A3, 24 * 512)
        P.step("gpsimd", lambda e: e.dma_start(out=mm_a, in_=wq_d[0:128, 0:512]),
               inc="dG", incn=16)
        c_ld = P.cnt("dG")
        prev_ev = {}
        for it in range(24):
            par = it % 2
            reg = ps[:, par * 512:(par + 1) * 512]
            w = [("dG", c_ld)] if (it == 0 or STRIP & 4096) else []
            if it >= 2:
                w.append(("eS", prev_ev[it - 2]))

            def mk_g(reg=reg):
                def f(e):
                    last = None
                    for k in range(16):
                        e.ldweights(mm_a[:, 0:128])
                        last = e.matmul(reg, mm_a[:, 0:128], mm_a[:, 0:512],
                                        start=(k == 0), stop=(k == 15),
                                        skip_group_check=True)
                    return last
                return f
            P.step("tensor", mk_g(), waits=w, inc="eT")
            c_g = P.cnt("eT")
            P.step("scalar", lambda e, it=it, reg=reg: e.activation(
                mm_ev[:, it * 512:(it + 1) * 512], reg, AF.Copy),
                waits=[("eT", c_g)], inc="eS")
            prev_ev[it] = P.cnt("eS")
        if STRIP & 2048 and BARENG == "L":
            P.step("tensor", lambda e: e.ldweights(idsb), inc="eT")
        elif STRIP & 2048 and BARENG == "T":
            P.step("tensor", None, inc="eT")
        elif STRIP & 2048 and BARENG == "W":
            P.step("tensor", None, waits=[("eT", 24)])
        elif STRIP & 2048:
            emap = {"y": "sync", "t": "tensor", "v": "vector", "s": "scalar", "p": "gpsimd"}
            sub = [emap[c] for c in BARENG]
            for e in sub:
                P.step(e, P.tinyops.get(e), inc=ESEM[e])
            tot = [(ESEM[e], P.cnt(ESEM[e])) for e in sub]
            for e in sub:
                P.step(e, None, waits=tot)
        if not (STRIP & 16384):
            P.step("sync", lambda e: e.dma_start(out=out_d[0:128, 0:512], in_=mm_ev[:, 0:512]),
                   waits=[("eS", prev_ev[23])], inc="dO", incn=16)
            P.step("sync", None, waits=[("dO", 16)])
        _replay(nc, P)
        return nc

    def dma(eng, out, in_, waits=(), sem="dS"):
        P.step(eng, lambda e, o=out, i=in_: e.dma_start(out=o, in_=i),
               waits=waits, inc=sem, incn=16)
        return P.cnt(sem)

    def ln_moments(buf, mva, sc, waits_first):
        def vstep(fn):
            P.step("vector", fn, waits=[("eV", P.cnt("eV"))], inc="eV")
        P.step("vector", lambda e, b=buf: e.bn_stats(statb[:, 0:6], b[:, 0:512]),
               waits=waits_first)
        for j in range(1, 4):
            P.step("vector", lambda e, b=buf, j=j: e.bn_stats(
                statb[:, 6 * j:6 * j + 6], b[:, 512 * j:512 * (j + 1)]),
                inc="eV" if j == 3 else None)
        vstep(lambda e, sc=sc: e.tensor_tensor(sc, statb, statb, ALU.mult))
        for (lo, mid, hi) in ((0, 12, 24), (0, 6, 12), (0, 3, 6)):
            vstep(lambda e, lo=lo, mid=mid, hi=hi: e.tensor_tensor(
                statb[:, lo:mid], statb[:, lo:mid], statb[:, mid:hi], ALU.add))
            vstep(lambda e, sc=sc, lo=lo, mid=mid, hi=hi: e.tensor_tensor(
                sc[:, lo:mid], sc[:, lo:mid], sc[:, mid:hi], ALU.add))
        vstep(lambda e, mva=mva: e.tensor_tensor(
            mvb[:, mva:mva + 1], statb[:, 1:2], csb[:, 5:6], ALU.mult))
        vstep(lambda e, sc=sc: e.tensor_tensor(
            sc[:, 1:2], sc[:, 1:2], csb[:, 6:7], ALU.mult))
        vstep(lambda e, sc=sc, mva=mva: e.tensor_tensor(
            sc[:, 3:4], mvb[:, mva:mva + 1], mvb[:, mva:mva + 1], ALU.mult))
        vstep(lambda e, sc=sc: e.tensor_tensor(
            sc[:, 2:3], sc[:, 1:2], statb[:, 2:3], ALU.add))
        vstep(lambda e, sc=sc: e.tensor_tensor(
            sc[:, 0:1], sc[:, 2:3], csb[:, 7:8], ALU.mult))
        vstep(lambda e, sc=sc, mva=mva: e.tensor_tensor(
            mvb[:, mva + 1:mva + 2], sc[:, 0:1], sc[:, 3:4], ALU.subtract))

    clist = [(ct_t, csb), (onescol_t, onescolb), (onesk1f_t, onesk1fb),
             (ident_t, idsb), (ones_t, onesb), (onesk1_t, onesk1b),
             (mfar_t, mfarb), (mdia_t, mdiab)]
    if STRIP & 1:
        clist = [(ident_t, idsb)]
    for t, dst in clist:
        dma("sync", dst, t.ap(), sem="dC")
    if not (STRIP & 1):
        dma("sync", ncrb0, ncr_d[0:1, :], sem="dC")
        dma("sync", ncrb1, ncr_d[1:2, :], sem="dC")
    for t in range(0 if STRIP & 1 else NST):
        dma("sync", rcsb[:, t * 256:(t + 1) * 256],
            rc_d[t * 128:(t + 1) * 128, :], sem="dC")
        dma("sync", rssb[:, t * 256:(t + 1) * 256],
            rs_d[t * 128:(t + 1) * 128, :], sem="dC")
    c_consts = P.cnt("dC")
    for i, v in enumerate(constvals):
        nc.const_aps.register(F32, v, csb[:, i:i + 1])

    for dt in range(0 if (PHASES == 130 or STRIP & 2) else NDT):
        dma("gpsimd", wksb[:, dt * HD:(dt + 1) * HD],
            wk_d[dt * 128:(dt + 1) * 128, :], sem="dG")
        dma("gpsimd", wvsb[:, dt * HD:(dt + 1) * HD],
            wv_d[dt * 128:(dt + 1) * 128, :], sem="dG")
    c_kvw = P.cnt("dG")
    wq_r = wq_d.rearrange("(dt p) (hf f) -> p dt hf f", p=128, f=1024)
    for dt in range(0 if (PHASES == 130 or STRIP & 2) else NDT):
        dma("gpsimd", wbig_half[:, dt * 1024:(dt + 1) * 1024],
            wq_r[:, dt, 0, :], sem="dG")
    c_wq = [P.cnt("dG"), None]
    if PHASES <= -1:
        P.step("sync", None, waits=[("dC", c_consts)])
        P.step("gpsimd", None, waits=[("dG", P.cnt("dG"))])
        _replay(nc, P)
        return nc
    if PHASES <= 0:
        P.step("sync", None, waits=[("dC", c_consts)])
        P.step("gpsimd", None, waits=[("dG", P.cnt("dG"))])
        P.barrier()
        _replay(nc, P)
        return nc
    if DEBUG >= 3:
        dma("sync", dbg_wk, wksb, [("dG", c_wq[0])], sem="dD")

    if STRIP & 128:
        mm_a = bf(A2, 256)
        mm_ev = f32(A3, 24 * 512)
        dma("gpsimd", mm_a, wq_d[0:128, 0:512], sem="dG")
        c_ld = P.cnt("dG")
        prev_ev = {}
        for it in range(24):
            par = it % 2
            reg = ps[:, par * 512:(par + 1) * 512]
            w = [("dG", c_ld)]
            if it >= 2:
                w.append(("eS", prev_ev[it - 2]))

            def mk_g(reg=reg):
                def f(e):
                    last = None
                    for k in range(16):
                        e.ldweights(mm_a[:, 0:128])
                        last = e.matmul(reg, mm_a[:, 0:128], mm_a[:, 0:512],
                                        start=(k == 0), stop=(k == 15),
                                        skip_group_check=True)
                    return last
                return f
            P.step("tensor", mk_g(), waits=w, inc="eT")
            c_g = P.cnt("eT")
            P.step("scalar", lambda e, it=it, reg=reg: e.activation(
                mm_ev[:, it * 512:(it + 1) * 512], reg, AF.Copy),
                waits=[("eT", c_g)], inc="eS")
            prev_ev[it] = P.cnt("eS")
        P.barrier()
        _replay(nc, P)
        return nc

    pa = {"i": 0, "dma": [], "done": []}
    PA_TOTAL = 3 * NHT

    def pa_src(slot):
        if slot < NHT:
            return w1_d[slot * 128:(slot + 1) * 128, :]
        if slot < 2 * NHT:
            t = slot - NHT
            return w3_d[t * 128:(t + 1) * 128, :]
        t = slot - 2 * NHT
        return w2t_d[t * 128:(t + 1) * 128, :]

    def pa_step():
        if STRIP & 16:
            return
        i = pa["i"]
        if i >= PA_TOTAL:
            return
        pa["i"] = i + 1
        buf = wtinA(i % 2)
        w = []
        if i >= 2:
            w = [pa["done"][i - 2]]
        sA = "dA%d" % (i % 2)
        c = dma("sync", buf, pa_src(i), w, sem=sA)
        pa["dma"].append((sA, c))
        cw = [(sA, c), ("dC", c_consts)]
        if i % 2 == 0:
            P.step("scalar", lambda e, b=buf, s=i: e.activation(
                b, b, AF.Abs, accum_out=absacc[:, s:s + 1]), waits=cw, inc="eS")
            pa["done"].append(("eS", P.cnt("eS")))
        else:
            P.step("vector", lambda e, b=buf, s=i: e.tensor_reduce(
                absacc[:, s:s + 1], b, AX.X, ALU.add, apply_absolute_value=True),
                waits=cw, inc="eV")
            pa["done"].append(("eV", P.cnt("eV")))

    c_apply = []
    c_trlast = []
    p1g = []
    for i in range(min(NST, P1LIM)):
        buf = xinb(i % 2)
        w = []
        if i >= 2:
            w.append(("eV", c_apply[i - 2]))
        sX = "dX%d" % (i % 2)
        if PHASES in (127, 128, 129, 130, 131):
            c_xd = 0
        else:
            c_xd = dma("sync", buf, x_d[i * 128:(i + 1) * 128, :], w, sem=sX)
        if PHASES not in (11, 12, 121, 122, 123, 124, 125, 126, 127, 128, 129, 130, 131):
            pa_step()
            pa_step()
        if PHASES in (126, 127, 128, 129, 130, 131):
            if not (STRIP & 4):
                P.step("vector", lambda e, b=buf: e.tensor_copy(statb[:, 0:24], b[:, 0:24]),
                       waits=([(sX, c_xd)] if PHASES == 126 else []) + [("eV", P.cnt("eV"))], inc="eV")
        else:
            ln_moments(buf, 0, f32(A3 + 17920, 24),
                       [(sX, c_xd), ("dC", c_consts), ("eV", P.cnt("eV"))])
        c_ag = P.cnt("eV")
        P.step("scalar", lambda e: e.activation(rstdb[:, 0:1], mvb[:, 1:2], AF.Ln, bias=EPS),
               waits=[("eV", c_ag)])
        P.step("scalar", lambda e: e.activation(rstdb[:, 0:1], rstdb[:, 0:1], AF.Exp, scale=-0.5),
               waits=[("eS", P.cnt("eS"))], inc="eS")
        c_rs = P.cnt("eS")
        xo = xn1tmpb(i % 2)
        if (PHASES in (125, 126, 127, 128, 130, 131)) and not (STRIP & 4):
            aw = [("eV", c_ag)]
            if i >= 2 and len(c_trlast) > i - 2:
                aw.append(("eT", c_trlast[i - 2]))
            P.step("vector", lambda e, b=buf, xo=xo: e.tensor_scalar(
                xo, b, 1.0, None, ALU.mult), waits=aw, inc="eV")
            c_apply.append(P.cnt("eV"))
        elif PHASES == 129 or (STRIP & 4):
            c_apply.append(P.cnt("eV"))
        aw = [("eS", c_rs)]
        if i >= 2 and len(c_trlast) > i - 2:
            aw.append(("eT", c_trlast[i - 2]))
        if PHASES not in (125, 126, 127, 128, 129, 130, 131):
            P.step("vector", lambda e, b=buf, xo=xo: e.tensor_scalar(
                xo, b, mvb[:, 0:1], rstdb[:, 0:1], ALU.subtract, ALU.mult),
                waits=aw, inc="eV")
            c_apply.append(P.cnt("eV"))
        for g in range(0 if PHASES == 11 else 4):
            gid = i * 4 + g
            reg = ps[:, (gid % 2) * 512:(gid % 2) * 512 + 512]
            tw = [("eV", c_apply[i])] if c_apply[i] else []
            if gid == 0:
                tw.append(("dC", c_consts))
            back = 1 if PHASES == 123 else 2
            if gid >= back:
                pv = p1g[gid - back]
                tw.append(pv if isinstance(pv, tuple) else ("eS", pv))
            if PHASES in (128, 129, 130, 131):
                tw = [("dC", c_consts)] if gid == 0 else []

            def mk_tr1(xo=xo, g=g, reg=reg):
                def f(e):
                    last = None
                    for b in range(4):
                        dt = g * 4 + b
                        e.ldweights(xo[:, dt * 128:(dt + 1) * 128])
                        last = e.matmul(reg[:, b * 128:(b + 1) * 128],
                                        xo[:, dt * 128:(dt + 1) * 128], idsb,
                                        start=True, stop=True, skip_group_check=True)
                    return last
                return f
            if PHASES == 121:
                tw = [x for x in tw if x[0] != "eS"]
            P.step("tensor", mk_tr1(), waits=tw, inc="eT")
            c_tr = P.cnt("eT")

            def mk_ev1(g=g, reg=reg, i=i):
                def f(e):
                    last = None
                    for b in range(4):
                        dt = g * 4 + b
                        last = e.activation(
                            xn1t[:, dt * S_TOT + i * 128: dt * S_TOT + (i + 1) * 128],
                            reg[:, b * 128:(b + 1) * 128], AF.Copy)
                    return last
                return f
            P.step("scalar", mk_ev1(), waits=[("eT", c_tr)], inc="eS")
            p1g.append(P.cnt("eS"))
        c_trlast.append(P.cnt("eT"))
        if PHASES not in (11, 12, 121, 122, 123, 124, 125, 126, 127, 128, 129, 130, 131):
            pa_step()
    if not (STRIP & 8):
        P.barrier()
    if PHASES <= 1 or PHASES in (11, 12, 121, 122, 123, 124, 125, 126, 127, 128, 129, 130, 131):
        _replay(nc, P)
        return nc

    q_ev = []
    q_tr = []
    kv_ev = []
    kv_rv = []
    k_tr = []
    p2g = []
    qn = 0
    last_rope = 0

    for h2 in range(1 if STRIP & 32 else 2):
        if h2 == 1:
            wqw = [("eT", P.cnt("eT"))]
            for dt in range(NDT):
                dma("gpsimd", wbig_half[:, dt * 1024:(dt + 1) * 1024],
                    wq_r[:, dt, 1, :], wqw if dt == 0 else (), sem="dG")
            c_wq[1] = P.cnt("dG")
        for i in range(NST):
            own = i >= 4
            if (not own) and h2 == 1:
                continue
            psQ = None
            psKV = None
            mw = [("dG", c_wq[h2])]
            if own:
                par = qn % 2
                psQ = ps[:, par * 1024:(par + 1) * 1024]
                if qn >= 2:
                    for sname, v in (("eV", q_ev[qn - 2][0]), ("eS", q_ev[qn - 2][1])):
                        if v:
                            mw.append((sname, v))
            if h2 == 0:
                kpar = i % 2
                kvbase = 2048 if kpar == 0 else 3584
                psKV = ps[:, kvbase:kvbase + 256]
                if i >= 2 and kv_ev[i - 2]:
                    mw += [("eS", kv_ev[i - 2]), ("eV", kv_rv[i - 2])]

            def mk_mm(i=i, h2=h2, own=own, psQ=psQ, psKV=psKV):
                def f(e):
                    last = None
                    for dt in range(NDT):
                        lhs = xn1t[:, dt * S_TOT + i * 128: dt * S_TOT + (i + 1) * 128]
                        e.ldweights(lhs)
                        if own and (Q2 & 4096):
                            last = e.matmul(psQ[:, 0:512], lhs,
                                            wbig_half[:, dt * 1024:dt * 1024 + 512],
                                            start=(dt == 0), stop=(dt == NDT - 1),
                                            skip_group_check=True)
                            continue
                        if own:
                            for c in range(2):
                                last = e.matmul(psQ[:, c * 512:(c + 1) * 512], lhs,
                                                wbig_half[:, dt * 1024 + c * 512:dt * 1024 + (c + 1) * 512],
                                                start=(dt == 0), stop=(dt == NDT - 1),
                                                skip_group_check=True)
                        if h2 == 0 and not (STRIP & 64):
                            last = e.matmul(psKV[:, 0:128], lhs, wksb[:, dt * HD:(dt + 1) * HD],
                                            start=(dt == 0), stop=(dt == NDT - 1),
                                            skip_group_check=True)
                            last = e.matmul(psKV[:, 128:256], lhs, wvsb[:, dt * HD:(dt + 1) * HD],
                                            start=False, stop=(dt == NDT - 1),
                                            skip_group_check=True)
                    return last
                return f
            P.step("tensor", mk_mm(), waits=mw, inc="eT")
            c_mm = P.cnt("eT")

            if PHASES == 23:
                q_ev.append((0, 0))
                q_tr.append(0)
                if own:
                    qn += 1
                if h2 == 0:
                    kv_ev.append(0)
                    kv_rv.append(0)
                    k_tr.append(0)
                pa_step()
                pa_step()
                continue

            if own and not (Q2 & 1):
                q_ev.append((0, 0))
                if not (Q2 & 2):
                    q_tr.append(0)
                qn += 1
            if h2 == 0 and not (Q2 & 4):
                kv_ev.append(0)
                kv_rv.append(0)
                if not (Q2 & 8):
                    k_tr.append(0)
            if own and (Q2 & 1024):
                def mk_cpv(psQ=psQ):
                    def f(e):
                        e.tensor_copy(qnatb(0)[:, 0:512], psQ[:, 0:512])
                        return e.tensor_copy(qnatb(1)[:, 0:512], psQ[:, 512:1024])
                    return f
                P.step("vector", mk_cpv(), waits=[("eT", c_mm)], inc="eV")
                q_ev.append((P.cnt("eV"), 0))
                q_tr.append(0)
                qn += 1
            if own and (Q2 & 512):
                def mk_cp512(psQ=psQ):
                    def f(e):
                        e.activation(qnatb(0)[:, 0:512], psQ[:, 0:512], AF.Copy)
                        return e.activation(qnatb(1)[:, 0:512], psQ[:, 512:1024], AF.Copy)
                    return f
                P.step("scalar", mk_cp512(), waits=[("eT", c_mm)], inc="eS")
                q_ev.append((0, P.cnt("eS")))
                q_tr.append(0)
                qn += 1
            if own and (Q2 & 256):
                P.step("scalar", lambda e, psQ=psQ: e.activation(
                    qnatb(0), psQ, AF.Copy), waits=[("eT", c_mm)], inc="eS")
                q_ev.append((0, P.cnt("eS")))
                q_tr.append(0)
                qn += 1
            if own and (Q2 & 64):
                def mk_qrest64(qb=qb if False else None, psQ=psQ):
                    def f(e):
                        last = None
                        for hh in range(8):
                            last = e.activation(
                                qnatb(0)[:, hh * 128 + 64:(hh + 1) * 128],
                                psQ[:, hh * 128 + 64:(hh + 1) * 128], AF.Copy)
                        return last
                    return f
                P.step("scalar", mk_qrest64(), waits=[("eT", c_mm)], inc="eS")
                q_ev.append((0, P.cnt("eS")))
                q_tr.append(0)
                qn += 1
            if h2 == 0 and (Q2 & 128):
                kb = knatb(i % 2)
                kx1, kx2 = psKV[:, 0:32], psKV[:, 32:64]
                cosk = rcsb[:, i * 256:i * 256 + 32]
                sink = rssb[:, i * 256:i * 256 + 32]

                def mk_kr128(kb=kb, kx1=kx1, kx2=kx2, cosk=cosk, sink=sink):
                    def f(e):
                        e.tensor_tensor(rtmp[:, 0:32], kx2, sink, ALU.mult)
                        e.tensor_tensor(rtmp[:, 32:64], kx1, cosk, ALU.mult)
                        e.tensor_tensor(rtmp[:, 64:96], kx1, sink, ALU.mult)
                        e.tensor_tensor(rtmp[:, 96:128], kx2, cosk, ALU.mult)
                        e.tensor_tensor(kb[:, 0:32], rtmp[:, 32:64], rtmp[:, 0:32], ALU.subtract)
                        return e.tensor_tensor(kb[:, 32:64], rtmp[:, 64:96], rtmp[:, 96:128], ALU.add)
                    return f
                P.step("vector", mk_kr128(), waits=[("eT", c_mm)], inc="eV")
                kv_ev.append(0)
                kv_rv.append(P.cnt("eV"))
                k_tr.append(0)
            if own and (Q2 & 1):
                qb = qnatb(par)
                cosm = rcsb[:, i * 256:(i + 1) * 256].rearrange("p (h c) -> p h c", c=32)
                sinm = rssb[:, i * 256:(i + 1) * 256].rearrange("p (h c) -> p h c", c=32)
                pq3 = psQ.rearrange("p (h c) -> p h c", c=128)
                x1, x2p = pq3[:, :, 0:32], pq3[:, :, 32:64]
                rt = lambda k: rtmp[:, k * 256:(k + 1) * 256].rearrange("p (h c) -> p h c", c=32)
                rw = [("eT", c_mm), ("eV", last_rope)]
                if qn >= 2:
                    rw.append(("eT", q_tr[qn - 2]))
                if PHASES == 21:
                    P.step("vector", lambda e, qb=qb, psQ=psQ: e.tensor_copy(qb, psQ), waits=rw, inc="eV")
                    c_rope = P.cnt("eV")
                    last_rope = c_rope
                    P.step("scalar", None, inc="eS")
                    c_qrest = P.cnt("eS")
                    q_ev.append((c_rope, c_qrest))
                elif PHASES == 25 or PHASES >= 9 or PHASES in (2, 3, 4, 22):
                    def mk_ropeq_mul(psQ=psQ, i=i):
                        def f(e):
                            cs = rcsb[:, i * 256:(i + 1) * 256]
                            sn = rssb[:, i * 256:(i + 1) * 256]
                            last = None
                            for hh in range(8):
                                x1h = psQ[:, hh * 128:hh * 128 + 32]
                                x2h = psQ[:, hh * 128 + 32:hh * 128 + 64]
                                ch = cs[:, hh * 32:(hh + 1) * 32]
                                sh = sn[:, hh * 32:(hh + 1) * 32]
                                b = hh * 128
                                e.tensor_tensor(rtmp[:, b:b + 32], x2h, sh, ALU.mult)
                                e.tensor_tensor(rtmp[:, b + 32:b + 64], x1h, ch, ALU.mult)
                                e.tensor_tensor(rtmp[:, b + 64:b + 96], x1h, sh, ALU.mult)
                                last = e.tensor_tensor(rtmp[:, b + 96:b + 128], x2h, ch, ALU.mult)
                            return last
                        return f

                    def mk_ropeq_comb(qb=qb):
                        def f(e):
                            last = None
                            for hh in range(8):
                                b = hh * 128
                                e.tensor_tensor(qb[:, b:b + 32],
                                                rtmp[:, b + 32:b + 64], rtmp[:, b:b + 32], ALU.subtract)
                                last = e.tensor_tensor(qb[:, b + 32:b + 64],
                                                       rtmp[:, b + 64:b + 96], rtmp[:, b + 96:b + 128], ALU.add)
                            return last
                        return f
                    if not (Q2 & 16384):
                        P.step("vector", mk_ropeq_mul(), waits=rw, inc="eV")
                        P.step("vector", mk_ropeq_comb(), waits=[("eV", P.cnt("eV"))], inc="eV")
                    c_rope = P.cnt("eV")
                    last_rope = c_rope
                else:
                    P.step("vector", lambda e, a=x2p, b=sinm, o=rt(0): e.tensor_tensor(o, a, b, ALU.mult), waits=rw)
                    P.step("vector", lambda e, a=x1, b=cosm, o=rt(1): e.tensor_tensor(o, a, b, ALU.mult))
                    P.step("vector", lambda e, a=x1, b=sinm, o=rt(2): e.tensor_tensor(o, a, b, ALU.mult))
                    P.step("vector", lambda e, a=x2p, b=cosm, o=rt(3): e.tensor_tensor(o, a, b, ALU.mult), inc="eV")
                    c_rt = P.cnt("eV")
                    qb3 = qb.rearrange("p (h c) -> p h c", c=128)
                    P.step("vector", lambda e, o=qb3[:, :, 0:32], a=rt(1), b=rt(0): e.tensor_tensor(
                        o, a, b, ALU.subtract), waits=[("eV", c_rt)])
                    P.step("vector", lambda e, o=qb3[:, :, 32:64], a=rt(2), b=rt(3): e.tensor_tensor(
                        o, a, b, ALU.add), inc="eV")
                if PHASES != 21:
                    c_rope = P.cnt("eV")
                    last_rope = c_rope

                    def mk_qrest(qb=qb, psQ=psQ):
                        def f(e):
                            last = None
                            for hh in range(8):
                                last = e.activation(qb[:, hh * 128 + 64:(hh + 1) * 128],
                                                    psQ[:, hh * 128 + 64:(hh + 1) * 128],
                                                    AF.Copy)
                            return last
                        return f
                    if not (Q2 & 8192):
                        qw = [("eT", c_mm), ("eV", c_rope)]
                        P.step("scalar", mk_qrest(), waits=qw, inc="eS")
                    c_qrest = P.cnt("eS")
                    q_ev.append((c_rope, c_qrest))
                for g in range(2 if (Q2 & 2) else 0):
                    gid = len(p2g)
                    reg = ps[:, 2560 + (gid % 2) * 512:2560 + (gid % 2) * 512 + 512]
                    tw = [("eV", c_rope), ("eS", c_qrest)]
                    if gid >= 2:
                        tw.append(("eS", p2g[gid - 2]))

                    def mk_tr2(qb=qb, g=g, reg=reg):
                        def f(e):
                            last = None
                            for b in range(4):
                                hh = g * 4 + b
                                e.ldweights(qb[:, hh * 128:(hh + 1) * 128])
                                last = e.matmul(reg[:, b * 128:(b + 1) * 128],
                                                qb[:, hh * 128:(hh + 1) * 128], idsb,
                                                start=True, stop=True, skip_group_check=True)
                            return last
                        return f
                    P.step("tensor", mk_tr2(), waits=tw, inc="eT")
                    c_t2 = P.cnt("eT")

                    def mk_ev2(g=g, reg=reg, i=i, h2=h2):
                        def f(e):
                            last = None
                            for b in range(4):
                                h = h2 * 8 + g * 4 + b
                                last = e.activation(
                                    qt_all[:, h * S_OWN + (i - 4) * 128: h * S_OWN + (i - 3) * 128],
                                    reg[:, b * 128:(b + 1) * 128], AF.Copy)
                            return last
                        return f
                    P.step("scalar", mk_ev2(), waits=[("eT", c_t2)], inc="eS")
                    p2g.append(P.cnt("eS"))
                if not (Q2 & 2):
                    pass
                q_tr.append(P.cnt("eT"))
                qn += 1

            if h2 == 0 and (Q2 & 4):
                kb = knatb(i % 2)
                cosk = rcsb[:, i * 256:i * 256 + 32]
                sink = rssb[:, i * 256:i * 256 + 32]
                kx1, kx2 = psKV[:, 0:32], psKV[:, 32:64]
                krw = [("eT", c_mm), ("eV", last_rope)]
                if i >= 2:
                    krw.append(("eT", k_tr[i - 2]))
                    if DEBUG >= 3:
                        krw.append(("dD", P.cnt("dD")))
                P.step("vector", lambda e, a=kx2, b=sink: e.tensor_tensor(rtmp[:, 0:32], a, b, ALU.mult), waits=krw)
                P.step("vector", lambda e, a=kx1, b=cosk: e.tensor_tensor(rtmp[:, 32:64], a, b, ALU.mult))
                P.step("vector", lambda e, a=kx1, b=sink: e.tensor_tensor(rtmp[:, 64:96], a, b, ALU.mult))
                P.step("vector", lambda e, a=kx2, b=cosk: e.tensor_tensor(rtmp[:, 96:128], a, b, ALU.mult), inc="eV")
                c_kt4 = P.cnt("eV")
                P.step("vector", lambda e, kb=kb: e.tensor_tensor(kb[:, 0:32], rtmp[:, 32:64],
                                                                  rtmp[:, 0:32], ALU.subtract),
                       waits=[("eV", c_kt4)])
                P.step("vector", lambda e, kb=kb: e.tensor_tensor(kb[:, 32:64], rtmp[:, 64:96],
                                                                  rtmp[:, 96:128], ALU.add), inc="eV")
                kv_rv.append(P.cnt("eV"))
                last_rope = P.cnt("eV")
                if DEBUG >= 3 and i == 5:
                    P.step("scalar", lambda e, psKV=psKV: e.activation(
                        silu_t[:, 0:256], psKV[:, 0:256], AF.Copy), waits=[("eT", c_mm)], inc="eS")
                    dma("sync", dbg_kps, silu_t[:, 0:256], [("eS", P.cnt("eS"))], sem="dD")
                P.step("scalar", lambda e, kb=kb, psKV=psKV: e.activation(
                    kb[:, 64:128], psKV[:, 64:128], AF.Copy),
                    waits=[("eT", c_mm), ("eV", kv_rv[i])])
                P.step("scalar", lambda e, i=i, psKV=psKV: e.activation(
                    vsb[:, i * 128:(i + 1) * 128], psKV[:, 128:256], AF.Copy), inc="eS")
                kv_ev.append(P.cnt("eS"))
                if DEBUG >= 3 and i == 5:
                    dma("sync", dbg_kb, kb, [("eV", kv_rv[i]), ("eS", kv_ev[i])], sem="dD")
                if not (Q2 & 8):
                    k_tr.append(0)
                    p2g.append(p2g[-1] if p2g else 0)
                    pa_step()
                    pa_step()
                    continue
                gid = len(p2g)
                reg = ps[:, 2560 + (gid % 2) * 512:2560 + (gid % 2) * 512 + 128]
                tw = [("eV", kv_rv[i]), ("eS", kv_ev[i])]
                if gid >= 2:
                    tw.append(("eS", p2g[gid - 2]))
                def mk_trk(kb=kb, reg=reg):
                    def f(e):
                        e.ldweights(kb)
                        return e.matmul(reg, kb, idsb, start=True, stop=True,
                                        skip_group_check=True)
                    return f
                P.step("tensor", mk_trk(), waits=tw, inc="eT")
                k_tr.append(P.cnt("eT"))
                P.step("scalar", lambda e, reg=reg, i=i: e.activation(
                    ktb[:, i * 128:(i + 1) * 128], reg, AF.Copy),
                    waits=[("eT", k_tr[i])], inc="eS")
                p2g.append(P.cnt("eS"))
            pa_step()
            pa_step()

    while pa["i"] < PA_TOTAL and not (STRIP & 16):
        pa_step()
    P.barrier()

    if PHASES in (22, 23) or not (Q2 & 16):
        _replay(nc, P)
        return nc
    for wi in range(3):
        P.step("vector", lambda e, wi=wi: e.tensor_reduce(
            thparts[:, wi:wi + 1], absacc[:, wi * NHT:(wi + 1) * NHT], AX.X, ALU.add),
            inc="eV" if wi == 2 else None)
    c_thp = P.cnt("eV")
    P.step("tensor", lambda e: e.matmul(ps[0:1, 3072:3075], onescolb, thparts,
                                        start=True, stop=True, skip_group_check=True),
           waits=[("eV", c_thp)], inc="eT")
    P.step("vector", lambda e: e.tensor_copy(thsum, ps[0:1, 3072:3075]),
           waits=[("eT", P.cnt("eT"))], inc="eV")
    P.step("tensor", lambda e: e.matmul(ps[:, 3072:3075], onesk1fb[0:1, 0:128], thsum,
                                        start=True, stop=True, skip_group_check=True),
           waits=[("eV", P.cnt("eV"))], inc="eT")
    c_thb = P.cnt("eT")
    for col, sc_ in ((0, 3072), (2, 3073), (4, 3074)):
        P.step("vector", lambda e, col=col, sc_=sc_: e.tensor_scalar(
            thsb[:, col:col + 1], ps[:, sc_:sc_ + 1], 1.0 / (2 * NW_ELEM), None, ALU.mult),
            waits=[("eT", c_thb)] if col == 0 else ())
        P.step("vector", lambda e, col=col, sc_=sc_: e.tensor_scalar(
            thsb[:, col + 1:col + 2], ps[:, sc_:sc_ + 1], -1.0 / (2 * NW_ELEM), None, ALU.mult))
    for wi, sc_ in enumerate((3072, 3073, 3074)):
        P.step("vector", lambda e, wi=wi, sc_=sc_: e.tensor_scalar(
            ngs[:, wi:wi + 1], ps[:, sc_:sc_ + 1], -1.0 / NW_ELEM, None, ALU.mult),
            inc="eV" if wi == 2 else None)
    c_th_ready = P.cnt("eV")
    P.step("vector", None, waits=[("eV", c_th_ready)])

    if DEBUG:
        dma("sync", dbg_qt, qt_all, sem="dD")
        dma("sync", dbg_kt, ktb, sem="dD")
        dma("sync", dbg_v, vsb, sem="dD")

    wow = [("eT", P.cnt("eT"))]
    for dt in range(NDT):
        dma("gpsimd", wbig[:, dt * D:(dt + 1) * D],
            wo_d[dt * 128:(dt + 1) * 128, :], wow if dt == 0 else (), sem="dG")
    c_wo = P.cnt("dG")
    c_bo = dma("sync", bosb, bo_d, sem="dB")
    if PHASES <= 2 or PHASES in (21, 23, 25):
        _replay(nc, P)
        return nc

    NPAIR = 8
    c_exp = {}
    c_mask = {}
    c_rec = {}
    c_avm = {}
    c_avd = {}
    c_x2p = []
    c_x2sp = []
    c_xre = []
    c_ln2ap = []
    p45g = []
    c_ylast = 0
    c_wo_eT_prev = 0
    c_stageev_prev = 0

    for qt in range(NSTO):
        xw = []
        if qt >= 2:
            xw.append(("eP", c_x2p[qt - 2]))
        c_xre.append(dma("sync", xreb(qt % 2), x_d[(qt + 4) * 128:(qt + 5) * 128, :], xw,
                         sem="dR%d" % (qt % 2)))
        for pr in range(NPAIR):
            gp = qt * NPAIR + pr
            par = gp % 2
            psSC = ps[:, par * 1280:(par + 1) * 1280]
            exb = exptb(par)
            sw = [("eV", c_th_ready)]
            if gp >= 2:
                sw.append(("eS", c_exp[gp - 2]))
            if gp >= 1:
                sw.append(("eS", c_exp[gp - 1]))

            def mk_sc(qt=qt, pr=pr, psSC=psSC):
                def f(e):
                    last = None
                    for j in range(5):
                        kslc = ktb[:, (qt + j) * 128:(qt + j + 1) * 128]
                        e.ldweights(kslc)
                        for m in range(2):
                            h = pr * 2 + m
                            last = e.matmul(
                                psSC[:, m * 640 + j * 128:m * 640 + (j + 1) * 128], kslc,
                                qt_all[:, h * S_OWN + qt * 128:h * S_OWN + (qt + 1) * 128],
                                start=True, stop=True, skip_group_check=True)
                    return last
                return f
            P.step("tensor", mk_sc(), waits=sw, inc="eT")
            c_scm = P.cnt("eT")
            ew = [("eT", c_scm)]
            if gp >= 2:
                ew.append(("eT", c_avm[gp - 2]))
            P.step("scalar", lambda e, psSC=psSC, exb=exb: e.activation(
                exb, psSC, AF.Exp, scale=SM), waits=ew, inc="eS")
            c_exp[gp] = P.cnt("eS")
            mwk = [("eS", c_exp[gp])]
            if gp == 0:
                mwk.append(("dC", c_consts))
            for m in range(2):
                P.step("gpsimd", lambda e, exb=exb, m=m: e.tensor_tensor(
                    exb[:, m * 640:m * 640 + 128], exb[:, m * 640:m * 640 + 128],
                    mfarb, ALU.mult), waits=mwk if m == 0 else ())
                P.step("gpsimd", lambda e, exb=exb, m=m: e.tensor_tensor(
                    exb[:, m * 640 + 512:m * 640 + 640],
                    exb[:, m * 640 + 512:m * 640 + 640], mdiab, ALU.mult),
                    inc="eP" if m == 1 else None)
            c_mask[gp] = P.cnt("eP")
            dnw = [("eP", c_mask[gp])]
            if gp >= 1:
                dnw.append(("eV", c_avd[gp - 1]))

            def mk_dn(qt=qt, exb=exb):
                def f(e):
                    dn = ps[:, 2560:2816]
                    e.ldweights(onesk1b[0:1, :])
                    for j2, nb in enumerate((ncrb0, ncrb1)):
                        e.matmul(dn, onesk1b[0:1, :], nb[0:1, qt * 256:(qt + 1) * 256],
                                 start=(j2 == 0), stop=False, skip_group_check=True)
                    e.ldweights(onesb)
                    last = None
                    for m in range(2):
                        for j in range(5):
                            last = e.matmul(dn[:, m * 128:(m + 1) * 128], onesb,
                                            exb[:, m * 640 + j * 128: m * 640 + (j + 1) * 128],
                                            start=False, stop=(m == 1 and j == 4),
                                            skip_group_check=True)
                    return last
                return f
            P.step("tensor", mk_dn(), waits=dnw, inc="eT")
            c_dnm = P.cnt("eT")
            if DEBUG >= 3 and qt == 4 and pr == 0:
                spare = f32(2720, 256)
                P.step("scalar", lambda e, spare=spare: e.activation(spare, ps[:, 2560:2816], AF.Copy),
                       waits=[("eT", c_dnm)], inc="eS")
                dma("sync", dbg_dn, spare, [("eS", P.cnt("eS"))], sem="dD")
                dma("sync", dbg_ex, exb, [("eP", c_mask[gp])], sem="dD")
            rb = recb(par)
            rcw = [("eT", c_dnm)]
            if gp >= 2:
                rcw.append(("eV", c_avd[gp - 2]))
            P.step("vector", lambda e, rb=rb: e.reciprocal(rb, ps[:, 2560:2816]),
                   waits=rcw, inc="eV")
            c_rec[gp] = P.cnt("eV")
            avw = [("eP", c_mask[gp]), ("eV", c_rec[gp])]
            if gp >= 1:
                avw.append(("eV", c_avd[gp - 1]))

            def mk_av(qt=qt, exb=exb):
                def f(e):
                    av = ps[:, 2816:3072]
                    last = None
                    for j in range(5):
                        vslc = vsb[:, (qt + j) * 128:(qt + j + 1) * 128]
                        e.ldweights(vslc)
                        for m in range(2):
                            last = e.matmul(av[:, m * 128:(m + 1) * 128], vslc,
                                            exb[:, m * 640 + j * 128:m * 640 + (j + 1) * 128],
                                            start=(j == 0 and m == 0), stop=(j == 4),
                                            skip_group_check=True)
                    return last
                return f
            P.step("tensor", mk_av(), waits=avw, inc="eT")
            c_avm[gp] = P.cnt("eT")
            if DEBUG >= 3 and qt == 4 and pr == 0:
                spare2 = f32(2976, 256)
                P.step("scalar", lambda e, spare2=spare2: e.activation(spare2, ps[:, 2816:3072], AF.Copy),
                       waits=[("eT", c_avm[gp])], inc="eS")
                dma("sync", dbg_av, spare2, [("eS", P.cnt("eS"))], sem="dD")
            dvw = [("eT", c_avm[gp]), ("eV", c_rec[gp])]
            if qt > 0 and pr < 2:
                dvw.append(("eT", c_wo_eT_prev))
                if DEBUG >= 3:
                    dvw.append(("dD", P.cnt("dD")))
            P.step("vector", lambda e, pr=pr, rb=rb: e.tensor_tensor(
                avtb[:, pr * 256:(pr + 1) * 256], ps[:, 2816:3072], rb, ALU.mult), waits=dvw, inc="eV")
            c_avd[gp] = P.cnt("eV")
        c_avd_all = c_avd[qt * NPAIR + NPAIR - 1]
        if DEBUG:
            dma("sync", dbg_avt[qt:qt + 1, :].rearrange("r (p f) -> (r p) f", p=128),
                avtb, [("eV", c_avd_all)], sem="dD")

        for dfp in range(2):
            ww = [("eV", c_avd_all), ("dG", c_wo)]
            if qt > 0 and dfp == 0:
                ww.append(("eS", c_stageev_prev))
            if c_ylast:
                ww.append(("eV", c_ylast))

            def mk_wo(dfp=dfp):
                def f(e):
                    last = None
                    for hh in range(H):
                        aslc = avtb[:, hh * 128:(hh + 1) * 128]
                        e.ldweights(aslc)
                        for c in range(2):
                            dfc = dfp * 2 + c
                            last = e.matmul(ps[:, 3072 + c * 512:3072 + (c + 1) * 512], aslc,
                                            wbig[:, hh * 2048 + dfc * 512: hh * 2048 + (dfc + 1) * 512],
                                            start=(hh == 0), stop=(hh == H - 1),
                                            skip_group_check=True)
                    return last
                return f
            P.step("tensor", mk_wo(), waits=ww, inc="eT")
            c_wom = P.cnt("eT")
            P.step("vector", lambda e, dfp=dfp: e.tensor_tensor(
                ytile[:, dfp * 1024:(dfp + 1) * 1024], ps[:, 3072:4096],
                bosb[:, dfp * 1024:(dfp + 1) * 1024], ALU.add),
                waits=[("eT", c_wom), ("dB", c_bo)], inc="eV")
            c_ylast = P.cnt("eV")
        c_wo_eT_prev = P.cnt("eT")

        if DEBUG:
            dma("sync", dbg_y[qt:qt + 1, :].rearrange("r (p f) -> (r p) f", p=128),
                ytile, [("eV", c_ylast)], sem="dD")
        ln_moments(ytile, 2, f32(2720, 24), [("eV", P.cnt("eV"))])
        c_ag = P.cnt("eV")
        P.step("scalar", lambda e: e.activation(rstdb[:, 1:2], mvb[:, 3:4], AF.Ln, bias=EPS),
               waits=[("eV", c_ag)])
        P.step("scalar", lambda e: e.activation(rstdb[:, 1:2], rstdb[:, 1:2], AF.Exp, scale=-0.5),
               waits=[("eS", P.cnt("eS"))], inc="eS")
        c_rs = P.cnt("eS")
        ynw = [("eS", c_rs)]
        if DEBUG >= 3:
            ynw.append(("dD", P.cnt("dD")))
        P.step("vector", lambda e: e.tensor_scalar(
            ytile, ytile, mvb[:, 2:3], rstdb[:, 1:2], ALU.subtract, ALU.mult),
            waits=ynw, inc="eV")
        c_yn = P.cnt("eV")
        spw = [("eV", c_yn), ("dR%d" % (qt % 2), c_xre[qt])]
        if qt >= 1:
            spw.append(("dP", c_x2sp[qt - 1]))
            spw.append(("eV", c_ln2ap[qt - 1]))
        P.step("gpsimd", lambda e, qt=qt: e.tensor_tensor(
            x2tb, ytile, xreb(qt % 2), ALU.add), waits=spw, inc="eP")
        c_x2p.append(P.cnt("eP"))
        c_x2sp.append(dma("sync", x2_d[qt * 128:(qt + 1) * 128, :], x2tb,
                          [("eP", c_x2p[qt])], sem="dP"))
        ln_moments(x2tb, 4, f32(2720, 24),
                   [("eP", c_x2p[qt]), ("eV", P.cnt("eV"))])
        c_ag2 = P.cnt("eV")
        P.step("scalar", lambda e: e.activation(rstdb[:, 2:3], mvb[:, 5:6], AF.Ln, bias=EPS),
               waits=[("eV", c_ag2)])
        P.step("scalar", lambda e: e.activation(rstdb[:, 2:3], rstdb[:, 2:3], AF.Exp, scale=-0.5),
               waits=[("eS", P.cnt("eS"))], inc="eS")
        c_rs2 = P.cnt("eS")
        aw2 = [("eS", c_rs2)]
        if qt >= 1:
            aw2.append(("eT", c_xn2tr_prev))
        P.step("vector", lambda e: e.tensor_scalar(
            xn2tmp, x2tb, mvb[:, 4:5], rstdb[:, 2:3], ALU.subtract, ALU.mult),
            waits=aw2, inc="eV")
        c_ln2ap.append(P.cnt("eV"))
        c_ap2 = P.cnt("eV")
        for g in range(4):
            gid = qt * 4 + g
            reg = ps[:, 3072 + (gid % 2) * 512:3072 + (gid % 2) * 512 + 512]
            tw = [("eV", c_ap2), ("eV", c_ylast)]
            if gid >= 2:
                tw.append(("eS", p45g[gid - 2]))

            def mk_tr3(g=g, reg=reg):
                def f(e):
                    last = None
                    for b in range(4):
                        dt = g * 4 + b
                        e.ldweights(xn2tmp[:, dt * 128:(dt + 1) * 128])
                        last = e.matmul(reg[:, b * 128:(b + 1) * 128],
                                        xn2tmp[:, dt * 128:(dt + 1) * 128], idsb,
                                        start=True, stop=True, skip_group_check=True)
                    return last
                return f
            P.step("tensor", mk_tr3(), waits=tw, inc="eT")
            c_t3 = P.cnt("eT")

            def mk_ev3(g=g, reg=reg, qt=qt):
                def f(e):
                    last = None
                    for b in range(4):
                        dt = g * 4 + b
                        last = e.activation(
                            xn2t[:, dt * S_OWN + qt * 128: dt * S_OWN + (qt + 1) * 128],
                            reg[:, b * 128:(b + 1) * 128], AF.Copy)
                    return last
                return f
            P.step("scalar", mk_ev3(), waits=[("eT", c_t3)], inc="eS")
            p45g.append(P.cnt("eS"))
        c_xn2tr_prev = P.cnt("eT")
        c_stageev_prev = P.cnt("eS")
    fw6 = [("dP", NSTO * 16)]
    if DEBUG:
        fw6.append(("dD", P.cnt("dD")))
    P.step("scalar", None, waits=fw6)
    P.barrier()
    if PHASES <= 3:
        _replay(nc, P)
        return nc

    c_hmm = {}
    c_silu = {}
    c_gev = {}
    c_st1ev = {}
    c_st3ev = {}
    c_tr1 = {}
    c_tr3 = {}
    c_tn = {}
    c_w2sp = {}
    c_bq = {}
    slot = {"i": 0}

    def quant(ht, wsel, dst_tn, extra_b_waits=(), extra_p_waits=()):
        i = slot["i"]
        slot["i"] += 1
        buf = q_wt(i % 2)
        src = {0: w1_d, 1: w3_d, 2: w2t_d}[wsel][ht * 128:(ht + 1) * 128, :]
        lw = [("eV", c_th_ready)]
        if i >= 2:
            lw.append(("eV", c_bq[i - 2]))
        sW = "dW%d" % (i % 2)
        c_d = dma("sync", buf, src, lw, sem=sW)
        thc = {0: 0, 1: 2, 2: 4}[wsel]
        bw = [(sW, c_d)] + list(extra_b_waits)
        if i >= 1:
            bw.append(("eP", P.cnt("eP")))
        P.step("vector", lambda e, buf=buf: e.tensor_scalar(
            q_b1, buf, thsb[:, thc:thc + 1], None, ALU.is_gt), waits=bw)
        P.step("vector", lambda e, buf=buf: e.tensor_scalar(
            q_b2, buf, thsb[:, thc + 1:thc + 2], None, ALU.is_lt), inc="eV")
        c_bq[i] = P.cnt("eV")
        pw = [("eV", c_bq[i])] + list(extra_p_waits)
        P.step("gpsimd", lambda e, dst_tn=dst_tn: e.tensor_tensor(
            dst_tn, q_b2, q_b1, ALU.subtract), waits=pw, inc="eP")
        return P.cnt("eP")

    for ht in range(NHT):
        par = ht % 2
        tn1, tn3, tn2 = q_tn1(par), q_tn3(par), q_tn2(par)
        ep1 = [] if ht < 2 else [("eT", c_tr1[ht - 2])]
        c_tn[(ht, 0)] = quant(ht, 0, tn1, extra_p_waits=ep1)
        ep3 = [] if ht < 2 else [("eT", c_tr3[ht - 2])]
        c_tn[(ht, 1)] = quant(ht, 1, tn3, extra_p_waits=ep3)
        ep2 = [] if ht < 2 else [("dQ", P.cnt("dQ"))]
        c_tn[(ht, 2)] = quant(ht, 2, tn2, extra_p_waits=ep2)
        c_w2sp[ht] = dma("sync", w2q_d[ht:ht + 1, :].rearrange("r (p f) -> (r p) f", p=128),
                         tn2, [("eP", c_tn[(ht, 2)])], sem="dQ")

        if ht == 0:
            P._c_twave = {0: 0, 1: 0}
        for wsel, tn, st in ((0, tn1, stage1(par)), (1, tn3, stage3(par))):
            for wv in range(2):
                treg = ps[:, 2048 + wv * 1024:2048 + (wv + 1) * 1024]
                tw = [("eP", c_tn[(ht, wsel)]), ("eS", P._c_twave[wv])]

                def mk_tr(tn=tn, wv=wv, treg=treg):
                    def f(e):
                        last = None
                        for b in range(8):
                            dt = wv * 8 + b
                            e.ldweights(tn[:, dt * 128:(dt + 1) * 128])
                            last = e.matmul(treg[:, b * 128:(b + 1) * 128],
                                            tn[:, dt * 128:(dt + 1) * 128], idsb,
                                            start=True, stop=True, skip_group_check=True)
                        return last
                    return f
                P.step("tensor", mk_tr(), waits=tw, inc="eT")
                c_trw = P.cnt("eT")
                evw = [("eT", c_trw)]
                if ht >= 2:
                    evw.append(("eT", c_hmm[ht - 2]))
                P.step("scalar", lambda e, st=st, wv=wv, treg=treg: e.activation(
                    st[:, wv * 1024:(wv + 1) * 1024], treg, AF.Copy), waits=evw, inc="eS")
                P._c_twave[wv] = P.cnt("eS")
            if wsel == 0:
                c_tr1[ht] = P.cnt("eT")
                c_st1ev[ht] = P.cnt("eS")
            else:
                c_tr3[ht] = P.cnt("eT")
                c_st3ev[ht] = P.cnt("eS")

        hw = [("eS", c_st1ev[ht]), ("eS", c_st3ev[ht])]
        if ht >= 1:
            hw.append(("eS", c_silu[ht - 1]))
            hw.append(("eV", c_gev[ht - 1]))

        def mk_h(st1=stage1(par), st3=stage3(par)):
            def f(e):
                last = None
                for st, base in ((st1, 0), (st3, 1024)):
                    for dt in range(NDT):
                        lhs = st[:, dt * 128:(dt + 1) * 128]
                        e.ldweights(lhs)
                        for sc in range(2):
                            last = e.matmul(ps[:, base + sc * 512:base + (sc + 1) * 512], lhs,
                                            xn2t[:, dt * S_OWN + sc * 512:dt * S_OWN + (sc + 1) * 512],
                                            start=(dt == 0), stop=(dt == NDT - 1),
                                            skip_group_check=True)
                return last
            return f
        P.step("tensor", mk_h(), waits=hw, inc="eT")
        c_hmm[ht] = P.cnt("eT")
        P.step("scalar", lambda e: e.activation(
            silu_t, ps[:, 0:1024], AF.Silu, scale=ngs[:, 0:1]),
            waits=[("eT", c_hmm[ht])], inc="eS")
        c_silu[ht] = P.cnt("eS")
        P.step("vector", lambda e, ht=ht: e.scalar_tensor_tensor(
            gt[:, ht * S_OWN:(ht + 1) * S_OWN], silu_t, ngs[:, 1:2],
            ps[:, 1024:2048], ALU.mult, ALU.mult),
            waits=[("eS", c_silu[ht])], inc="eV")
        c_gev[ht] = P.cnt("eV")
    P.barrier()
    if PHASES <= 4:
        _replay(nc, P)
        return nc

    c_w2mm = {}
    c_oev = 0
    for dfc in range(4):
        for ht in range(NHT):
            par = ht % 2
            sl = w2slb(par)
            lw = [("dQ", NHT * 16)]
            if ht >= 2:
                lw.append(("eT", c_w2mm[(dfc, ht - 2)]))
            elif dfc > 0:
                lw.append(("eT", c_w2mm[(dfc - 1, NHT - 1)]))
            src = w2q_d[ht:ht + 1, :].rearrange("r (p f) -> (r p) f", p=128)[
                :, dfc * 512:(dfc + 1) * 512]
            sL = "dL%d" % par
            c_sl = dma("sync", sl, src, lw, sem=sL)
            mw = [(sL, c_sl)]
            if dfc > 0 and ht == 0:
                mw.append(("eV", c_oev))

            def mk_w2(ht=ht, sl=sl):
                def f(e):
                    last = None
                    for st in range(NSTO):
                        lhs = gt[:, ht * S_OWN + st * 128:ht * S_OWN + (st + 1) * 128]
                        e.ldweights(lhs)
                        last = e.matmul(ps[:, st * 512:(st + 1) * 512], lhs, sl,
                                        start=(ht == 0), stop=(ht == NHT - 1),
                                        skip_group_check=True)
                    return last
                return f
            P.step("tensor", mk_w2(), waits=mw, inc="eT")
            c_w2mm[(dfc, ht)] = P.cnt("eT")
        c_last = P.cnt("eT")
        for st in range(NSTO):
            P.step("vector", lambda e, st=st, dfc=dfc: e.tensor_scalar(
                osb[:, st * 2048 + dfc * 512:st * 2048 + (dfc + 1) * 512],
                ps[:, st * 512:(st + 1) * 512], ngs[:, 2:3], None, ALU.mult),
                waits=[("eT", c_last)] if st == 0 else (),
                inc="eV" if st == NSTO - 1 else None)
        c_oev = P.cnt("eV")
    c_w2ev = c_oev

    c_x3 = []
    for st in range(NSTO):
        rw = [("dP", NSTO * 16)]
        if st >= 2:
            rw.append(("eP", c_x3[st - 2]))
        sE = "dE%d" % (st % 2)
        c_r = dma("sync", x2re(st % 2), x2_d[st * 128:(st + 1) * 128, :], rw, sem=sE)
        ob = osb[:, st * 2048:(st + 1) * 2048]
        ln_moments(ob, 6, f32(3232, 24), [("eV", P.cnt("eV"))])
        c_ag3 = P.cnt("eV")
        P.step("scalar", lambda e: e.activation(rstdb[:, 3:4], mvb[:, 7:8], AF.Ln, bias=EPS),
               waits=[("eV", c_ag3)])
        P.step("scalar", lambda e: e.activation(rstdb[:, 3:4], rstdb[:, 3:4], AF.Exp, scale=-0.5),
               waits=[("eS", P.cnt("eS"))], inc="eS")
        c_rs3 = P.cnt("eS")
        P.step("vector", lambda e, ob=ob: e.tensor_scalar(
            ob, ob, mvb[:, 6:7], rstdb[:, 3:4], ALU.subtract, ALU.mult),
            waits=[("eS", c_rs3)], inc="eV")
        c_yn2 = P.cnt("eV")
        xb = x3tb(st % 2)
        fw = [("eV", c_yn2), (sE, c_r)]
        if st >= 2:
            fw.append(("dO", st * 16))
        P.step("gpsimd", lambda e, xb=xb, ob=ob, st=st: e.tensor_tensor(
            xb, ob, x2re(st % 2), ALU.add), waits=fw, inc="eP")
        c_x3.append(P.cnt("eP"))
        dma("sync", out_d[st * 128:(st + 1) * 128, :], xb, [("eP", c_x3[st])], sem="dO")
    P.step("sync", None, waits=[("dO", NSTO * 16)])
    _replay(nc, P)
    return nc


def _replay(nc, P):
    with nc.Block() as block:
        @block.sync
        def _(e):
            for s in P.steps["sync"]:
                s(e)

        @block.tensor
        def _(e):
            for s in P.steps["tensor"]:
                s(e)

        @block.vector
        def _(e):
            for s in P.steps["vector"]:
                s(e)

        @block.scalar
        def _(e):
            for s in P.steps["scalar"]:
                s(e)

        @block.gpsimd
        def _(e):
            for s in P.steps["gpsimd"]:
                s(e)


def _host_prep(L0_attn_x, L0_attn_Wq, L0_attn_Wk, L0_attn_Wv, L0_attn_Wo,
               L0_attn_bo, L0_ffn_w1, L0_ffn_w3, L0_ffn_w2):
    x = np.ascontiguousarray(np.asarray(L0_attn_x, np.float32).reshape(SEQ, D))
    wq = np.ascontiguousarray(np.asarray(L0_attn_Wq, np.float32))
    wk = np.ascontiguousarray(np.asarray(L0_attn_Wk, np.float32))
    wv = np.ascontiguousarray(np.asarray(L0_attn_Wv, np.float32))
    wo = np.ascontiguousarray(np.asarray(L0_attn_Wo, np.float32))
    bo = np.ascontiguousarray(np.broadcast_to(
        np.asarray(L0_attn_bo, np.float32)[None, :], (128, D)))
    w1 = np.zeros((HIDP, D), np.float32)
    w1[:HID] = np.asarray(L0_ffn_w1, np.float32)
    w3 = np.zeros((HIDP, D), np.float32)
    w3[:HID] = np.asarray(L0_ffn_w3, np.float32)
    w2t = np.zeros((HIDP, D), np.float32)
    w2t[:HID] = np.ascontiguousarray(np.asarray(L0_ffn_w2, np.float32).T)

    inv = np.exp(-np.log(10000.0) * (np.arange(0, ROPE_D, 2, dtype=np.float64) / ROPE_D))
    in_maps = []
    for c in range(NCORES):
        lo = c * S_OWN - WIN
        if c == 0:
            xc = np.concatenate([np.zeros((WIN, D), np.float32), x[0:S_OWN]], 0)
        else:
            xc = np.ascontiguousarray(x[lo:lo + S_TOT])
        pos = np.maximum(np.arange(lo, lo + S_TOT, dtype=np.float64), 0.0)
        th = pos[:, None] * inv[None, :]
        cosT = np.ascontiguousarray(np.tile(np.cos(th).astype(np.float32), (1, 8)))
        sinT = np.ascontiguousarray(np.tile(np.sin(th).astype(np.float32), (1, 8)))
        qabs = np.arange(c * S_OWN, (c + 1) * S_OWN)
        corr = np.maximum(0, WIN - 1 - qabs).astype(np.float64)
        hi = np.minimum(corr, 256.0)
        lo2 = corr - hi
        ncr = np.zeros((2, 2048), np.float32)
        for qt in range(NSTO):
            for m in range(2):
                ncr[0, qt * 256 + m * 128:qt * 256 + (m + 1) * 128] = -hi[qt * 128:(qt + 1) * 128]
                ncr[1, qt * 256 + m * 128:qt * 256 + (m + 1) * 128] = -lo2[qt * 128:(qt + 1) * 128]
        in_maps.append({
            "x": xc, "wq": wq, "wk": wk, "wv": wv, "wo": wo, "bo": bo,
            "ropec": cosT, "ropes": sinT,
            "ncr": ncr.astype(ml_dtypes.bfloat16),
            "w1": w1, "w3": w3, "w2t": w2t,
        })
    return in_maps


NRUN = NCORES


def kernel(**inputs):
    in_maps = _host_prep(**inputs)[:NRUN]
    nc = build_program()
    res = run_bass_kernel_spmd(nc, in_maps, core_ids=list(range(NRUN)))
    if NRUN < NCORES:
        return np.zeros((1, SEQ, D), np.float32)
    out = np.concatenate([res.results[c]["out"] for c in range(NCORES)], 0)
    return out.reshape(1, SEQ, D)


# version 10
# speedup vs baseline: 1.0205x; 1.0205x over previous
import numpy as np
import ml_dtypes
import concourse.bass as bass
import concourse.mybir as mybir
from concourse.bass_utils import run_bass_kernel_spmd

F32 = mybir.dt.float32
BF16 = mybir.dt.bfloat16
AF = mybir.ActivationFunctionType
ALU = mybir.AluOpType
AX = mybir.AxisListType

NCORES = 8
D = 2048
H = 16
HD = 128
ROPE_D = 64
WIN = 512
SEQ = 8192
S_OWN = 1024
S_TOT = 1536
NST = 12
NSTO = 8
NDT = 16
HID = 5461
HIDP = 5504
NHT = 43
EPS = 1e-5
SM = float(1.0 / np.sqrt(HD))
NW_ELEM = HID * D

DEBUG = False
SCRATCH = 12288
NCOLSG = 54048
RACEDET = False
NDEV = NCORES = 8
PHASES = 9
P1LIM = 99
BARENG = "ytvsp"
STRIP = 0
Q2 = 63
ENGS = ("sync", "tensor", "vector", "scalar", "gpsimd")
ESEM = {"sync": "eY", "tensor": "eT", "vector": "eV", "scalar": "eS", "gpsimd": "eP"}


class Plan:
    def __init__(self, nc):
        self.nc = nc
        self.steps = {e: [] for e in ENGS}
        self.sems = {}
        self.counts = {}

    def sem(self, name):
        if name not in self.sems:
            self.sems[name] = self.nc.alloc_semaphore(name)
            self.counts[name] = 0
        return self.sems[name]

    def cnt(self, name):
        return self.counts[name]

    def step(self, eng, fn=None, waits=(), inc=None, incn=1, nochain=False):
        waits = list(waits)
        for s, v in waits:
            assert v <= self.counts[s], f"wait on future {s}:{v}>{self.counts[s]}"
        if inc is not None:
            self.sem(inc)
            self.counts[inc] += incn
        sems = self.sems

        def run(engine, fn=fn, waits=tuple(waits), inc=inc, incn=incn):
            for s, v in waits:
                if v > 0:
                    engine.wait_ge(sems[s], v)
            inst = fn(engine) if fn is not None else None
            if inc is not None:
                if inst is None:
                    engine.sem_inc(sems[inc], incn)
                else:
                    inst.then_inc(sems[inc], incn)
        self.steps[eng].append(run)
        return self

    def barrier(self, tiny=None):
        t = tiny or self.tinyops
        for e in ENGS:
            self.step(e, t.get(e), inc=ESEM[e])
        tot = [(ESEM[e], self.cnt(ESEM[e])) for e in ENGS]
        for e in ENGS:
            self.step(e, None, waits=tot)


def build_program():
    nc = bass.Bass("TRN2", target_bir_lowering=False, num_devices=NDEV,
                   dynamic_dma_scratch_size=SCRATCH, detect_race_conditions=RACEDET)

    x_d = nc.dram_tensor("x", [S_TOT, D], F32, kind="ExternalInput").ap()
    wq_d = nc.dram_tensor("wq", [D, D], F32, kind="ExternalInput").ap()
    wk_d = nc.dram_tensor("wk", [D, HD], F32, kind="ExternalInput").ap()
    wv_d = nc.dram_tensor("wv", [D, HD], F32, kind="ExternalInput").ap()
    wo_d = nc.dram_tensor("wo", [D, D], F32, kind="ExternalInput").ap()
    bo_d = nc.dram_tensor("bo", [128, D], F32, kind="ExternalInput").ap()
    rc_d = nc.dram_tensor("ropec", [S_TOT, 256], F32, kind="ExternalInput").ap()
    rs_d = nc.dram_tensor("ropes", [S_TOT, 256], F32, kind="ExternalInput").ap()
    ncr_d = nc.dram_tensor("ncr", [2, 2048], BF16, kind="ExternalInput").ap()
    w1_d = nc.dram_tensor("w1", [HIDP, D], F32, kind="ExternalInput").ap()
    w3_d = nc.dram_tensor("w3", [HIDP, D], F32, kind="ExternalInput").ap()
    w2t_d = nc.dram_tensor("w2t", [HIDP, D], F32, kind="ExternalInput").ap()
    out_d = nc.dram_tensor("out", [S_OWN, D], F32, kind="ExternalOutput").ap()
    x2_d = nc.dram_tensor("x2spill", [S_OWN, D], F32, kind="Internal").ap()
    if DEBUG:
        dbg_kps = nc.dram_tensor("dbg_kps", [128, 256], F32, kind="ExternalOutput").ap()
        dbg_ex = nc.dram_tensor("dbg_ex", [128, 1280], BF16, kind="ExternalOutput").ap()
        dbg_dn = nc.dram_tensor("dbg_dn", [128, 256], F32, kind="ExternalOutput").ap()
        dbg_av = nc.dram_tensor("dbg_av", [128, 256], F32, kind="ExternalOutput").ap()
        dbg_wk = nc.dram_tensor("dbg_wk", [128, NDT * HD], BF16, kind="ExternalOutput").ap()
        dbg_kb = nc.dram_tensor("dbg_kb", [128, HD], BF16, kind="ExternalOutput").ap()
        dbg_qt = nc.dram_tensor("dbg_qt", [128, H * S_OWN], BF16, kind="ExternalOutput").ap()
        dbg_kt = nc.dram_tensor("dbg_kt", [128, S_TOT], BF16, kind="ExternalOutput").ap()
        dbg_v = nc.dram_tensor("dbg_v", [128, NST * HD], BF16, kind="ExternalOutput").ap()
        dbg_avt = nc.dram_tensor("dbg_avt", [NSTO, 128 * H * 128], BF16, kind="ExternalOutput").ap()
        dbg_y = nc.dram_tensor("dbg_y", [NSTO, 128 * D], F32, kind="ExternalOutput").ap()
    w2q_d = nc.dram_tensor("w2q", [NHT, 128 * D], BF16, kind="Internal").ap()

    rr = np.arange(128)
    ident_t = nc.inline_tensor(np.eye(128, dtype=ml_dtypes.bfloat16), name="identc")
    ones_t = nc.inline_tensor(np.ones((128, 128), dtype=ml_dtypes.bfloat16), name="onesc")
    onesk1_t = nc.inline_tensor(np.ones((1, 128), dtype=ml_dtypes.bfloat16), name="onesk1c")
    onesk1f_t = nc.inline_tensor(np.ones((1, 128), dtype=np.float32), name="onesk1fc")
    onescol_t = nc.inline_tensor(np.ones((128, 1), dtype=np.float32), name="onescolc")
    mfar_t = nc.inline_tensor((rr[:, None] >= rr[None, :] + 1).astype(ml_dtypes.bfloat16), name="mfarc")
    mdia_t = nc.inline_tensor((rr[:, None] <= rr[None, :]).astype(ml_dtypes.bfloat16), name="mdiac")
    constvals = [0.0, 1.0, EPS, -0.5, SM, 0.125, 256.0, 1.0 / 2048.0]
    ct_t = nc.inline_tensor(np.tile(np.array(constvals, np.float32)[None, :], (128, 1)), name="ctensc")

    NCOLS = NCOLSG
    arena = nc.alloc_sbuf_tensor("arena", [128, NCOLS], F32)
    arenab = arena.bitcast(BF16)

    def f32(off, n, rows=128):
        assert off >= 0 and (off + n <= NCOLS or STRIP)
        return arena[0:rows, off:off + n] if off + n <= NCOLS else arena[0:rows, 0:n]

    def bf(off, n, rows=128):
        assert off >= 0 and (off + n <= NCOLS or STRIP)
        return (arenab[0:rows, 2 * off:2 * (off + n)] if off + n <= NCOLS
                else arenab[0:rows, 0:2 * n])

    csb = f32(0, 8)
    onescolb = f32(8, 1)
    thparts = f32(16, 3)
    thsum = f32(24, 3, rows=1)
    thsb = f32(32, 8)
    ngs = f32(40, 3)
    absacc = f32(48, 132)
    statb = f32(184, 24)
    mvb = f32(208, 8)
    rstdb = f32(216, 8)
    onesk1fb = f32(224, 128, rows=1)
    idsb = bf(352, 64)
    onesb = bf(416, 64)
    onesk1b = bf(480, 64, rows=1)
    mfarb = bf(544, 64)
    mdiab = bf(608, 64)
    ncrb0 = bf(672, 1024, rows=1)
    ncrb1 = bf(1696, 1024, rows=1)
    w2slb = lambda p: bf(2720 + p * 256, 256)
    knatb = lambda p: bf(3232 + p * 64, 64)
    SM_END = 3360

    WB = SM_END
    wbig_half = bf(WB, 8192)
    wbig = bf(WB, 16384)
    wtinA = lambda p: f32(WB + 8192 + p * 2048, 2048)
    q_wt = lambda p: f32(WB + p * 2048, 2048)
    q_b1 = bf(WB + 4096, 1024)
    q_b2 = bf(WB + 5120, 1024)
    q_tn1 = lambda p: bf(WB + 6144 + p * 1024, 1024)
    q_tn3 = lambda p: bf(WB + 8192 + p * 1024, 1024)
    q_tn2 = lambda p: bf(WB + 10240 + p * 1024, 1024)
    silu_t = f32(WB + 12288, 1024)
    osb = f32(WB, 16384)
    WB_END = WB + 16384

    A2 = WB_END
    xn1t = bf(A2, 12288)
    xn2t = bf(A2, 8192)
    stage1 = lambda p: bf(A2 + 8192 + p * 1024, 1024)
    stage3 = lambda p: bf(A2 + 10240 + p * 1024, 1024)
    x2tb = f32(A2 + 8192, 2048)
    xn2tmp = bf(A2 + 10240, 1024)
    x2re = lambda p: f32(A2 + p * 2048, 2048)
    x3tb = lambda p: f32(A2 + 4096 + p * 2048, 2048)
    A2_END = A2 + 12288

    A3 = A2_END
    qt_all = bf(A3, 8192)
    ktb = bf(A3 + 8192, 768)
    vsb = bf(A3 + 8960, 768)
    rcsb = f32(A3 + 9728, 3072)
    rssb = f32(A3 + 12800, 3072)
    wksb = bf(A3 + 15872, 1024)
    wvsb = bf(A3 + 16896, 1024)
    exptb = lambda p: bf(A3 + 17920 + p * 640, 640)
    recb = lambda p: f32(A3 + 19200 + p * 256, 256)
    avtb = bf(A3 + 19712, 1024)
    ytile = f32(A3 + 9728, 2048)
    xreb = lambda p: f32(A3 + 11776, 2048) if p == 0 else f32(A3 + 13824, 2048)
    bosb = f32(A3 + 15872, 2048)
    xinb = lambda p: f32(A3 + p * 2048, 2048)
    xn1tmpb = lambda p: bf(A3 + 4096 + p * 1024, 1024)
    qnatb = lambda p: bf(A3 + 17920 + p * 512, 512)
    rtmp = f32(A3 + 18944, 1024)
    gt = bf(A3, 22016)
    A3_END = A3 + 22016
    assert A3_END <= NCOLS or STRIP & 128, A3_END

    ps = nc.alloc_psum_tensor("ps", [128, 4096], F32)
    psb = ps.bitcast(BF16)

    if STRIP & 512:
        mm_a = bf(A2, 256)
        mm_ev = f32(A3, 24 * 512)
        dsem = nc.alloc_semaphore("dsem")
        tsem = nc.alloc_semaphore("tsem")
        ssem = nc.alloc_semaphore("ssem")
        osem = nc.alloc_semaphore("osem")
        with nc.Block() as block:
            @block.gpsimd
            def _(g):
                g.dma_start(out=mm_a, in_=wq_d[0:128, 0:512]).then_inc(dsem, 16)

            @block.tensor
            def _(t):
                t.wait_ge(dsem, 16)
                for it in range(24):
                    par = it % 2
                    reg = ps[:, par * 512:(par + 1) * 512]
                    if it >= 2:
                        t.wait_ge(ssem, it - 1)
                    last = None
                    for k in range(16):
                        t.ldweights(mm_a[:, 0:128])
                        last = t.matmul(reg, mm_a[:, 0:128], mm_a[:, 0:512],
                                        start=(k == 0), stop=(k == 15),
                                        skip_group_check=True)
                    last.then_inc(tsem, 1)

            @block.scalar
            def _(s):
                for it in range(24):
                    par = it % 2
                    s.wait_ge(tsem, it + 1)
                    s.activation(mm_ev[:, it * 512:(it + 1) * 512],
                                 ps[:, par * 512:(par + 1) * 512],
                                 AF.Copy).then_inc(ssem, 1)

            @block.sync
            def _(sy):
                sy.wait_ge(ssem, 24)
                sy.dma_start(out=out_d[0:128, 0:512], in_=mm_ev[:, 0:512]).then_inc(osem, 16)
                sy.wait_ge(osem, 16)
        return nc

    P = Plan(nc)
    for s in ("dC", "dB", "dD", "dX0", "dX1", "dA0", "dA1", "dR0", "dR1", "dP",
              "dW0", "dW1", "dQ", "dL0", "dL1", "dE0", "dE1", "dG", "dO",
              "eY", "eT", "eV", "eS", "eP"):
        P.sem(s)
    P.tinyops = {
        "tensor": lambda e: e.ldweights(idsb),
        "vector": lambda e: e.tensor_copy(statb[:, 0:1], csb[:, 0:1]),
        "scalar": lambda e: e.activation(statb[:, 1:2], csb[:, 0:1], AF.Copy),
        "gpsimd": lambda e: e.tensor_copy(statb[:, 2:3], csb[:, 0:1]),
    }

    if STRIP & 1024:
        if STRIP & 8192:
            for i, v in enumerate(constvals):
                nc.const_aps.register(F32, v, csb[:, i:i + 1])
        mm_a = bf(A2, 256)
        mm_ev = f32(A3, 24 * 512)
        P.step("gpsimd", lambda e: e.dma_start(out=mm_a, in_=wq_d[0:128, 0:512]),
               inc="dG", incn=16)
        c_ld = P.cnt("dG")
        prev_ev = {}
        for it in range(24):
            par = it % 2
            reg = ps[:, par * 512:(par + 1) * 512]
            w = [("dG", c_ld)] if (it == 0 or STRIP & 4096) else []
            if it >= 2:
                w.append(("eS", prev_ev[it - 2]))

            def mk_g(reg=reg):
                def f(e):
                    last = None
                    for k in range(16):
                        e.ldweights(mm_a[:, 0:128])
                        last = e.matmul(reg, mm_a[:, 0:128], mm_a[:, 0:512],
                                        start=(k == 0), stop=(k == 15),
                                        skip_group_check=True)
                    return last
                return f
            P.step("tensor", mk_g(), waits=w, inc="eT")
            c_g = P.cnt("eT")
            P.step("scalar", lambda e, it=it, reg=reg: e.activation(
                mm_ev[:, it * 512:(it + 1) * 512], reg, AF.Copy),
                waits=[("eT", c_g)], inc="eS")
            prev_ev[it] = P.cnt("eS")
        if STRIP & 2048 and BARENG == "L":
            P.step("tensor", lambda e: e.ldweights(idsb), inc="eT")
        elif STRIP & 2048 and BARENG == "T":
            P.step("tensor", None, inc="eT")
        elif STRIP & 2048 and BARENG == "W":
            P.step("tensor", None, waits=[("eT", 24)])
        elif STRIP & 2048:
            emap = {"y": "sync", "t": "tensor", "v": "vector", "s": "scalar", "p": "gpsimd"}
            sub = [emap[c] for c in BARENG]
            for e in sub:
                P.step(e, P.tinyops.get(e), inc=ESEM[e])
            tot = [(ESEM[e], P.cnt(ESEM[e])) for e in sub]
            for e in sub:
                P.step(e, None, waits=tot)
        if not (STRIP & 16384):
            P.step("sync", lambda e: e.dma_start(out=out_d[0:128, 0:512], in_=mm_ev[:, 0:512]),
                   waits=[("eS", prev_ev[23])], inc="dO", incn=16)
            P.step("sync", None, waits=[("dO", 16)])
        _replay(nc, P)
        return nc

    def dma(eng, out, in_, waits=(), sem="dS"):
        P.step(eng, lambda e, o=out, i=in_: e.dma_start(out=o, in_=i),
               waits=waits, inc=sem, incn=16)
        return P.cnt(sem)

    def ln_moments(buf, mva, sc, waits_first):
        def vstep(fn):
            P.step("vector", fn, waits=[("eV", P.cnt("eV"))], inc="eV")
        P.step("vector", lambda e, b=buf: e.bn_stats(statb[:, 0:6], b[:, 0:512]),
               waits=waits_first)
        for j in range(1, 4):
            P.step("vector", lambda e, b=buf, j=j: e.bn_stats(
                statb[:, 6 * j:6 * j + 6], b[:, 512 * j:512 * (j + 1)]),
                inc="eV" if j == 3 else None)
        vstep(lambda e, sc=sc: e.tensor_tensor(sc, statb, statb, ALU.mult))
        for (lo, mid, hi) in ((0, 12, 24), (0, 6, 12), (0, 3, 6)):
            vstep(lambda e, lo=lo, mid=mid, hi=hi: e.tensor_tensor(
                statb[:, lo:mid], statb[:, lo:mid], statb[:, mid:hi], ALU.add))
            vstep(lambda e, sc=sc, lo=lo, mid=mid, hi=hi: e.tensor_tensor(
                sc[:, lo:mid], sc[:, lo:mid], sc[:, mid:hi], ALU.add))
        vstep(lambda e, mva=mva: e.tensor_tensor(
            mvb[:, mva:mva + 1], statb[:, 1:2], csb[:, 5:6], ALU.mult))
        vstep(lambda e, sc=sc: e.tensor_tensor(
            sc[:, 1:2], sc[:, 1:2], csb[:, 6:7], ALU.mult))
        vstep(lambda e, sc=sc, mva=mva: e.tensor_tensor(
            sc[:, 3:4], mvb[:, mva:mva + 1], mvb[:, mva:mva + 1], ALU.mult))
        vstep(lambda e, sc=sc: e.tensor_tensor(
            sc[:, 2:3], sc[:, 1:2], statb[:, 2:3], ALU.add))
        vstep(lambda e, sc=sc: e.tensor_tensor(
            sc[:, 0:1], sc[:, 2:3], csb[:, 7:8], ALU.mult))
        vstep(lambda e, sc=sc, mva=mva: e.tensor_tensor(
            mvb[:, mva + 1:mva + 2], sc[:, 0:1], sc[:, 3:4], ALU.subtract))

    clist = [(ct_t, csb), (onescol_t, onescolb), (onesk1f_t, onesk1fb),
             (ident_t, idsb), (ones_t, onesb), (onesk1_t, onesk1b),
             (mfar_t, mfarb), (mdia_t, mdiab)]
    if STRIP & 1:
        clist = [(ident_t, idsb)]
    for t, dst in clist:
        dma("sync", dst, t.ap(), sem="dC")
    if not (STRIP & 1):
        dma("sync", ncrb0, ncr_d[0:1, :], sem="dC")
        dma("sync", ncrb1, ncr_d[1:2, :], sem="dC")
    for t in range(0 if STRIP & 1 else NST):
        dma("sync", rcsb[:, t * 256:(t + 1) * 256],
            rc_d[t * 128:(t + 1) * 128, :], sem="dC")
        dma("sync", rssb[:, t * 256:(t + 1) * 256],
            rs_d[t * 128:(t + 1) * 128, :], sem="dC")
    c_consts = P.cnt("dC")
    for i, v in enumerate(constvals):
        nc.const_aps.register(F32, v, csb[:, i:i + 1])

    for dt in range(0 if (PHASES == 130 or STRIP & 2) else NDT):
        dma("gpsimd", wksb[:, dt * HD:(dt + 1) * HD],
            wk_d[dt * 128:(dt + 1) * 128, :], sem="dG")
        dma("gpsimd", wvsb[:, dt * HD:(dt + 1) * HD],
            wv_d[dt * 128:(dt + 1) * 128, :], sem="dG")
    c_kvw = P.cnt("dG")
    wq_r = wq_d.rearrange("(dt p) (hf f) -> p dt hf f", p=128, f=1024)
    for dt in range(0 if (PHASES == 130 or STRIP & 2) else NDT):
        dma("gpsimd", wbig_half[:, dt * 1024:(dt + 1) * 1024],
            wq_r[:, dt, 0, :], sem="dG")
    c_wq = [P.cnt("dG"), None]
    if PHASES <= -1:
        P.step("sync", None, waits=[("dC", c_consts)])
        P.step("gpsimd", None, waits=[("dG", P.cnt("dG"))])
        _replay(nc, P)
        return nc
    if PHASES <= 0:
        P.step("sync", None, waits=[("dC", c_consts)])
        P.step("gpsimd", None, waits=[("dG", P.cnt("dG"))])
        P.barrier()
        _replay(nc, P)
        return nc
    if DEBUG >= 3:
        dma("sync", dbg_wk, wksb, [("dG", c_wq[0])], sem="dD")

    if STRIP & 128:
        mm_a = bf(A2, 256)
        mm_ev = f32(A3, 24 * 512)
        dma("gpsimd", mm_a, wq_d[0:128, 0:512], sem="dG")
        c_ld = P.cnt("dG")
        prev_ev = {}
        for it in range(24):
            par = it % 2
            reg = ps[:, par * 512:(par + 1) * 512]
            w = [("dG", c_ld)]
            if it >= 2:
                w.append(("eS", prev_ev[it - 2]))

            def mk_g(reg=reg):
                def f(e):
                    last = None
                    for k in range(16):
                        e.ldweights(mm_a[:, 0:128])
                        last = e.matmul(reg, mm_a[:, 0:128], mm_a[:, 0:512],
                                        start=(k == 0), stop=(k == 15),
                                        skip_group_check=True)
                    return last
                return f
            P.step("tensor", mk_g(), waits=w, inc="eT")
            c_g = P.cnt("eT")
            P.step("scalar", lambda e, it=it, reg=reg: e.activation(
                mm_ev[:, it * 512:(it + 1) * 512], reg, AF.Copy),
                waits=[("eT", c_g)], inc="eS")
            prev_ev[it] = P.cnt("eS")
        P.barrier()
        _replay(nc, P)
        return nc

    pa = {"i": 0, "dma": [], "done": []}
    PA_TOTAL = 3 * NHT

    def pa_src(slot):
        if slot < NHT:
            return w1_d[slot * 128:(slot + 1) * 128, :]
        if slot < 2 * NHT:
            t = slot - NHT
            return w3_d[t * 128:(t + 1) * 128, :]
        t = slot - 2 * NHT
        return w2t_d[t * 128:(t + 1) * 128, :]

    def pa_step():
        if STRIP & 16:
            return
        i = pa["i"]
        if i >= PA_TOTAL:
            return
        pa["i"] = i + 1
        buf = wtinA(i % 2)
        w = []
        if i >= 2:
            w = [pa["done"][i - 2]]
        sA = "dA%d" % (i % 2)
        c = dma("sync", buf, pa_src(i), w, sem=sA)
        pa["dma"].append((sA, c))
        cw = [(sA, c), ("dC", c_consts)]
        if i % 2 == 0:
            P.step("scalar", lambda e, b=buf, s=i: e.activation(
                b, b, AF.Abs, accum_out=absacc[:, s:s + 1]), waits=cw, inc="eS")
            pa["done"].append(("eS", P.cnt("eS")))
        else:
            P.step("vector", lambda e, b=buf, s=i: e.tensor_reduce(
                absacc[:, s:s + 1], b, AX.X, ALU.add, apply_absolute_value=True),
                waits=cw, inc="eV")
            pa["done"].append(("eV", P.cnt("eV")))

    c_apply = []
    c_trlast = []
    p1g = []
    for i in range(min(NST, P1LIM)):
        buf = xinb(i % 2)
        w = []
        if i >= 2:
            w.append(("eV", c_apply[i - 2]))
        sX = "dX%d" % (i % 2)
        if PHASES in (127, 128, 129, 130, 131):
            c_xd = 0
        else:
            c_xd = dma("sync", buf, x_d[i * 128:(i + 1) * 128, :], w, sem=sX)
        if PHASES not in (11, 12, 121, 122, 123, 124, 125, 126, 127, 128, 129, 130, 131):
            pa_step()
            pa_step()
        if PHASES in (126, 127, 128, 129, 130, 131):
            if not (STRIP & 4):
                P.step("vector", lambda e, b=buf: e.tensor_copy(statb[:, 0:24], b[:, 0:24]),
                       waits=([(sX, c_xd)] if PHASES == 126 else []) + [("eV", P.cnt("eV"))], inc="eV")
        else:
            ln_moments(buf, 0, f32(A3 + 17920, 24),
                       [(sX, c_xd), ("dC", c_consts), ("eV", P.cnt("eV"))])
        c_ag = P.cnt("eV")
        P.step("scalar", lambda e: e.activation(rstdb[:, 0:1], mvb[:, 1:2], AF.Ln, bias=EPS),
               waits=[("eV", c_ag)])
        P.step("scalar", lambda e: e.activation(rstdb[:, 0:1], rstdb[:, 0:1], AF.Exp, scale=-0.5),
               waits=[("eS", P.cnt("eS"))], inc="eS")
        c_rs = P.cnt("eS")
        xo = xn1tmpb(i % 2)
        if (PHASES in (125, 126, 127, 128, 130, 131)) and not (STRIP & 4):
            aw = [("eV", c_ag)]
            if i >= 2 and len(c_trlast) > i - 2:
                aw.append(("eT", c_trlast[i - 2]))
            P.step("vector", lambda e, b=buf, xo=xo: e.tensor_scalar(
                xo, b, 1.0, None, ALU.mult), waits=aw, inc="eV")
            c_apply.append(P.cnt("eV"))
        elif PHASES == 129 or (STRIP & 4):
            c_apply.append(P.cnt("eV"))
        aw = [("eS", c_rs)]
        if i >= 2 and len(c_trlast) > i - 2:
            aw.append(("eT", c_trlast[i - 2]))
        if PHASES not in (125, 126, 127, 128, 129, 130, 131):
            P.step("vector", lambda e, b=buf, xo=xo: e.tensor_scalar(
                xo, b, mvb[:, 0:1], rstdb[:, 0:1], ALU.subtract, ALU.mult),
                waits=aw, inc="eV")
            c_apply.append(P.cnt("eV"))
        for g in range(0 if PHASES == 11 else 4):
            gid = i * 4 + g
            reg = ps[:, (gid % 2) * 512:(gid % 2) * 512 + 512]
            tw = [("eV", c_apply[i])] if c_apply[i] else []
            if gid == 0:
                tw.append(("dC", c_consts))
            back = 1 if PHASES == 123 else 2
            if gid >= back:
                pv = p1g[gid - back]
                tw.append(pv if isinstance(pv, tuple) else ("eS", pv))
            if PHASES in (128, 129, 130, 131):
                tw = [("dC", c_consts)] if gid == 0 else []

            def mk_tr1(xo=xo, g=g, reg=reg):
                def f(e):
                    last = None
                    for b in range(4):
                        dt = g * 4 + b
                        e.ldweights(xo[:, dt * 128:(dt + 1) * 128])
                        last = e.matmul(reg[:, b * 128:(b + 1) * 128],
                                        xo[:, dt * 128:(dt + 1) * 128], idsb,
                                        start=True, stop=True, skip_group_check=True)
                    return last
                return f
            if PHASES == 121:
                tw = [x for x in tw if x[0] != "eS"]
            P.step("tensor", mk_tr1(), waits=tw, inc="eT")
            c_tr = P.cnt("eT")

            def mk_ev1(g=g, reg=reg, i=i):
                def f(e):
                    last = None
                    for b in range(4):
                        dt = g * 4 + b
                        last = e.activation(
                            xn1t[:, dt * S_TOT + i * 128: dt * S_TOT + (i + 1) * 128],
                            reg[:, b * 128:(b + 1) * 128], AF.Copy)
                    return last
                return f
            P.step("scalar", mk_ev1(), waits=[("eT", c_tr)], inc="eS")
            p1g.append(P.cnt("eS"))
        c_trlast.append(P.cnt("eT"))
        if PHASES not in (11, 12, 121, 122, 123, 124, 125, 126, 127, 128, 129, 130, 131):
            pa_step()
    if not (STRIP & 8):
        P.barrier()
    if PHASES <= 1 or PHASES in (11, 12, 121, 122, 123, 124, 125, 126, 127, 128, 129, 130, 131):
        _replay(nc, P)
        return nc

    q_ev = []
    q_tr = []
    kv_ev = []
    kv_rv = []
    k_tr = []
    p2g = []
    qn = 0
    last_rope = 0

    for h2 in range(1 if STRIP & 32 else 2):
        if h2 == 1:
            wqw = [("eT", P.cnt("eT"))]
            for dt in range(NDT):
                dma("gpsimd", wbig_half[:, dt * 1024:(dt + 1) * 1024],
                    wq_r[:, dt, 1, :], wqw if dt == 0 else (), sem="dG")
            c_wq[1] = P.cnt("dG")
        for i in range(NST):
            own = i >= 4
            if (not own) and h2 == 1:
                continue
            psQ = None
            psKV = None
            mw = [("dG", c_wq[h2])]
            if own:
                par = qn % 2
                psQ = ps[:, par * 1024:(par + 1) * 1024]
                if qn >= 2:
                    for sname, v in (("eV", q_ev[qn - 2][0]), ("eS", q_ev[qn - 2][1])):
                        if v:
                            mw.append((sname, v))
            if h2 == 0:
                kpar = i % 2
                kvbase = 2048 if kpar == 0 else 3584
                psKV = ps[:, kvbase:kvbase + 256]
                if i >= 2 and kv_ev[i - 2]:
                    mw += [("eS", kv_ev[i - 2]), ("eV", kv_rv[i - 2])]

            def mk_mm(i=i, h2=h2, own=own, psQ=psQ, psKV=psKV):
                def f(e):
                    last = None
                    for dt in range(NDT):
                        lhs = xn1t[:, dt * S_TOT + i * 128: dt * S_TOT + (i + 1) * 128]
                        e.ldweights(lhs)
                        if own and (Q2 & 4096):
                            last = e.matmul(psQ[:, 0:512], lhs,
                                            wbig_half[:, dt * 1024:dt * 1024 + 512],
                                            start=(dt == 0), stop=(dt == NDT - 1),
                                            skip_group_check=True)
                            continue
                        if own:
                            for c in range(2):
                                last = e.matmul(psQ[:, c * 512:(c + 1) * 512], lhs,
                                                wbig_half[:, dt * 1024 + c * 512:dt * 1024 + (c + 1) * 512],
                                                start=(dt == 0), stop=(dt == NDT - 1),
                                                skip_group_check=True)
                        if h2 == 0 and not (STRIP & 64):
                            last = e.matmul(psKV[:, 0:128], lhs, wksb[:, dt * HD:(dt + 1) * HD],
                                            start=(dt == 0), stop=(dt == NDT - 1),
                                            skip_group_check=True)
                            last = e.matmul(psKV[:, 128:256], lhs, wvsb[:, dt * HD:(dt + 1) * HD],
                                            start=False, stop=(dt == NDT - 1),
                                            skip_group_check=True)
                    return last
                return f
            P.step("tensor", mk_mm(), waits=mw, inc="eT")
            c_mm = P.cnt("eT")

            if PHASES == 23:
                q_ev.append((0, 0))
                q_tr.append(0)
                if own:
                    qn += 1
                if h2 == 0:
                    kv_ev.append(0)
                    kv_rv.append(0)
                    k_tr.append(0)
                pa_step()
                pa_step()
                continue

            if own and not (Q2 & 1):
                q_ev.append((0, 0))
                if not (Q2 & 2):
                    q_tr.append(0)
                qn += 1
            if h2 == 0 and not (Q2 & 4):
                kv_ev.append(0)
                kv_rv.append(0)
                if not (Q2 & 8):
                    k_tr.append(0)
            if own and (Q2 & 1024):
                def mk_cpv(psQ=psQ):
                    def f(e):
                        e.tensor_copy(qnatb(0)[:, 0:512], psQ[:, 0:512])
                        return e.tensor_copy(qnatb(1)[:, 0:512], psQ[:, 512:1024])
                    return f
                P.step("vector", mk_cpv(), waits=[("eT", c_mm)], inc="eV")
                q_ev.append((P.cnt("eV"), 0))
                q_tr.append(0)
                qn += 1
            if own and (Q2 & 512):
                def mk_cp512(psQ=psQ):
                    def f(e):
                        e.activation(qnatb(0)[:, 0:512], psQ[:, 0:512], AF.Copy)
                        return e.activation(qnatb(1)[:, 0:512], psQ[:, 512:1024], AF.Copy)
                    return f
                P.step("scalar", mk_cp512(), waits=[("eT", c_mm)], inc="eS")
                q_ev.append((0, P.cnt("eS")))
                q_tr.append(0)
                qn += 1
            if own and (Q2 & 256):
                P.step("scalar", lambda e, psQ=psQ: e.activation(
                    qnatb(0), psQ, AF.Copy), waits=[("eT", c_mm)], inc="eS")
                q_ev.append((0, P.cnt("eS")))
                q_tr.append(0)
                qn += 1
            if own and (Q2 & 64):
                def mk_qrest64(qb=qb if False else None, psQ=psQ):
                    def f(e):
                        last = None
                        for hh in range(8):
                            last = e.activation(
                                qnatb(0)[:, hh * 128 + 64:(hh + 1) * 128],
                                psQ[:, hh * 128 + 64:(hh + 1) * 128], AF.Copy)
                        return last
                    return f
                P.step("scalar", mk_qrest64(), waits=[("eT", c_mm)], inc="eS")
                q_ev.append((0, P.cnt("eS")))
                q_tr.append(0)
                qn += 1
            if h2 == 0 and (Q2 & 128):
                kb = knatb(i % 2)
                kx1, kx2 = psKV[:, 0:32], psKV[:, 32:64]
                cosk = rcsb[:, i * 256:i * 256 + 32]
                sink = rssb[:, i * 256:i * 256 + 32]

                def mk_kr128(kb=kb, kx1=kx1, kx2=kx2, cosk=cosk, sink=sink):
                    def f(e):
                        e.tensor_tensor(rtmp[:, 0:32], kx2, sink, ALU.mult)
                        e.tensor_tensor(rtmp[:, 32:64], kx1, cosk, ALU.mult)
                        e.tensor_tensor(rtmp[:, 64:96], kx1, sink, ALU.mult)
                        e.tensor_tensor(rtmp[:, 96:128], kx2, cosk, ALU.mult)
                        e.tensor_tensor(kb[:, 0:32], rtmp[:, 32:64], rtmp[:, 0:32], ALU.subtract)
                        return e.tensor_tensor(kb[:, 32:64], rtmp[:, 64:96], rtmp[:, 96:128], ALU.add)
                    return f
                P.step("vector", mk_kr128(), waits=[("eT", c_mm)], inc="eV")
                kv_ev.append(0)
                kv_rv.append(P.cnt("eV"))
                k_tr.append(0)
            if own and (Q2 & 1):
                qb = qnatb(par)
                cosm = rcsb[:, i * 256:(i + 1) * 256].rearrange("p (h c) -> p h c", c=32)
                sinm = rssb[:, i * 256:(i + 1) * 256].rearrange("p (h c) -> p h c", c=32)
                pq3 = psQ.rearrange("p (h c) -> p h c", c=128)
                x1, x2p = pq3[:, :, 0:32], pq3[:, :, 32:64]
                rt = lambda k: rtmp[:, k * 256:(k + 1) * 256].rearrange("p (h c) -> p h c", c=32)
                rw = [("eT", c_mm), ("eV", last_rope)]
                if qn >= 2:
                    rw.append(("eT", q_tr[qn - 2]))
                if PHASES == 21:
                    P.step("vector", lambda e, qb=qb, psQ=psQ: e.tensor_copy(qb, psQ), waits=rw, inc="eV")
                    c_rope = P.cnt("eV")
                    last_rope = c_rope
                    P.step("scalar", None, inc="eS")
                    c_qrest = P.cnt("eS")
                    q_ev.append((c_rope, c_qrest))
                elif PHASES == 25 or PHASES >= 9 or PHASES in (2, 3, 4, 22):
                    def mk_ropeq_mul(psQ=psQ, i=i):
                        def f(e):
                            cs = rcsb[:, i * 256:(i + 1) * 256]
                            sn = rssb[:, i * 256:(i + 1) * 256]
                            last = None
                            for hh in range(8):
                                x1h = psQ[:, hh * 128:hh * 128 + 32]
                                x2h = psQ[:, hh * 128 + 32:hh * 128 + 64]
                                ch = cs[:, hh * 32:(hh + 1) * 32]
                                sh = sn[:, hh * 32:(hh + 1) * 32]
                                b = hh * 128
                                e.tensor_tensor(rtmp[:, b:b + 32], x2h, sh, ALU.mult)
                                e.tensor_tensor(rtmp[:, b + 32:b + 64], x1h, ch, ALU.mult)
                                e.tensor_tensor(rtmp[:, b + 64:b + 96], x1h, sh, ALU.mult)
                                last = e.tensor_tensor(rtmp[:, b + 96:b + 128], x2h, ch, ALU.mult)
                            return last
                        return f

                    def mk_ropeq_comb(qb=qb):
                        def f(e):
                            last = None
                            for hh in range(8):
                                b = hh * 128
                                e.tensor_tensor(qb[:, b:b + 32],
                                                rtmp[:, b + 32:b + 64], rtmp[:, b:b + 32], ALU.subtract)
                                last = e.tensor_tensor(qb[:, b + 32:b + 64],
                                                       rtmp[:, b + 64:b + 96], rtmp[:, b + 96:b + 128], ALU.add)
                            return last
                        return f
                    if not (Q2 & 16384):
                        P.step("vector", mk_ropeq_mul(), waits=rw, inc="eV")
                        P.step("vector", mk_ropeq_comb(), waits=[("eV", P.cnt("eV"))], inc="eV")
                    c_rope = P.cnt("eV")
                    last_rope = c_rope
                else:
                    P.step("vector", lambda e, a=x2p, b=sinm, o=rt(0): e.tensor_tensor(o, a, b, ALU.mult), waits=rw)
                    P.step("vector", lambda e, a=x1, b=cosm, o=rt(1): e.tensor_tensor(o, a, b, ALU.mult))
                    P.step("vector", lambda e, a=x1, b=sinm, o=rt(2): e.tensor_tensor(o, a, b, ALU.mult))
                    P.step("vector", lambda e, a=x2p, b=cosm, o=rt(3): e.tensor_tensor(o, a, b, ALU.mult), inc="eV")
                    c_rt = P.cnt("eV")
                    qb3 = qb.rearrange("p (h c) -> p h c", c=128)
                    P.step("vector", lambda e, o=qb3[:, :, 0:32], a=rt(1), b=rt(0): e.tensor_tensor(
                        o, a, b, ALU.subtract), waits=[("eV", c_rt)])
                    P.step("vector", lambda e, o=qb3[:, :, 32:64], a=rt(2), b=rt(3): e.tensor_tensor(
                        o, a, b, ALU.add), inc="eV")
                if PHASES != 21:
                    c_rope = P.cnt("eV")
                    last_rope = c_rope

                    def mk_qrest(qb=qb, psQ=psQ):
                        def f(e):
                            last = None
                            for hh in range(8):
                                last = e.activation(qb[:, hh * 128 + 64:(hh + 1) * 128],
                                                    psQ[:, hh * 128 + 64:(hh + 1) * 128],
                                                    AF.Copy)
                            return last
                        return f
                    if not (Q2 & 8192):
                        qw = [("eT", c_mm), ("eV", c_rope)]
                        P.step("scalar", mk_qrest(), waits=qw, inc="eS")
                    c_qrest = P.cnt("eS")
                    q_ev.append((c_rope, c_qrest))
                for g in range(2 if (Q2 & 2) else 0):
                    gid = len(p2g)
                    reg = ps[:, 2560 + (gid % 2) * 512:2560 + (gid % 2) * 512 + 512]
                    tw = [("eV", c_rope), ("eS", c_qrest)]
                    if gid >= 2:
                        tw.append(("eS", p2g[gid - 2]))

                    def mk_tr2(qb=qb, g=g, reg=reg):
                        def f(e):
                            last = None
                            for b in range(4):
                                hh = g * 4 + b
                                e.ldweights(qb[:, hh * 128:(hh + 1) * 128])
                                last = e.matmul(reg[:, b * 128:(b + 1) * 128],
                                                qb[:, hh * 128:(hh + 1) * 128], idsb,
                                                start=True, stop=True, skip_group_check=True)
                            return last
                        return f
                    P.step("tensor", mk_tr2(), waits=tw, inc="eT")
                    c_t2 = P.cnt("eT")

                    def mk_ev2(g=g, reg=reg, i=i, h2=h2):
                        def f(e):
                            last = None
                            for b in range(4):
                                h = h2 * 8 + g * 4 + b
                                last = e.activation(
                                    qt_all[:, h * S_OWN + (i - 4) * 128: h * S_OWN + (i - 3) * 128],
                                    reg[:, b * 128:(b + 1) * 128], AF.Copy)
                            return last
                        return f
                    P.step("scalar", mk_ev2(), waits=[("eT", c_t2)], inc="eS")
                    p2g.append(P.cnt("eS"))
                if not (Q2 & 2):
                    pass
                q_tr.append(P.cnt("eT"))
                qn += 1

            if h2 == 0 and (Q2 & 4):
                kb = knatb(i % 2)
                cosk = rcsb[:, i * 256:i * 256 + 32]
                sink = rssb[:, i * 256:i * 256 + 32]
                kx1, kx2 = psKV[:, 0:32], psKV[:, 32:64]
                krw = [("eT", c_mm), ("eV", last_rope)]
                if i >= 2:
                    krw.append(("eT", k_tr[i - 2]))
                    if DEBUG >= 3:
                        krw.append(("dD", P.cnt("dD")))
                P.step("vector", lambda e, a=kx2, b=sink: e.tensor_tensor(rtmp[:, 0:32], a, b, ALU.mult), waits=krw)
                P.step("vector", lambda e, a=kx1, b=cosk: e.tensor_tensor(rtmp[:, 32:64], a, b, ALU.mult))
                P.step("vector", lambda e, a=kx1, b=sink: e.tensor_tensor(rtmp[:, 64:96], a, b, ALU.mult))
                P.step("vector", lambda e, a=kx2, b=cosk: e.tensor_tensor(rtmp[:, 96:128], a, b, ALU.mult), inc="eV")
                c_kt4 = P.cnt("eV")
                P.step("vector", lambda e, kb=kb: e.tensor_tensor(kb[:, 0:32], rtmp[:, 32:64],
                                                                  rtmp[:, 0:32], ALU.subtract),
                       waits=[("eV", c_kt4)])
                P.step("vector", lambda e, kb=kb: e.tensor_tensor(kb[:, 32:64], rtmp[:, 64:96],
                                                                  rtmp[:, 96:128], ALU.add), inc="eV")
                kv_rv.append(P.cnt("eV"))
                last_rope = P.cnt("eV")
                if DEBUG >= 3 and i == 5:
                    P.step("scalar", lambda e, psKV=psKV: e.activation(
                        silu_t[:, 0:256], psKV[:, 0:256], AF.Copy), waits=[("eT", c_mm)], inc="eS")
                    dma("sync", dbg_kps, silu_t[:, 0:256], [("eS", P.cnt("eS"))], sem="dD")
                P.step("scalar", lambda e, kb=kb, psKV=psKV: e.activation(
                    kb[:, 64:128], psKV[:, 64:128], AF.Copy),
                    waits=[("eT", c_mm), ("eV", kv_rv[i])])
                P.step("scalar", lambda e, i=i, psKV=psKV: e.activation(
                    vsb[:, i * 128:(i + 1) * 128], psKV[:, 128:256], AF.Copy), inc="eS")
                kv_ev.append(P.cnt("eS"))
                if DEBUG >= 3 and i == 5:
                    dma("sync", dbg_kb, kb, [("eV", kv_rv[i]), ("eS", kv_ev[i])], sem="dD")
                if not (Q2 & 8):
                    k_tr.append(0)
                    p2g.append(p2g[-1] if p2g else 0)
                    pa_step()
                    pa_step()
                    continue
                gid = len(p2g)
                reg = ps[:, 2560 + (gid % 2) * 512:2560 + (gid % 2) * 512 + 128]
                tw = [("eV", kv_rv[i]), ("eS", kv_ev[i])]
                if gid >= 2:
                    tw.append(("eS", p2g[gid - 2]))
                def mk_trk(kb=kb, reg=reg):
                    def f(e):
                        e.ldweights(kb)
                        return e.matmul(reg, kb, idsb, start=True, stop=True,
                                        skip_group_check=True)
                    return f
                P.step("tensor", mk_trk(), waits=tw, inc="eT")
                k_tr.append(P.cnt("eT"))
                P.step("scalar", lambda e, reg=reg, i=i: e.activation(
                    ktb[:, i * 128:(i + 1) * 128], reg, AF.Copy),
                    waits=[("eT", k_tr[i])], inc="eS")
                p2g.append(P.cnt("eS"))
            pa_step()
            pa_step()

    while pa["i"] < PA_TOTAL and not (STRIP & 16):
        pa_step()
    P.barrier()

    if PHASES in (22, 23) or not (Q2 & 16):
        _replay(nc, P)
        return nc
    for wi in range(3):
        P.step("vector", lambda e, wi=wi: e.tensor_reduce(
            thparts[:, wi:wi + 1], absacc[:, wi * NHT:(wi + 1) * NHT], AX.X, ALU.add),
            inc="eV" if wi == 2 else None)
    c_thp = P.cnt("eV")
    P.step("tensor", lambda e: e.matmul(ps[0:1, 3072:3075], onescolb, thparts,
                                        start=True, stop=True, skip_group_check=True),
           waits=[("eV", c_thp)], inc="eT")
    P.step("vector", lambda e: e.tensor_copy(thsum, ps[0:1, 3072:3075]),
           waits=[("eT", P.cnt("eT"))], inc="eV")
    P.step("tensor", lambda e: e.matmul(ps[:, 3072:3075], onesk1fb[0:1, 0:128], thsum,
                                        start=True, stop=True, skip_group_check=True),
           waits=[("eV", P.cnt("eV"))], inc="eT")
    c_thb = P.cnt("eT")
    for col, sc_ in ((0, 3072), (2, 3073), (4, 3074)):
        P.step("vector", lambda e, col=col, sc_=sc_: e.tensor_scalar(
            thsb[:, col:col + 1], ps[:, sc_:sc_ + 1], 1.0 / (2 * NW_ELEM), None, ALU.mult),
            waits=[("eT", c_thb)] if col == 0 else ())
        P.step("vector", lambda e, col=col, sc_=sc_: e.tensor_scalar(
            thsb[:, col + 1:col + 2], ps[:, sc_:sc_ + 1], -1.0 / (2 * NW_ELEM), None, ALU.mult))
    for wi, sc_ in enumerate((3072, 3073, 3074)):
        P.step("vector", lambda e, wi=wi, sc_=sc_: e.tensor_scalar(
            ngs[:, wi:wi + 1], ps[:, sc_:sc_ + 1], -1.0 / NW_ELEM, None, ALU.mult),
            inc="eV" if wi == 2 else None)
    c_th_ready = P.cnt("eV")
    P.step("vector", None, waits=[("eV", c_th_ready)])

    if DEBUG:
        dma("sync", dbg_qt, qt_all, sem="dD")
        dma("sync", dbg_kt, ktb, sem="dD")
        dma("sync", dbg_v, vsb, sem="dD")

    wow = [("eT", P.cnt("eT"))]
    for dt in range(NDT):
        dma("gpsimd", wbig[:, dt * D:(dt + 1) * D],
            wo_d[dt * 128:(dt + 1) * 128, :], wow if dt == 0 else (), sem="dG")
    c_wo = P.cnt("dG")
    c_bo = dma("sync", bosb, bo_d, sem="dB")
    if PHASES <= 2 or PHASES in (21, 23, 25):
        _replay(nc, P)
        return nc

    NPAIR = 8
    c_exp = {}
    c_mask = {}
    c_rec = {}
    c_avm = {}
    c_avd = {}
    c_x2p = []
    c_x2sp = []
    c_xre = []
    c_ln2ap = []
    p45g = []
    c_ylast = 0
    c_wo_eT_prev = 0
    c_stageev_prev = 0

    for qt in range(NSTO):
        xw = []
        if qt >= 2:
            xw.append(("eP", c_x2p[qt - 2]))
        c_xre.append(dma("sync", xreb(qt % 2), x_d[(qt + 4) * 128:(qt + 5) * 128, :], xw,
                         sem="dR%d" % (qt % 2)))
        for pr in range(NPAIR):
            gp = qt * NPAIR + pr
            par = gp % 2
            psSC = ps[:, par * 1280:(par + 1) * 1280]
            exb = exptb(par)
            sw = [("eV", c_th_ready)]
            if gp >= 2:
                sw.append(("eS", c_exp[gp - 2]))
            if gp >= 1:
                sw.append(("eS", c_exp[gp - 1]))

            def mk_sc(qt=qt, pr=pr, psSC=psSC):
                def f(e):
                    last = None
                    for j in range(5):
                        kslc = ktb[:, (qt + j) * 128:(qt + j + 1) * 128]
                        e.ldweights(kslc)
                        for m in range(2):
                            h = pr * 2 + m
                            last = e.matmul(
                                psSC[:, m * 640 + j * 128:m * 640 + (j + 1) * 128], kslc,
                                qt_all[:, h * S_OWN + qt * 128:h * S_OWN + (qt + 1) * 128],
                                start=True, stop=True, skip_group_check=True)
                    return last
                return f
            P.step("tensor", mk_sc(), waits=sw, inc="eT")
            c_scm = P.cnt("eT")
            ew = [("eT", c_scm)]
            if gp >= 2:
                ew.append(("eT", c_avm[gp - 2]))
            P.step("scalar", lambda e, psSC=psSC, exb=exb: e.activation(
                exb, psSC, AF.Exp, scale=SM), waits=ew, inc="eS")
            c_exp[gp] = P.cnt("eS")
            mwk = [("eS", c_exp[gp])]
            if gp == 0:
                mwk.append(("dC", c_consts))
            for m in range(2):
                P.step("gpsimd", lambda e, exb=exb, m=m: e.tensor_tensor(
                    exb[:, m * 640:m * 640 + 128], exb[:, m * 640:m * 640 + 128],
                    mfarb, ALU.mult), waits=mwk if m == 0 else ())
                P.step("gpsimd", lambda e, exb=exb, m=m: e.tensor_tensor(
                    exb[:, m * 640 + 512:m * 640 + 640],
                    exb[:, m * 640 + 512:m * 640 + 640], mdiab, ALU.mult),
                    inc="eP" if m == 1 else None)
            c_mask[gp] = P.cnt("eP")
            dnw = [("eP", c_mask[gp])]
            if gp >= 1:
                dnw.append(("eV", c_avd[gp - 1]))

            def mk_dn(qt=qt, exb=exb):
                def f(e):
                    dn = ps[:, 2560:2816]
                    e.ldweights(onesk1b[0:1, :])
                    for j2, nb in enumerate((ncrb0, ncrb1)):
                        e.matmul(dn, onesk1b[0:1, :], nb[0:1, qt * 256:(qt + 1) * 256],
                                 start=(j2 == 0), stop=False, skip_group_check=True)
                    e.ldweights(onesb)
                    last = None
                    for m in range(2):
                        for j in range(5):
                            last = e.matmul(dn[:, m * 128:(m + 1) * 128], onesb,
                                            exb[:, m * 640 + j * 128: m * 640 + (j + 1) * 128],
                                            start=False, stop=(m == 1 and j == 4),
                                            skip_group_check=True)
                    return last
                return f
            P.step("tensor", mk_dn(), waits=dnw, inc="eT")
            c_dnm = P.cnt("eT")
            if DEBUG >= 3 and qt == 4 and pr == 0:
                spare = f32(2720, 256)
                P.step("scalar", lambda e, spare=spare: e.activation(spare, ps[:, 2560:2816], AF.Copy),
                       waits=[("eT", c_dnm)], inc="eS")
                dma("sync", dbg_dn, spare, [("eS", P.cnt("eS"))], sem="dD")
                dma("sync", dbg_ex, exb, [("eP", c_mask[gp])], sem="dD")
            rb = recb(par)
            rcw = [("eT", c_dnm)]
            if gp >= 2:
                rcw.append(("eV", c_avd[gp - 2]))
            P.step("vector", lambda e, rb=rb: e.reciprocal(rb, ps[:, 2560:2816]),
                   waits=rcw, inc="eV")
            c_rec[gp] = P.cnt("eV")
            avw = [("eP", c_mask[gp]), ("eV", c_rec[gp])]
            if gp >= 1:
                avw.append(("eV", c_avd[gp - 1]))

            def mk_av(qt=qt, exb=exb):
                def f(e):
                    av = ps[:, 2816:3072]
                    last = None
                    for j in range(5):
                        vslc = vsb[:, (qt + j) * 128:(qt + j + 1) * 128]
                        e.ldweights(vslc)
                        for m in range(2):
                            last = e.matmul(av[:, m * 128:(m + 1) * 128], vslc,
                                            exb[:, m * 640 + j * 128:m * 640 + (j + 1) * 128],
                                            start=(j == 0 and m == 0), stop=(j == 4),
                                            skip_group_check=True)
                    return last
                return f
            P.step("tensor", mk_av(), waits=avw, inc="eT")
            c_avm[gp] = P.cnt("eT")
            if DEBUG >= 3 and qt == 4 and pr == 0:
                spare2 = f32(2976, 256)
                P.step("scalar", lambda e, spare2=spare2: e.activation(spare2, ps[:, 2816:3072], AF.Copy),
                       waits=[("eT", c_avm[gp])], inc="eS")
                dma("sync", dbg_av, spare2, [("eS", P.cnt("eS"))], sem="dD")
            dvw = [("eT", c_avm[gp]), ("eV", c_rec[gp])]
            if qt > 0 and pr < 2:
                dvw.append(("eT", c_wo_eT_prev))
                if DEBUG >= 3:
                    dvw.append(("dD", P.cnt("dD")))
            P.step("vector", lambda e, pr=pr, rb=rb: e.tensor_tensor(
                avtb[:, pr * 256:(pr + 1) * 256], ps[:, 2816:3072], rb, ALU.mult), waits=dvw, inc="eV")
            c_avd[gp] = P.cnt("eV")
        c_avd_all = c_avd[qt * NPAIR + NPAIR - 1]
        if DEBUG:
            dma("sync", dbg_avt[qt:qt + 1, :].rearrange("r (p f) -> (r p) f", p=128),
                avtb, [("eV", c_avd_all)], sem="dD")

        for dfp in range(2):
            ww = [("eV", c_avd_all), ("dG", c_wo)]
            if qt > 0 and dfp == 0:
                ww.append(("eS", c_stageev_prev))
            if c_ylast:
                ww.append(("eV", c_ylast))

            def mk_wo(dfp=dfp):
                def f(e):
                    last = None
                    for hh in range(H):
                        aslc = avtb[:, hh * 128:(hh + 1) * 128]
                        e.ldweights(aslc)
                        for c in range(2):
                            dfc = dfp * 2 + c
                            last = e.matmul(ps[:, 3072 + c * 512:3072 + (c + 1) * 512], aslc,
                                            wbig[:, hh * 2048 + dfc * 512: hh * 2048 + (dfc + 1) * 512],
                                            start=(hh == 0), stop=(hh == H - 1),
                                            skip_group_check=True)
                    return last
                return f
            P.step("tensor", mk_wo(), waits=ww, inc="eT")
            c_wom = P.cnt("eT")
            P.step("vector", lambda e, dfp=dfp: e.tensor_tensor(
                ytile[:, dfp * 1024:(dfp + 1) * 1024], ps[:, 3072:4096],
                bosb[:, dfp * 1024:(dfp + 1) * 1024], ALU.add),
                waits=[("eT", c_wom), ("dB", c_bo)], inc="eV")
            c_ylast = P.cnt("eV")
        c_wo_eT_prev = P.cnt("eT")

        if DEBUG:
            dma("sync", dbg_y[qt:qt + 1, :].rearrange("r (p f) -> (r p) f", p=128),
                ytile, [("eV", c_ylast)], sem="dD")
        ln_moments(ytile, 2, f32(2720, 24), [("eV", P.cnt("eV"))])
        c_ag = P.cnt("eV")
        P.step("scalar", lambda e: e.activation(rstdb[:, 1:2], mvb[:, 3:4], AF.Ln, bias=EPS),
               waits=[("eV", c_ag)])
        P.step("scalar", lambda e: e.activation(rstdb[:, 1:2], rstdb[:, 1:2], AF.Exp, scale=-0.5),
               waits=[("eS", P.cnt("eS"))], inc="eS")
        c_rs = P.cnt("eS")
        ynw = [("eS", c_rs)]
        if DEBUG >= 3:
            ynw.append(("dD", P.cnt("dD")))
        P.step("vector", lambda e: e.tensor_scalar(
            ytile, ytile, mvb[:, 2:3], rstdb[:, 1:2], ALU.subtract, ALU.mult),
            waits=ynw, inc="eV")
        c_yn = P.cnt("eV")
        spw = [("eV", c_yn), ("dR%d" % (qt % 2), c_xre[qt])]
        if qt >= 1:
            spw.append(("dP", c_x2sp[qt - 1]))
            spw.append(("eV", c_ln2ap[qt - 1]))
        P.step("gpsimd", lambda e, qt=qt: e.tensor_tensor(
            x2tb, ytile, xreb(qt % 2), ALU.add), waits=spw, inc="eP")
        c_x2p.append(P.cnt("eP"))
        c_x2sp.append(dma("sync", x2_d[qt * 128:(qt + 1) * 128, :], x2tb,
                          [("eP", c_x2p[qt])], sem="dP"))
        ln_moments(x2tb, 4, f32(2720, 24),
                   [("eP", c_x2p[qt]), ("eV", P.cnt("eV"))])
        c_ag2 = P.cnt("eV")
        P.step("scalar", lambda e: e.activation(rstdb[:, 2:3], mvb[:, 5:6], AF.Ln, bias=EPS),
               waits=[("eV", c_ag2)])
        P.step("scalar", lambda e: e.activation(rstdb[:, 2:3], rstdb[:, 2:3], AF.Exp, scale=-0.5),
               waits=[("eS", P.cnt("eS"))], inc="eS")
        c_rs2 = P.cnt("eS")
        aw2 = [("eS", c_rs2)]
        if qt >= 1:
            aw2.append(("eT", c_xn2tr_prev))
        P.step("vector", lambda e: e.tensor_scalar(
            xn2tmp, x2tb, mvb[:, 4:5], rstdb[:, 2:3], ALU.subtract, ALU.mult),
            waits=aw2, inc="eV")
        c_ln2ap.append(P.cnt("eV"))
        c_ap2 = P.cnt("eV")
        for g in range(4):
            gid = qt * 4 + g
            reg = ps[:, 3072 + (gid % 2) * 512:3072 + (gid % 2) * 512 + 512]
            tw = [("eV", c_ap2), ("eV", c_ylast)]
            if gid >= 2:
                tw.append(("eS", p45g[gid - 2]))

            def mk_tr3(g=g, reg=reg):
                def f(e):
                    last = None
                    for b in range(4):
                        dt = g * 4 + b
                        e.ldweights(xn2tmp[:, dt * 128:(dt + 1) * 128])
                        last = e.matmul(reg[:, b * 128:(b + 1) * 128],
                                        xn2tmp[:, dt * 128:(dt + 1) * 128], idsb,
                                        start=True, stop=True, skip_group_check=True)
                    return last
                return f
            P.step("tensor", mk_tr3(), waits=tw, inc="eT")
            c_t3 = P.cnt("eT")

            def mk_ev3(g=g, reg=reg, qt=qt):
                def f(e):
                    last = None
                    for b in range(4):
                        dt = g * 4 + b
                        last = e.activation(
                            xn2t[:, dt * S_OWN + qt * 128: dt * S_OWN + (qt + 1) * 128],
                            reg[:, b * 128:(b + 1) * 128], AF.Copy)
                    return last
                return f
            P.step("scalar", mk_ev3(), waits=[("eT", c_t3)], inc="eS")
            p45g.append(P.cnt("eS"))
        c_xn2tr_prev = P.cnt("eT")
        c_stageev_prev = P.cnt("eS")
    fw6 = [("dP", NSTO * 16)]
    if DEBUG:
        fw6.append(("dD", P.cnt("dD")))
    P.step("scalar", None, waits=fw6)
    P.barrier()
    if PHASES <= 3:
        _replay(nc, P)
        return nc

    c_hmm = {}
    c_silu = {}
    c_gev = {}
    c_st1ev = {}
    c_st3ev = {}
    c_tr1 = {}
    c_tr3 = {}
    c_tn = {}
    c_w2sp = {}
    c_bq = {}
    slot = {"i": 0}

    def quant(ht, wsel, dst_tn, extra_b_waits=(), extra_p_waits=()):
        i = slot["i"]
        slot["i"] += 1
        buf = q_wt(i % 2)
        src = {0: w1_d, 1: w3_d, 2: w2t_d}[wsel][ht * 128:(ht + 1) * 128, :]
        lw = [("eV", c_th_ready)]
        if i >= 2:
            lw.append(("eV", c_bq[i - 2]))
        sW = "dW%d" % (i % 2)
        c_d = dma("sync", buf, src, lw, sem=sW)
        thc = {0: 0, 1: 2, 2: 4}[wsel]
        bw = [(sW, c_d)] + list(extra_b_waits)
        if i >= 1:
            bw.append(("eP", P.cnt("eP")))
        P.step("vector", lambda e, buf=buf: e.tensor_scalar(
            q_b1, buf, thsb[:, thc:thc + 1], None, ALU.is_gt), waits=bw)
        P.step("vector", lambda e, buf=buf: e.tensor_scalar(
            q_b2, buf, thsb[:, thc + 1:thc + 2], None, ALU.is_lt), inc="eV")
        c_bq[i] = P.cnt("eV")
        pw = [("eV", c_bq[i])] + list(extra_p_waits)
        P.step("gpsimd", lambda e, dst_tn=dst_tn: e.tensor_tensor(
            dst_tn, q_b2, q_b1, ALU.subtract), waits=pw, inc="eP")
        return P.cnt("eP")

    for ht in range(NHT):
        par = ht % 2
        tn1, tn3, tn2 = q_tn1(par), q_tn3(par), q_tn2(par)
        ep1 = [] if ht < 2 else [("eT", c_tr1[ht - 2])]
        c_tn[(ht, 0)] = quant(ht, 0, tn1, extra_p_waits=ep1)
        ep3 = [] if ht < 2 else [("eT", c_tr3[ht - 2])]
        c_tn[(ht, 1)] = quant(ht, 1, tn3, extra_p_waits=ep3)
        ep2 = [] if ht < 2 else [("dQ", P.cnt("dQ"))]
        c_tn[(ht, 2)] = quant(ht, 2, tn2, extra_p_waits=ep2)
        c_w2sp[ht] = dma("sync", w2q_d[ht:ht + 1, :].rearrange("r (p f) -> (r p) f", p=128),
                         tn2, [("eP", c_tn[(ht, 2)])], sem="dQ")

        if ht == 0:
            P._c_twave = {0: 0, 1: 0}
        for wsel, tn, st in ((0, tn1, stage1(par)), (1, tn3, stage3(par))):
            for wv in range(2):
                treg = ps[:, 2048 + wv * 1024:2048 + (wv + 1) * 1024]
                tw = [("eP", c_tn[(ht, wsel)]), ("eS", P._c_twave[wv])]

                def mk_tr(tn=tn, wv=wv, treg=treg):
                    def f(e):
                        last = None
                        for b in range(8):
                            dt = wv * 8 + b
                            e.ldweights(tn[:, dt * 128:(dt + 1) * 128])
                            last = e.matmul(treg[:, b * 128:(b + 1) * 128],
                                            tn[:, dt * 128:(dt + 1) * 128], idsb,
                                            start=True, stop=True, skip_group_check=True)
                        return last
                    return f
                P.step("tensor", mk_tr(), waits=tw, inc="eT")
                c_trw = P.cnt("eT")
                evw = [("eT", c_trw)]
                if ht >= 2:
                    evw.append(("eT", c_hmm[ht - 2]))
                P.step("scalar", lambda e, st=st, wv=wv, treg=treg: e.activation(
                    st[:, wv * 1024:(wv + 1) * 1024], treg, AF.Copy), waits=evw, inc="eS")
                P._c_twave[wv] = P.cnt("eS")
            if wsel == 0:
                c_tr1[ht] = P.cnt("eT")
                c_st1ev[ht] = P.cnt("eS")
            else:
                c_tr3[ht] = P.cnt("eT")
                c_st3ev[ht] = P.cnt("eS")

        def mk_hhalf(st, base):
            def f(e):
                last = None
                for dt in range(NDT):
                    lhs = st[:, dt * 128:(dt + 1) * 128]
                    e.ldweights(lhs)
                    for sc in range(2):
                        last = e.matmul(ps[:, base + sc * 512:base + (sc + 1) * 512], lhs,
                                        xn2t[:, dt * S_OWN + sc * 512:dt * S_OWN + (sc + 1) * 512],
                                        start=(dt == 0), stop=(dt == NDT - 1),
                                        skip_group_check=True)
                return last
            return f
        hw1 = [("eS", c_st1ev[ht])]
        hw3 = [("eS", c_st3ev[ht])]
        if ht >= 1:
            hw1.append(("eS", c_silu[ht - 1]))
            hw3.append(("eV", c_gev[ht - 1]))
        P.step("tensor", mk_hhalf(stage1(par), 0), waits=hw1, inc="eT")
        c_h1 = P.cnt("eT")
        P.step("tensor", mk_hhalf(stage3(par), 1024), waits=hw3, inc="eT")
        c_hmm[ht] = P.cnt("eT")
        P.step("scalar", lambda e: e.activation(
            silu_t, ps[:, 0:1024], AF.Silu, scale=ngs[:, 0:1]),
            waits=[("eT", c_h1)], inc="eS")
        c_silu[ht] = P.cnt("eS")
        P.step("vector", lambda e, ht=ht: e.scalar_tensor_tensor(
            gt[:, ht * S_OWN:(ht + 1) * S_OWN], silu_t, ngs[:, 1:2],
            ps[:, 1024:2048], ALU.mult, ALU.mult),
            waits=[("eS", c_silu[ht]), ("eT", c_hmm[ht])], inc="eV")
        c_gev[ht] = P.cnt("eV")
    P.barrier()
    if PHASES <= 4:
        _replay(nc, P)
        return nc

    c_w2mm = {}
    c_oev = 0
    for dfc in range(4):
        for ht in range(NHT):
            par = ht % 2
            sl = w2slb(par)
            lw = [("dQ", NHT * 16)]
            if ht >= 2:
                lw.append(("eT", c_w2mm[(dfc, ht - 2)]))
            elif dfc > 0:
                lw.append(("eT", c_w2mm[(dfc - 1, NHT - 1)]))
            src = w2q_d[ht:ht + 1, :].rearrange("r (p f) -> (r p) f", p=128)[
                :, dfc * 512:(dfc + 1) * 512]
            sL = "dL%d" % par
            c_sl = dma("sync", sl, src, lw, sem=sL)
            mw = [(sL, c_sl)]
            if dfc > 0 and ht == 0:
                mw.append(("eV", c_oev))

            def mk_w2(ht=ht, sl=sl):
                def f(e):
                    last = None
                    for st in range(NSTO):
                        lhs = gt[:, ht * S_OWN + st * 128:ht * S_OWN + (st + 1) * 128]
                        e.ldweights(lhs)
                        last = e.matmul(ps[:, st * 512:(st + 1) * 512], lhs, sl,
                                        start=(ht == 0), stop=(ht == NHT - 1),
                                        skip_group_check=True)
                    return last
                return f
            P.step("tensor", mk_w2(), waits=mw, inc="eT")
            c_w2mm[(dfc, ht)] = P.cnt("eT")
        c_last = P.cnt("eT")
        for st in range(NSTO):
            P.step("vector", lambda e, st=st, dfc=dfc: e.tensor_scalar(
                osb[:, st * 2048 + dfc * 512:st * 2048 + (dfc + 1) * 512],
                ps[:, st * 512:(st + 1) * 512], ngs[:, 2:3], None, ALU.mult),
                waits=[("eT", c_last)] if st == 0 else (),
                inc="eV" if st == NSTO - 1 else None)
        c_oev = P.cnt("eV")
    c_w2ev = c_oev

    c_x3 = []
    for st in range(NSTO):
        rw = [("dP", NSTO * 16)]
        if st >= 2:
            rw.append(("eP", c_x3[st - 2]))
        sE = "dE%d" % (st % 2)
        c_r = dma("sync", x2re(st % 2), x2_d[st * 128:(st + 1) * 128, :], rw, sem=sE)
        ob = osb[:, st * 2048:(st + 1) * 2048]
        ln_moments(ob, 6, f32(3232, 24), [("eV", P.cnt("eV"))])
        c_ag3 = P.cnt("eV")
        P.step("scalar", lambda e: e.activation(rstdb[:, 3:4], mvb[:, 7:8], AF.Ln, bias=EPS),
               waits=[("eV", c_ag3)])
        P.step("scalar", lambda e: e.activation(rstdb[:, 3:4], rstdb[:, 3:4], AF.Exp, scale=-0.5),
               waits=[("eS", P.cnt("eS"))], inc="eS")
        c_rs3 = P.cnt("eS")
        P.step("vector", lambda e, ob=ob: e.tensor_scalar(
            ob, ob, mvb[:, 6:7], rstdb[:, 3:4], ALU.subtract, ALU.mult),
            waits=[("eS", c_rs3)], inc="eV")
        c_yn2 = P.cnt("eV")
        xb = x3tb(st % 2)
        fw = [("eV", c_yn2), (sE, c_r)]
        if st >= 2:
            fw.append(("dO", st * 16))
        P.step("gpsimd", lambda e, xb=xb, ob=ob, st=st: e.tensor_tensor(
            xb, ob, x2re(st % 2), ALU.add), waits=fw, inc="eP")
        c_x3.append(P.cnt("eP"))
        dma("sync", out_d[st * 128:(st + 1) * 128, :], xb, [("eP", c_x3[st])], sem="dO")
    P.step("sync", None, waits=[("dO", NSTO * 16)])
    _replay(nc, P)
    return nc


def _replay(nc, P):
    with nc.Block() as block:
        @block.sync
        def _(e):
            for s in P.steps["sync"]:
                s(e)

        @block.tensor
        def _(e):
            for s in P.steps["tensor"]:
                s(e)

        @block.vector
        def _(e):
            for s in P.steps["vector"]:
                s(e)

        @block.scalar
        def _(e):
            for s in P.steps["scalar"]:
                s(e)

        @block.gpsimd
        def _(e):
            for s in P.steps["gpsimd"]:
                s(e)


def _host_prep(L0_attn_x, L0_attn_Wq, L0_attn_Wk, L0_attn_Wv, L0_attn_Wo,
               L0_attn_bo, L0_ffn_w1, L0_ffn_w3, L0_ffn_w2):
    x = np.ascontiguousarray(np.asarray(L0_attn_x, np.float32).reshape(SEQ, D))
    wq = np.ascontiguousarray(np.asarray(L0_attn_Wq, np.float32))
    wk = np.ascontiguousarray(np.asarray(L0_attn_Wk, np.float32))
    wv = np.ascontiguousarray(np.asarray(L0_attn_Wv, np.float32))
    wo = np.ascontiguousarray(np.asarray(L0_attn_Wo, np.float32))
    bo = np.ascontiguousarray(np.broadcast_to(
        np.asarray(L0_attn_bo, np.float32)[None, :], (128, D)))
    w1 = np.zeros((HIDP, D), np.float32)
    w1[:HID] = np.asarray(L0_ffn_w1, np.float32)
    w3 = np.zeros((HIDP, D), np.float32)
    w3[:HID] = np.asarray(L0_ffn_w3, np.float32)
    w2t = np.zeros((HIDP, D), np.float32)
    w2t[:HID] = np.ascontiguousarray(np.asarray(L0_ffn_w2, np.float32).T)

    inv = np.exp(-np.log(10000.0) * (np.arange(0, ROPE_D, 2, dtype=np.float64) / ROPE_D))
    in_maps = []
    for c in range(NCORES):
        lo = c * S_OWN - WIN
        if c == 0:
            xc = np.concatenate([np.zeros((WIN, D), np.float32), x[0:S_OWN]], 0)
        else:
            xc = np.ascontiguousarray(x[lo:lo + S_TOT])
        pos = np.maximum(np.arange(lo, lo + S_TOT, dtype=np.float64), 0.0)
        th = pos[:, None] * inv[None, :]
        cosT = np.ascontiguousarray(np.tile(np.cos(th).astype(np.float32), (1, 8)))
        sinT = np.ascontiguousarray(np.tile(np.sin(th).astype(np.float32), (1, 8)))
        qabs = np.arange(c * S_OWN, (c + 1) * S_OWN)
        corr = np.maximum(0, WIN - 1 - qabs).astype(np.float64)
        hi = np.minimum(corr, 256.0)
        lo2 = corr - hi
        ncr = np.zeros((2, 2048), np.float32)
        for qt in range(NSTO):
            for m in range(2):
                ncr[0, qt * 256 + m * 128:qt * 256 + (m + 1) * 128] = -hi[qt * 128:(qt + 1) * 128]
                ncr[1, qt * 256 + m * 128:qt * 256 + (m + 1) * 128] = -lo2[qt * 128:(qt + 1) * 128]
        in_maps.append({
            "x": xc, "wq": wq, "wk": wk, "wv": wv, "wo": wo, "bo": bo,
            "ropec": cosT, "ropes": sinT,
            "ncr": ncr.astype(ml_dtypes.bfloat16),
            "w1": w1, "w3": w3, "w2t": w2t,
        })
    return in_maps


NRUN = NCORES


def kernel(**inputs):
    in_maps = _host_prep(**inputs)[:NRUN]
    nc = build_program()
    res = run_bass_kernel_spmd(nc, in_maps, core_ids=list(range(NRUN)))
    if NRUN < NCORES:
        return np.zeros((1, SEQ, D), np.float32)
    out = np.concatenate([res.results[c]["out"] for c in range(NCORES)], 0)
    return out.reshape(1, SEQ, D)


# version 11 (final)
# speedup vs baseline: 1.0450x; 1.0239x over previous
import numpy as np
import ml_dtypes
import concourse.bass as bass
import concourse.mybir as mybir
from concourse.bass_utils import run_bass_kernel_spmd

F32 = mybir.dt.float32
BF16 = mybir.dt.bfloat16
AF = mybir.ActivationFunctionType
ALU = mybir.AluOpType
AX = mybir.AxisListType

NCORES = 8
D = 2048
H = 16
HD = 128
ROPE_D = 64
WIN = 512
SEQ = 8192
S_OWN = 1024
S_TOT = 1536
NST = 12
NSTO = 8
NDT = 16
HID = 5461
HIDP = 5504
NHT = 43
EPS = 1e-5
SM = float(1.0 / np.sqrt(HD))
NW_ELEM = HID * D

DEBUG = False
SCRATCH = 12288
NCOLSG = 54048
RACEDET = False
NDEV = NCORES = 8
PHASES = 9
P1LIM = 99
BARENG = "ytvsp"
STRIP = 0
Q2 = 63
ENGS = ("sync", "tensor", "vector", "scalar", "gpsimd")
ESEM = {"sync": "eY", "tensor": "eT", "vector": "eV", "scalar": "eS", "gpsimd": "eP"}


class Plan:
    def __init__(self, nc):
        self.nc = nc
        self.steps = {e: [] for e in ENGS}
        self.sems = {}
        self.counts = {}

    def sem(self, name):
        if name not in self.sems:
            self.sems[name] = self.nc.alloc_semaphore(name)
            self.counts[name] = 0
        return self.sems[name]

    def cnt(self, name):
        return self.counts[name]

    def step(self, eng, fn=None, waits=(), inc=None, incn=1, nochain=False):
        waits = list(waits)
        for s, v in waits:
            assert v <= self.counts[s], f"wait on future {s}:{v}>{self.counts[s]}"
        if inc is not None:
            self.sem(inc)
            self.counts[inc] += incn
        sems = self.sems

        def run(engine, fn=fn, waits=tuple(waits), inc=inc, incn=incn):
            for s, v in waits:
                if v > 0:
                    engine.wait_ge(sems[s], v)
            inst = fn(engine) if fn is not None else None
            if inc is not None:
                if inst is None:
                    engine.sem_inc(sems[inc], incn)
                else:
                    inst.then_inc(sems[inc], incn)
        self.steps[eng].append(run)
        return self

    def barrier(self, tiny=None):
        t = tiny or self.tinyops
        for e in ENGS:
            self.step(e, t.get(e), inc=ESEM[e])
        tot = [(ESEM[e], self.cnt(ESEM[e])) for e in ENGS]
        for e in ENGS:
            self.step(e, None, waits=tot)


def build_program():
    nc = bass.Bass("TRN2", target_bir_lowering=False, num_devices=NDEV,
                   dynamic_dma_scratch_size=SCRATCH, detect_race_conditions=RACEDET)

    x_d = nc.dram_tensor("x", [S_TOT, D], F32, kind="ExternalInput").ap()
    wq_d = nc.dram_tensor("wq", [D, D], F32, kind="ExternalInput").ap()
    wk_d = nc.dram_tensor("wk", [D, HD], F32, kind="ExternalInput").ap()
    wv_d = nc.dram_tensor("wv", [D, HD], F32, kind="ExternalInput").ap()
    wo_d = nc.dram_tensor("wo", [D, D], F32, kind="ExternalInput").ap()
    bo_d = nc.dram_tensor("bo", [128, D], F32, kind="ExternalInput").ap()
    rc_d = nc.dram_tensor("ropec", [S_TOT, 256], F32, kind="ExternalInput").ap()
    rs_d = nc.dram_tensor("ropes", [S_TOT, 256], F32, kind="ExternalInput").ap()
    ncr_d = nc.dram_tensor("ncr", [2, 2048], BF16, kind="ExternalInput").ap()
    w1_d = nc.dram_tensor("w1", [HIDP, D], F32, kind="ExternalInput").ap()
    w3_d = nc.dram_tensor("w3", [HIDP, D], F32, kind="ExternalInput").ap()
    w2t_d = nc.dram_tensor("w2t", [HIDP, D], F32, kind="ExternalInput").ap()
    out_d = nc.dram_tensor("out", [S_OWN, D], F32, kind="ExternalOutput").ap()
    x2_d = nc.dram_tensor("x2spill", [S_OWN, D], F32, kind="Internal").ap()
    if DEBUG:
        dbg_kps = nc.dram_tensor("dbg_kps", [128, 256], F32, kind="ExternalOutput").ap()
        dbg_ex = nc.dram_tensor("dbg_ex", [128, 1280], BF16, kind="ExternalOutput").ap()
        dbg_dn = nc.dram_tensor("dbg_dn", [128, 256], F32, kind="ExternalOutput").ap()
        dbg_av = nc.dram_tensor("dbg_av", [128, 256], F32, kind="ExternalOutput").ap()
        dbg_wk = nc.dram_tensor("dbg_wk", [128, NDT * HD], BF16, kind="ExternalOutput").ap()
        dbg_kb = nc.dram_tensor("dbg_kb", [128, HD], BF16, kind="ExternalOutput").ap()
        dbg_qt = nc.dram_tensor("dbg_qt", [128, H * S_OWN], BF16, kind="ExternalOutput").ap()
        dbg_kt = nc.dram_tensor("dbg_kt", [128, S_TOT], BF16, kind="ExternalOutput").ap()
        dbg_v = nc.dram_tensor("dbg_v", [128, NST * HD], BF16, kind="ExternalOutput").ap()
        dbg_avt = nc.dram_tensor("dbg_avt", [NSTO, 128 * H * 128], BF16, kind="ExternalOutput").ap()
        dbg_y = nc.dram_tensor("dbg_y", [NSTO, 128 * D], F32, kind="ExternalOutput").ap()
    w2q_d = nc.dram_tensor("w2q", [NHT, 128 * D], BF16, kind="Internal").ap()

    rr = np.arange(128)
    ident_t = nc.inline_tensor(np.eye(128, dtype=ml_dtypes.bfloat16), name="identc")
    ones_t = nc.inline_tensor(np.ones((128, 128), dtype=ml_dtypes.bfloat16), name="onesc")
    onesk1_t = nc.inline_tensor(np.ones((1, 128), dtype=ml_dtypes.bfloat16), name="onesk1c")
    onesk1f_t = nc.inline_tensor(np.ones((1, 128), dtype=np.float32), name="onesk1fc")
    onescol_t = nc.inline_tensor(np.ones((128, 1), dtype=np.float32), name="onescolc")
    mfar_t = nc.inline_tensor((rr[:, None] >= rr[None, :] + 1).astype(ml_dtypes.bfloat16), name="mfarc")
    mdia_t = nc.inline_tensor((rr[:, None] <= rr[None, :]).astype(ml_dtypes.bfloat16), name="mdiac")
    constvals = [0.0, 1.0, EPS, -0.5, SM, 0.125, 256.0, 1.0 / 2048.0]
    ct_t = nc.inline_tensor(np.tile(np.array(constvals, np.float32)[None, :], (128, 1)), name="ctensc")

    NCOLS = NCOLSG
    arena = nc.alloc_sbuf_tensor("arena", [128, NCOLS], F32)
    arenab = arena.bitcast(BF16)

    def f32(off, n, rows=128):
        assert off >= 0 and (off + n <= NCOLS or STRIP)
        return arena[0:rows, off:off + n] if off + n <= NCOLS else arena[0:rows, 0:n]

    def bf(off, n, rows=128):
        assert off >= 0 and (off + n <= NCOLS or STRIP)
        return (arenab[0:rows, 2 * off:2 * (off + n)] if off + n <= NCOLS
                else arenab[0:rows, 0:2 * n])

    csb = f32(0, 8)
    onescolb = f32(8, 1)
    thparts = f32(16, 3)
    thsum = f32(24, 3, rows=1)
    thsb = f32(32, 8)
    ngs = f32(40, 3)
    absacc = f32(48, 132)
    statb = f32(184, 24)
    mvb = f32(208, 8)
    rstdb = f32(216, 8)
    onesk1fb = f32(224, 128, rows=1)
    idsb = bf(352, 64)
    onesb = bf(416, 64)
    onesk1b = bf(480, 64, rows=1)
    mfarb = bf(544, 64)
    mdiab = bf(608, 64)
    ncrb0 = bf(672, 1024, rows=1)
    ncrb1 = bf(1696, 1024, rows=1)
    w2slb = lambda p: bf(2720 + p * 256, 256)
    knatb = lambda p: bf(3232 + p * 64, 64)
    SM_END = 3360

    WB = SM_END
    wbig_half = bf(WB, 8192)
    wbig = bf(WB, 16384)
    wtinA = lambda p: f32(WB + 8192 + p * 2048, 2048)
    q_wt = lambda p: f32(WB + p * 2048, 2048)
    q_b1 = bf(WB + 4096, 1024)
    q_b2 = bf(WB + 5120, 1024)
    q_tn1 = lambda p: bf(WB + 6144 + p * 1024, 1024)
    q_tn3 = lambda p: bf(WB + 8192 + p * 1024, 1024)
    q_tn2 = lambda p: bf(WB + 10240 + p * 1024, 1024)
    silu_t = f32(WB + 12288, 1024)
    osb = f32(WB, 16384)
    WB_END = WB + 16384

    A2 = WB_END
    xn1t = bf(A2, 12288)
    xn2t = bf(A2, 8192)
    stage1 = lambda p: bf(A2 + 8192 + p * 1024, 1024)
    stage3 = lambda p: bf(A2 + 10240 + p * 1024, 1024)
    x2tb = f32(A2 + 8192, 2048)
    xn2tmp = bf(A2 + 10240, 1024)
    x2re = lambda p: f32(A2 + p * 2048, 2048)
    x3tb = lambda p: f32(A2 + 4096 + p * 2048, 2048)
    A2_END = A2 + 12288

    A3 = A2_END
    qt_all = bf(A3, 8192)
    ktb = bf(A3 + 8192, 768)
    vsb = bf(A3 + 8960, 768)
    rcsb = f32(A3 + 9728, 3072)
    rssb = f32(A3 + 12800, 3072)
    wksb = bf(A3 + 15872, 1024)
    wvsb = bf(A3 + 16896, 1024)
    exptb = lambda p: bf(A3 + 17920 + p * 640, 640)
    recb = lambda p: f32(A3 + 19200 + p * 256, 256)
    avtb = bf(A3 + 19712, 1024)
    ytile = f32(A3 + 9728, 2048)
    xreb = lambda p: f32(A3 + 11776, 2048) if p == 0 else f32(A3 + 13824, 2048)
    bosb = f32(A3 + 15872, 2048)
    xinb = lambda p: f32(A3 + p * 2048, 2048)
    xn1tmpb = lambda p: bf(A3 + 4096 + p * 1024, 1024)
    qnatb = lambda p: bf(A3 + 17920 + p * 512, 512)
    rtmp = f32(A3 + 18944, 1024)
    gt = bf(A3, 22016)
    A3_END = A3 + 22016
    assert A3_END <= NCOLS or STRIP & 128, A3_END

    ps = nc.alloc_psum_tensor("ps", [128, 4096], F32)
    psb = ps.bitcast(BF16)

    if STRIP & 512:
        mm_a = bf(A2, 256)
        mm_ev = f32(A3, 24 * 512)
        dsem = nc.alloc_semaphore("dsem")
        tsem = nc.alloc_semaphore("tsem")
        ssem = nc.alloc_semaphore("ssem")
        osem = nc.alloc_semaphore("osem")
        with nc.Block() as block:
            @block.gpsimd
            def _(g):
                g.dma_start(out=mm_a, in_=wq_d[0:128, 0:512]).then_inc(dsem, 16)

            @block.tensor
            def _(t):
                t.wait_ge(dsem, 16)
                for it in range(24):
                    par = it % 2
                    reg = ps[:, par * 512:(par + 1) * 512]
                    if it >= 2:
                        t.wait_ge(ssem, it - 1)
                    last = None
                    for k in range(16):
                        t.ldweights(mm_a[:, 0:128])
                        last = t.matmul(reg, mm_a[:, 0:128], mm_a[:, 0:512],
                                        start=(k == 0), stop=(k == 15),
                                        skip_group_check=True)
                    last.then_inc(tsem, 1)

            @block.scalar
            def _(s):
                for it in range(24):
                    par = it % 2
                    s.wait_ge(tsem, it + 1)
                    s.activation(mm_ev[:, it * 512:(it + 1) * 512],
                                 ps[:, par * 512:(par + 1) * 512],
                                 AF.Copy).then_inc(ssem, 1)

            @block.sync
            def _(sy):
                sy.wait_ge(ssem, 24)
                sy.dma_start(out=out_d[0:128, 0:512], in_=mm_ev[:, 0:512]).then_inc(osem, 16)
                sy.wait_ge(osem, 16)
        return nc

    P = Plan(nc)
    for s in ("dC", "dB", "dD", "dX0", "dX1", "dA0", "dA1", "dR0", "dR1", "dP",
              "dW0", "dW1", "dQ", "dL0", "dL1", "dE0", "dE1", "dG", "dO",
              "eY", "eT", "eV", "eS", "eP"):
        P.sem(s)
    P.tinyops = {
        "tensor": lambda e: e.ldweights(idsb),
        "vector": lambda e: e.tensor_copy(statb[:, 0:1], csb[:, 0:1]),
        "scalar": lambda e: e.activation(statb[:, 1:2], csb[:, 0:1], AF.Copy),
        "gpsimd": lambda e: e.tensor_copy(statb[:, 2:3], csb[:, 0:1]),
    }

    if STRIP & 1024:
        if STRIP & 8192:
            for i, v in enumerate(constvals):
                nc.const_aps.register(F32, v, csb[:, i:i + 1])
        mm_a = bf(A2, 256)
        mm_ev = f32(A3, 24 * 512)
        P.step("gpsimd", lambda e: e.dma_start(out=mm_a, in_=wq_d[0:128, 0:512]),
               inc="dG", incn=16)
        c_ld = P.cnt("dG")
        prev_ev = {}
        for it in range(24):
            par = it % 2
            reg = ps[:, par * 512:(par + 1) * 512]
            w = [("dG", c_ld)] if (it == 0 or STRIP & 4096) else []
            if it >= 2:
                w.append(("eS", prev_ev[it - 2]))

            def mk_g(reg=reg):
                def f(e):
                    last = None
                    for k in range(16):
                        e.ldweights(mm_a[:, 0:128])
                        last = e.matmul(reg, mm_a[:, 0:128], mm_a[:, 0:512],
                                        start=(k == 0), stop=(k == 15),
                                        skip_group_check=True)
                    return last
                return f
            P.step("tensor", mk_g(), waits=w, inc="eT")
            c_g = P.cnt("eT")
            P.step("scalar", lambda e, it=it, reg=reg: e.activation(
                mm_ev[:, it * 512:(it + 1) * 512], reg, AF.Copy),
                waits=[("eT", c_g)], inc="eS")
            prev_ev[it] = P.cnt("eS")
        if STRIP & 2048 and BARENG == "L":
            P.step("tensor", lambda e: e.ldweights(idsb), inc="eT")
        elif STRIP & 2048 and BARENG == "T":
            P.step("tensor", None, inc="eT")
        elif STRIP & 2048 and BARENG == "W":
            P.step("tensor", None, waits=[("eT", 24)])
        elif STRIP & 2048:
            emap = {"y": "sync", "t": "tensor", "v": "vector", "s": "scalar", "p": "gpsimd"}
            sub = [emap[c] for c in BARENG]
            for e in sub:
                P.step(e, P.tinyops.get(e), inc=ESEM[e])
            tot = [(ESEM[e], P.cnt(ESEM[e])) for e in sub]
            for e in sub:
                P.step(e, None, waits=tot)
        if not (STRIP & 16384):
            P.step("sync", lambda e: e.dma_start(out=out_d[0:128, 0:512], in_=mm_ev[:, 0:512]),
                   waits=[("eS", prev_ev[23])], inc="dO", incn=16)
            P.step("sync", None, waits=[("dO", 16)])
        _replay(nc, P)
        return nc

    def dma(eng, out, in_, waits=(), sem="dS"):
        P.step(eng, lambda e, o=out, i=in_: e.dma_start(out=o, in_=i),
               waits=waits, inc=sem, incn=16)
        return P.cnt(sem)

    def ln_moments(buf, mva, sc, waits_first):
        def vstep(fn):
            P.step("vector", fn, waits=[("eV", P.cnt("eV"))], inc="eV")
        P.step("vector", lambda e, b=buf: e.bn_stats(statb[:, 0:6], b[:, 0:512]),
               waits=waits_first)
        for j in range(1, 4):
            P.step("vector", lambda e, b=buf, j=j: e.bn_stats(
                statb[:, 6 * j:6 * j + 6], b[:, 512 * j:512 * (j + 1)]),
                inc="eV" if j == 3 else None)
        vstep(lambda e, sc=sc: e.tensor_tensor(sc, statb, statb, ALU.mult))
        for (lo, mid, hi) in ((0, 12, 24), (0, 6, 12), (0, 3, 6)):
            vstep(lambda e, lo=lo, mid=mid, hi=hi: e.tensor_tensor(
                statb[:, lo:mid], statb[:, lo:mid], statb[:, mid:hi], ALU.add))
            vstep(lambda e, sc=sc, lo=lo, mid=mid, hi=hi: e.tensor_tensor(
                sc[:, lo:mid], sc[:, lo:mid], sc[:, mid:hi], ALU.add))
        vstep(lambda e, mva=mva: e.tensor_tensor(
            mvb[:, mva:mva + 1], statb[:, 1:2], csb[:, 5:6], ALU.mult))
        vstep(lambda e, sc=sc: e.tensor_tensor(
            sc[:, 1:2], sc[:, 1:2], csb[:, 6:7], ALU.mult))
        vstep(lambda e, sc=sc, mva=mva: e.tensor_tensor(
            sc[:, 3:4], mvb[:, mva:mva + 1], mvb[:, mva:mva + 1], ALU.mult))
        vstep(lambda e, sc=sc: e.tensor_tensor(
            sc[:, 2:3], sc[:, 1:2], statb[:, 2:3], ALU.add))
        vstep(lambda e, sc=sc: e.tensor_tensor(
            sc[:, 0:1], sc[:, 2:3], csb[:, 7:8], ALU.mult))
        vstep(lambda e, sc=sc, mva=mva: e.tensor_tensor(
            mvb[:, mva + 1:mva + 2], sc[:, 0:1], sc[:, 3:4], ALU.subtract))

    clist = [(ct_t, csb), (onescol_t, onescolb), (onesk1f_t, onesk1fb),
             (ident_t, idsb), (ones_t, onesb), (onesk1_t, onesk1b),
             (mfar_t, mfarb), (mdia_t, mdiab)]
    if STRIP & 1:
        clist = [(ident_t, idsb)]
    for t, dst in clist:
        dma("sync", dst, t.ap(), sem="dC")
    if not (STRIP & 1):
        dma("sync", ncrb0, ncr_d[0:1, :], sem="dC")
        dma("sync", ncrb1, ncr_d[1:2, :], sem="dC")
    for t in range(0 if STRIP & 1 else NST):
        dma("sync", rcsb[:, t * 256:(t + 1) * 256],
            rc_d[t * 128:(t + 1) * 128, :], sem="dC")
        dma("sync", rssb[:, t * 256:(t + 1) * 256],
            rs_d[t * 128:(t + 1) * 128, :], sem="dC")
    c_consts = P.cnt("dC")
    for i, v in enumerate(constvals):
        nc.const_aps.register(F32, v, csb[:, i:i + 1])

    for dt in range(0 if (PHASES == 130 or STRIP & 2) else NDT):
        dma("gpsimd", wksb[:, dt * HD:(dt + 1) * HD],
            wk_d[dt * 128:(dt + 1) * 128, :], sem="dG")
        dma("gpsimd", wvsb[:, dt * HD:(dt + 1) * HD],
            wv_d[dt * 128:(dt + 1) * 128, :], sem="dG")
    c_kvw = P.cnt("dG")
    wq_r = wq_d.rearrange("(dt p) (hf f) -> p dt hf f", p=128, f=1024)
    for dt in range(0 if (PHASES == 130 or STRIP & 2) else NDT):
        dma("gpsimd", wbig_half[:, dt * 1024:(dt + 1) * 1024],
            wq_r[:, dt, 0, :], sem="dG")
    c_wq = [P.cnt("dG"), None]
    if PHASES <= -1:
        P.step("sync", None, waits=[("dC", c_consts)])
        P.step("gpsimd", None, waits=[("dG", P.cnt("dG"))])
        _replay(nc, P)
        return nc
    if PHASES <= 0:
        P.step("sync", None, waits=[("dC", c_consts)])
        P.step("gpsimd", None, waits=[("dG", P.cnt("dG"))])
        P.barrier()
        _replay(nc, P)
        return nc
    if DEBUG >= 3:
        dma("sync", dbg_wk, wksb, [("dG", c_wq[0])], sem="dD")

    if STRIP & 128:
        mm_a = bf(A2, 256)
        mm_ev = f32(A3, 24 * 512)
        dma("gpsimd", mm_a, wq_d[0:128, 0:512], sem="dG")
        c_ld = P.cnt("dG")
        prev_ev = {}
        for it in range(24):
            par = it % 2
            reg = ps[:, par * 512:(par + 1) * 512]
            w = [("dG", c_ld)]
            if it >= 2:
                w.append(("eS", prev_ev[it - 2]))

            def mk_g(reg=reg):
                def f(e):
                    last = None
                    for k in range(16):
                        e.ldweights(mm_a[:, 0:128])
                        last = e.matmul(reg, mm_a[:, 0:128], mm_a[:, 0:512],
                                        start=(k == 0), stop=(k == 15),
                                        skip_group_check=True)
                    return last
                return f
            P.step("tensor", mk_g(), waits=w, inc="eT")
            c_g = P.cnt("eT")
            P.step("scalar", lambda e, it=it, reg=reg: e.activation(
                mm_ev[:, it * 512:(it + 1) * 512], reg, AF.Copy),
                waits=[("eT", c_g)], inc="eS")
            prev_ev[it] = P.cnt("eS")
        P.barrier()
        _replay(nc, P)
        return nc

    pa = {"i": 0, "dma": [], "done": []}
    PA_TOTAL = 3 * NHT

    def pa_src(slot):
        if slot < NHT:
            return w1_d[slot * 128:(slot + 1) * 128, :]
        if slot < 2 * NHT:
            t = slot - NHT
            return w3_d[t * 128:(t + 1) * 128, :]
        t = slot - 2 * NHT
        return w2t_d[t * 128:(t + 1) * 128, :]

    def pa_step():
        if STRIP & 16:
            return
        i = pa["i"]
        if i >= PA_TOTAL:
            return
        pa["i"] = i + 1
        buf = wtinA(i % 2)
        w = []
        if i >= 2:
            w = [pa["done"][i - 2]]
        sA = "dA%d" % (i % 2)
        c = dma("sync", buf, pa_src(i), w, sem=sA)
        pa["dma"].append((sA, c))
        cw = [(sA, c), ("dC", c_consts)]
        if i % 2 == 0:
            P.step("scalar", lambda e, b=buf, s=i: e.activation(
                b, b, AF.Abs, accum_out=absacc[:, s:s + 1]), waits=cw, inc="eS")
            pa["done"].append(("eS", P.cnt("eS")))
        else:
            P.step("vector", lambda e, b=buf, s=i: e.tensor_reduce(
                absacc[:, s:s + 1], b, AX.X, ALU.add, apply_absolute_value=True),
                waits=cw, inc="eV")
            pa["done"].append(("eV", P.cnt("eV")))

    c_apply = []
    c_trlast = []
    p1g = []
    for i in range(min(NST, P1LIM)):
        buf = xinb(i % 2)
        w = []
        if i >= 2:
            w.append(("eV", c_apply[i - 2]))
        sX = "dX%d" % (i % 2)
        if PHASES in (127, 128, 129, 130, 131):
            c_xd = 0
        else:
            c_xd = dma("sync", buf, x_d[i * 128:(i + 1) * 128, :], w, sem=sX)
        if PHASES not in (11, 12, 121, 122, 123, 124, 125, 126, 127, 128, 129, 130, 131):
            pa_step()
            pa_step()
        if PHASES in (126, 127, 128, 129, 130, 131):
            if not (STRIP & 4):
                P.step("vector", lambda e, b=buf: e.tensor_copy(statb[:, 0:24], b[:, 0:24]),
                       waits=([(sX, c_xd)] if PHASES == 126 else []) + [("eV", P.cnt("eV"))], inc="eV")
        else:
            ln_moments(buf, 0, f32(A3 + 17920, 24),
                       [(sX, c_xd), ("dC", c_consts), ("eV", P.cnt("eV"))])
        c_ag = P.cnt("eV")
        P.step("scalar", lambda e: e.activation(rstdb[:, 0:1], mvb[:, 1:2], AF.Ln, bias=EPS),
               waits=[("eV", c_ag)])
        P.step("scalar", lambda e: e.activation(rstdb[:, 0:1], rstdb[:, 0:1], AF.Exp, scale=-0.5),
               waits=[("eS", P.cnt("eS"))], inc="eS")
        c_rs = P.cnt("eS")
        xo = xn1tmpb(i % 2)
        if (PHASES in (125, 126, 127, 128, 130, 131)) and not (STRIP & 4):
            aw = [("eV", c_ag)]
            if i >= 2 and len(c_trlast) > i - 2:
                aw.append(("eT", c_trlast[i - 2]))
            P.step("vector", lambda e, b=buf, xo=xo: e.tensor_scalar(
                xo, b, 1.0, None, ALU.mult), waits=aw, inc="eV")
            c_apply.append(P.cnt("eV"))
        elif PHASES == 129 or (STRIP & 4):
            c_apply.append(P.cnt("eV"))
        aw = [("eS", c_rs)]
        if i >= 2 and len(c_trlast) > i - 2:
            aw.append(("eT", c_trlast[i - 2]))
        if PHASES not in (125, 126, 127, 128, 129, 130, 131):
            P.step("vector", lambda e, b=buf, xo=xo: e.tensor_scalar(
                xo, b, mvb[:, 0:1], rstdb[:, 0:1], ALU.subtract, ALU.mult),
                waits=aw, inc="eV")
            c_apply.append(P.cnt("eV"))
        for g in range(0 if PHASES == 11 else 4):
            gid = i * 4 + g
            reg = ps[:, (gid % 2) * 512:(gid % 2) * 512 + 512]
            tw = [("eV", c_apply[i])] if c_apply[i] else []
            if gid == 0:
                tw.append(("dC", c_consts))
            back = 1 if PHASES == 123 else 2
            if gid >= back:
                pv = p1g[gid - back]
                tw.append(pv if isinstance(pv, tuple) else ("eS", pv))
            if PHASES in (128, 129, 130, 131):
                tw = [("dC", c_consts)] if gid == 0 else []

            def mk_tr1(xo=xo, g=g, reg=reg):
                def f(e):
                    last = None
                    for b in range(4):
                        dt = g * 4 + b
                        e.ldweights(xo[:, dt * 128:(dt + 1) * 128])
                        last = e.matmul(reg[:, b * 128:(b + 1) * 128],
                                        xo[:, dt * 128:(dt + 1) * 128], idsb,
                                        start=True, stop=True, skip_group_check=True)
                    return last
                return f
            if PHASES == 121:
                tw = [x for x in tw if x[0] != "eS"]
            P.step("tensor", mk_tr1(), waits=tw, inc="eT")
            c_tr = P.cnt("eT")

            def mk_ev1(g=g, reg=reg, i=i):
                def f(e):
                    last = None
                    for b in range(4):
                        dt = g * 4 + b
                        last = e.activation(
                            xn1t[:, dt * S_TOT + i * 128: dt * S_TOT + (i + 1) * 128],
                            reg[:, b * 128:(b + 1) * 128], AF.Copy)
                    return last
                return f
            P.step("scalar", mk_ev1(), waits=[("eT", c_tr)], inc="eS")
            p1g.append(P.cnt("eS"))
        c_trlast.append(P.cnt("eT"))
        if PHASES not in (11, 12, 121, 122, 123, 124, 125, 126, 127, 128, 129, 130, 131):
            pa_step()
    if not (STRIP & 8):
        P.barrier()
    if PHASES <= 1 or PHASES in (11, 12, 121, 122, 123, 124, 125, 126, 127, 128, 129, 130, 131):
        _replay(nc, P)
        return nc

    q_ev = []
    q_tr = []
    kv_ev = []
    kv_rv = []
    k_tr = []
    p2g = []
    qn = 0
    last_rope = 0

    for h2 in range(1 if STRIP & 32 else 2):
        if h2 == 1:
            wqw = [("eT", P.cnt("eT"))]
            for dt in range(NDT):
                dma("gpsimd", wbig_half[:, dt * 1024:(dt + 1) * 1024],
                    wq_r[:, dt, 1, :], wqw if dt == 0 else (), sem="dG")
            c_wq[1] = P.cnt("dG")
        for i in range(NST):
            own = i >= 4
            if (not own) and h2 == 1:
                continue
            psQ = None
            psKV = None
            mw = [("dG", c_wq[h2])]
            if own:
                par = qn % 2
                psQ = ps[:, par * 1024:(par + 1) * 1024]
                if qn >= 2:
                    for sname, v in (("eV", q_ev[qn - 2][0]), ("eS", q_ev[qn - 2][1])):
                        if v:
                            mw.append((sname, v))
            if h2 == 0:
                kpar = i % 2
                kvbase = 2048 if kpar == 0 else 3584
                psKV = ps[:, kvbase:kvbase + 256]
                if i >= 2 and kv_ev[i - 2]:
                    mw += [("eS", kv_ev[i - 2]), ("eV", kv_rv[i - 2])]

            def mk_mm(i=i, h2=h2, own=own, psQ=psQ, psKV=psKV):
                def f(e):
                    last = None
                    for dt in range(NDT):
                        lhs = xn1t[:, dt * S_TOT + i * 128: dt * S_TOT + (i + 1) * 128]
                        e.ldweights(lhs)
                        if own and (Q2 & 4096):
                            last = e.matmul(psQ[:, 0:512], lhs,
                                            wbig_half[:, dt * 1024:dt * 1024 + 512],
                                            start=(dt == 0), stop=(dt == NDT - 1),
                                            skip_group_check=True)
                            continue
                        if own:
                            for c in range(2):
                                last = e.matmul(psQ[:, c * 512:(c + 1) * 512], lhs,
                                                wbig_half[:, dt * 1024 + c * 512:dt * 1024 + (c + 1) * 512],
                                                start=(dt == 0), stop=(dt == NDT - 1),
                                                skip_group_check=True)
                        if h2 == 0 and not (STRIP & 64):
                            last = e.matmul(psKV[:, 0:128], lhs, wksb[:, dt * HD:(dt + 1) * HD],
                                            start=(dt == 0), stop=(dt == NDT - 1),
                                            skip_group_check=True)
                            last = e.matmul(psKV[:, 128:256], lhs, wvsb[:, dt * HD:(dt + 1) * HD],
                                            start=False, stop=(dt == NDT - 1),
                                            skip_group_check=True)
                    return last
                return f
            P.step("tensor", mk_mm(), waits=mw, inc="eT")
            c_mm = P.cnt("eT")

            if PHASES == 23:
                q_ev.append((0, 0))
                q_tr.append(0)
                if own:
                    qn += 1
                if h2 == 0:
                    kv_ev.append(0)
                    kv_rv.append(0)
                    k_tr.append(0)
                pa_step()
                pa_step()
                continue

            if own and not (Q2 & 1):
                q_ev.append((0, 0))
                if not (Q2 & 2):
                    q_tr.append(0)
                qn += 1
            if h2 == 0 and not (Q2 & 4):
                kv_ev.append(0)
                kv_rv.append(0)
                if not (Q2 & 8):
                    k_tr.append(0)
            if own and (Q2 & 1024):
                def mk_cpv(psQ=psQ):
                    def f(e):
                        e.tensor_copy(qnatb(0)[:, 0:512], psQ[:, 0:512])
                        return e.tensor_copy(qnatb(1)[:, 0:512], psQ[:, 512:1024])
                    return f
                P.step("vector", mk_cpv(), waits=[("eT", c_mm)], inc="eV")
                q_ev.append((P.cnt("eV"), 0))
                q_tr.append(0)
                qn += 1
            if own and (Q2 & 512):
                def mk_cp512(psQ=psQ):
                    def f(e):
                        e.activation(qnatb(0)[:, 0:512], psQ[:, 0:512], AF.Copy)
                        return e.activation(qnatb(1)[:, 0:512], psQ[:, 512:1024], AF.Copy)
                    return f
                P.step("scalar", mk_cp512(), waits=[("eT", c_mm)], inc="eS")
                q_ev.append((0, P.cnt("eS")))
                q_tr.append(0)
                qn += 1
            if own and (Q2 & 256):
                P.step("scalar", lambda e, psQ=psQ: e.activation(
                    qnatb(0), psQ, AF.Copy), waits=[("eT", c_mm)], inc="eS")
                q_ev.append((0, P.cnt("eS")))
                q_tr.append(0)
                qn += 1
            if own and (Q2 & 64):
                def mk_qrest64(qb=qb if False else None, psQ=psQ):
                    def f(e):
                        last = None
                        for hh in range(8):
                            last = e.activation(
                                qnatb(0)[:, hh * 128 + 64:(hh + 1) * 128],
                                psQ[:, hh * 128 + 64:(hh + 1) * 128], AF.Copy)
                        return last
                    return f
                P.step("scalar", mk_qrest64(), waits=[("eT", c_mm)], inc="eS")
                q_ev.append((0, P.cnt("eS")))
                q_tr.append(0)
                qn += 1
            if h2 == 0 and (Q2 & 128):
                kb = knatb(i % 2)
                kx1, kx2 = psKV[:, 0:32], psKV[:, 32:64]
                cosk = rcsb[:, i * 256:i * 256 + 32]
                sink = rssb[:, i * 256:i * 256 + 32]

                def mk_kr128(kb=kb, kx1=kx1, kx2=kx2, cosk=cosk, sink=sink):
                    def f(e):
                        e.tensor_tensor(rtmp[:, 0:32], kx2, sink, ALU.mult)
                        e.tensor_tensor(rtmp[:, 32:64], kx1, cosk, ALU.mult)
                        e.tensor_tensor(rtmp[:, 64:96], kx1, sink, ALU.mult)
                        e.tensor_tensor(rtmp[:, 96:128], kx2, cosk, ALU.mult)
                        e.tensor_tensor(kb[:, 0:32], rtmp[:, 32:64], rtmp[:, 0:32], ALU.subtract)
                        return e.tensor_tensor(kb[:, 32:64], rtmp[:, 64:96], rtmp[:, 96:128], ALU.add)
                    return f
                P.step("vector", mk_kr128(), waits=[("eT", c_mm)], inc="eV")
                kv_ev.append(0)
                kv_rv.append(P.cnt("eV"))
                k_tr.append(0)
            if own and (Q2 & 1):
                qb = qnatb(par)
                cosm = rcsb[:, i * 256:(i + 1) * 256].rearrange("p (h c) -> p h c", c=32)
                sinm = rssb[:, i * 256:(i + 1) * 256].rearrange("p (h c) -> p h c", c=32)
                pq3 = psQ.rearrange("p (h c) -> p h c", c=128)
                x1, x2p = pq3[:, :, 0:32], pq3[:, :, 32:64]
                rt = lambda k: rtmp[:, k * 256:(k + 1) * 256].rearrange("p (h c) -> p h c", c=32)
                rw = [("eT", c_mm), ("eV", last_rope)]
                if qn >= 2:
                    rw.append(("eT", q_tr[qn - 2]))
                if PHASES == 21:
                    P.step("vector", lambda e, qb=qb, psQ=psQ: e.tensor_copy(qb, psQ), waits=rw, inc="eV")
                    c_rope = P.cnt("eV")
                    last_rope = c_rope
                    P.step("scalar", None, inc="eS")
                    c_qrest = P.cnt("eS")
                    q_ev.append((c_rope, c_qrest))
                elif PHASES == 25 or PHASES >= 9 or PHASES in (2, 3, 4, 22):
                    def mk_ropeq_mul(psQ=psQ, i=i):
                        def f(e):
                            cs = rcsb[:, i * 256:(i + 1) * 256]
                            sn = rssb[:, i * 256:(i + 1) * 256]
                            last = None
                            for hh in range(8):
                                x1h = psQ[:, hh * 128:hh * 128 + 32]
                                x2h = psQ[:, hh * 128 + 32:hh * 128 + 64]
                                ch = cs[:, hh * 32:(hh + 1) * 32]
                                sh = sn[:, hh * 32:(hh + 1) * 32]
                                b = hh * 128
                                e.tensor_tensor(rtmp[:, b:b + 32], x2h, sh, ALU.mult)
                                e.tensor_tensor(rtmp[:, b + 32:b + 64], x1h, ch, ALU.mult)
                                e.tensor_tensor(rtmp[:, b + 64:b + 96], x1h, sh, ALU.mult)
                                last = e.tensor_tensor(rtmp[:, b + 96:b + 128], x2h, ch, ALU.mult)
                            return last
                        return f

                    def mk_ropeq_comb(qb=qb):
                        def f(e):
                            last = None
                            for hh in range(8):
                                b = hh * 128
                                e.tensor_tensor(qb[:, b:b + 32],
                                                rtmp[:, b + 32:b + 64], rtmp[:, b:b + 32], ALU.subtract)
                                last = e.tensor_tensor(qb[:, b + 32:b + 64],
                                                       rtmp[:, b + 64:b + 96], rtmp[:, b + 96:b + 128], ALU.add)
                            return last
                        return f
                    if not (Q2 & 16384):
                        P.step("vector", mk_ropeq_mul(), waits=rw, inc="eV")
                        P.step("vector", mk_ropeq_comb(), waits=[("eV", P.cnt("eV"))], inc="eV")
                    c_rope = P.cnt("eV")
                    last_rope = c_rope
                else:
                    P.step("vector", lambda e, a=x2p, b=sinm, o=rt(0): e.tensor_tensor(o, a, b, ALU.mult), waits=rw)
                    P.step("vector", lambda e, a=x1, b=cosm, o=rt(1): e.tensor_tensor(o, a, b, ALU.mult))
                    P.step("vector", lambda e, a=x1, b=sinm, o=rt(2): e.tensor_tensor(o, a, b, ALU.mult))
                    P.step("vector", lambda e, a=x2p, b=cosm, o=rt(3): e.tensor_tensor(o, a, b, ALU.mult), inc="eV")
                    c_rt = P.cnt("eV")
                    qb3 = qb.rearrange("p (h c) -> p h c", c=128)
                    P.step("vector", lambda e, o=qb3[:, :, 0:32], a=rt(1), b=rt(0): e.tensor_tensor(
                        o, a, b, ALU.subtract), waits=[("eV", c_rt)])
                    P.step("vector", lambda e, o=qb3[:, :, 32:64], a=rt(2), b=rt(3): e.tensor_tensor(
                        o, a, b, ALU.add), inc="eV")
                if PHASES != 21:
                    c_rope = P.cnt("eV")
                    last_rope = c_rope

                    def mk_qrest(qb=qb, psQ=psQ):
                        def f(e):
                            last = None
                            for hh in range(8):
                                last = e.activation(qb[:, hh * 128 + 64:(hh + 1) * 128],
                                                    psQ[:, hh * 128 + 64:(hh + 1) * 128],
                                                    AF.Copy)
                            return last
                        return f
                    if not (Q2 & 8192):
                        qw = [("eT", c_mm), ("eV", c_rope)]
                        P.step("scalar", mk_qrest(), waits=qw, inc="eS")
                    c_qrest = P.cnt("eS")
                    q_ev.append((c_rope, c_qrest))
                for g in range(2 if (Q2 & 2) else 0):
                    gid = len(p2g)
                    reg = ps[:, 2560 + (gid % 2) * 512:2560 + (gid % 2) * 512 + 512]
                    tw = [("eV", c_rope), ("eS", c_qrest)]
                    if gid >= 2:
                        tw.append(("eS", p2g[gid - 2]))

                    def mk_tr2(qb=qb, g=g, reg=reg):
                        def f(e):
                            last = None
                            for b in range(4):
                                hh = g * 4 + b
                                e.ldweights(qb[:, hh * 128:(hh + 1) * 128])
                                last = e.matmul(reg[:, b * 128:(b + 1) * 128],
                                                qb[:, hh * 128:(hh + 1) * 128], idsb,
                                                start=True, stop=True, skip_group_check=True)
                            return last
                        return f
                    P.step("tensor", mk_tr2(), waits=tw, inc="eT")
                    c_t2 = P.cnt("eT")

                    def mk_ev2(g=g, reg=reg, i=i, h2=h2):
                        def f(e):
                            last = None
                            for b in range(4):
                                h = h2 * 8 + g * 4 + b
                                last = e.activation(
                                    qt_all[:, h * S_OWN + (i - 4) * 128: h * S_OWN + (i - 3) * 128],
                                    reg[:, b * 128:(b + 1) * 128], AF.Copy)
                            return last
                        return f
                    P.step("scalar", mk_ev2(), waits=[("eT", c_t2)], inc="eS")
                    p2g.append(P.cnt("eS"))
                if not (Q2 & 2):
                    pass
                q_tr.append(P.cnt("eT"))
                qn += 1

            if h2 == 0 and (Q2 & 4):
                kb = knatb(i % 2)
                cosk = rcsb[:, i * 256:i * 256 + 32]
                sink = rssb[:, i * 256:i * 256 + 32]
                kx1, kx2 = psKV[:, 0:32], psKV[:, 32:64]
                krw = [("eT", c_mm), ("eV", last_rope)]
                if i >= 2:
                    krw.append(("eT", k_tr[i - 2]))
                    if DEBUG >= 3:
                        krw.append(("dD", P.cnt("dD")))
                P.step("vector", lambda e, a=kx2, b=sink: e.tensor_tensor(rtmp[:, 0:32], a, b, ALU.mult), waits=krw)
                P.step("vector", lambda e, a=kx1, b=cosk: e.tensor_tensor(rtmp[:, 32:64], a, b, ALU.mult))
                P.step("vector", lambda e, a=kx1, b=sink: e.tensor_tensor(rtmp[:, 64:96], a, b, ALU.mult))
                P.step("vector", lambda e, a=kx2, b=cosk: e.tensor_tensor(rtmp[:, 96:128], a, b, ALU.mult), inc="eV")
                c_kt4 = P.cnt("eV")
                P.step("vector", lambda e, kb=kb: e.tensor_tensor(kb[:, 0:32], rtmp[:, 32:64],
                                                                  rtmp[:, 0:32], ALU.subtract),
                       waits=[("eV", c_kt4)])
                P.step("vector", lambda e, kb=kb: e.tensor_tensor(kb[:, 32:64], rtmp[:, 64:96],
                                                                  rtmp[:, 96:128], ALU.add), inc="eV")
                kv_rv.append(P.cnt("eV"))
                last_rope = P.cnt("eV")
                if DEBUG >= 3 and i == 5:
                    P.step("scalar", lambda e, psKV=psKV: e.activation(
                        silu_t[:, 0:256], psKV[:, 0:256], AF.Copy), waits=[("eT", c_mm)], inc="eS")
                    dma("sync", dbg_kps, silu_t[:, 0:256], [("eS", P.cnt("eS"))], sem="dD")
                P.step("scalar", lambda e, kb=kb, psKV=psKV: e.activation(
                    kb[:, 64:128], psKV[:, 64:128], AF.Copy),
                    waits=[("eT", c_mm), ("eV", kv_rv[i])])
                P.step("scalar", lambda e, i=i, psKV=psKV: e.activation(
                    vsb[:, i * 128:(i + 1) * 128], psKV[:, 128:256], AF.Copy), inc="eS")
                kv_ev.append(P.cnt("eS"))
                if DEBUG >= 3 and i == 5:
                    dma("sync", dbg_kb, kb, [("eV", kv_rv[i]), ("eS", kv_ev[i])], sem="dD")
                if not (Q2 & 8):
                    k_tr.append(0)
                    p2g.append(p2g[-1] if p2g else 0)
                    pa_step()
                    pa_step()
                    continue
                gid = len(p2g)
                reg = ps[:, 2560 + (gid % 2) * 512:2560 + (gid % 2) * 512 + 128]
                tw = [("eV", kv_rv[i]), ("eS", kv_ev[i])]
                if gid >= 2:
                    tw.append(("eS", p2g[gid - 2]))
                def mk_trk(kb=kb, reg=reg):
                    def f(e):
                        e.ldweights(kb)
                        return e.matmul(reg, kb, idsb, start=True, stop=True,
                                        skip_group_check=True)
                    return f
                P.step("tensor", mk_trk(), waits=tw, inc="eT")
                k_tr.append(P.cnt("eT"))
                P.step("scalar", lambda e, reg=reg, i=i: e.activation(
                    ktb[:, i * 128:(i + 1) * 128], reg, AF.Copy),
                    waits=[("eT", k_tr[i])], inc="eS")
                p2g.append(P.cnt("eS"))
            pa_step()
            pa_step()

    while pa["i"] < PA_TOTAL and not (STRIP & 16):
        pa_step()
    P.barrier()

    if PHASES in (22, 23) or not (Q2 & 16):
        _replay(nc, P)
        return nc
    for wi in range(3):
        P.step("vector", lambda e, wi=wi: e.tensor_reduce(
            thparts[:, wi:wi + 1], absacc[:, wi * NHT:(wi + 1) * NHT], AX.X, ALU.add),
            inc="eV" if wi == 2 else None)
    c_thp = P.cnt("eV")
    P.step("tensor", lambda e: e.matmul(ps[0:1, 3072:3075], onescolb, thparts,
                                        start=True, stop=True, skip_group_check=True),
           waits=[("eV", c_thp)], inc="eT")
    P.step("vector", lambda e: e.tensor_copy(thsum, ps[0:1, 3072:3075]),
           waits=[("eT", P.cnt("eT"))], inc="eV")
    P.step("tensor", lambda e: e.matmul(ps[:, 3072:3075], onesk1fb[0:1, 0:128], thsum,
                                        start=True, stop=True, skip_group_check=True),
           waits=[("eV", P.cnt("eV"))], inc="eT")
    c_thb = P.cnt("eT")
    for col, sc_ in ((0, 3072), (2, 3073), (4, 3074)):
        P.step("vector", lambda e, col=col, sc_=sc_: e.tensor_scalar(
            thsb[:, col:col + 1], ps[:, sc_:sc_ + 1], 1.0 / (2 * NW_ELEM), None, ALU.mult),
            waits=[("eT", c_thb)] if col == 0 else ())
        P.step("vector", lambda e, col=col, sc_=sc_: e.tensor_scalar(
            thsb[:, col + 1:col + 2], ps[:, sc_:sc_ + 1], -1.0 / (2 * NW_ELEM), None, ALU.mult))
    for wi, sc_ in enumerate((3072, 3073, 3074)):
        P.step("vector", lambda e, wi=wi, sc_=sc_: e.tensor_scalar(
            ngs[:, wi:wi + 1], ps[:, sc_:sc_ + 1], -1.0 / NW_ELEM, None, ALU.mult),
            inc="eV" if wi == 2 else None)
    c_th_ready = P.cnt("eV")
    P.step("vector", None, waits=[("eV", c_th_ready)])

    if DEBUG:
        dma("sync", dbg_qt, qt_all, sem="dD")
        dma("sync", dbg_kt, ktb, sem="dD")
        dma("sync", dbg_v, vsb, sem="dD")

    wow = [("eT", P.cnt("eT"))]
    for dt in range(NDT):
        dma("gpsimd", wbig[:, dt * D:(dt + 1) * D],
            wo_d[dt * 128:(dt + 1) * 128, :], wow if dt == 0 else (), sem="dG")
    c_wo = P.cnt("dG")
    c_bo = dma("sync", bosb, bo_d, sem="dB")
    if PHASES <= 2 or PHASES in (21, 23, 25):
        _replay(nc, P)
        return nc

    NPAIR = 8
    c_exp = {}
    c_mask = {}
    c_rec = {}
    c_avm = {}
    c_avd = {}
    c_x2p = []
    c_x2sp = []
    c_xre = []
    c_ln2ap = []
    p45g = []
    c_ylast = 0
    c_wo_eT_prev = 0
    c_stageev_prev = 0

    for qt in range(NSTO):
        xw = []
        if qt >= 2:
            xw.append(("eP", c_x2p[qt - 2]))
        c_xre.append(dma("sync", xreb(qt % 2), x_d[(qt + 4) * 128:(qt + 5) * 128, :], xw,
                         sem="dR%d" % (qt % 2)))
        for pr in range(NPAIR):
            gp = qt * NPAIR + pr
            par = gp % 2
            psSC = ps[:, par * 1280:(par + 1) * 1280]
            exb = exptb(par)
            sw = [("eV", c_th_ready)]
            if gp >= 2:
                sw.append(("eS", c_exp[gp - 2]))
            if gp >= 1:
                sw.append(("eS", c_exp[gp - 1]))

            def mk_sc(qt=qt, pr=pr, psSC=psSC):
                def f(e):
                    last = None
                    for j in range(5):
                        kslc = ktb[:, (qt + j) * 128:(qt + j + 1) * 128]
                        e.ldweights(kslc)
                        for m in range(2):
                            h = pr * 2 + m
                            last = e.matmul(
                                psSC[:, m * 640 + j * 128:m * 640 + (j + 1) * 128], kslc,
                                qt_all[:, h * S_OWN + qt * 128:h * S_OWN + (qt + 1) * 128],
                                start=True, stop=True, skip_group_check=True)
                    return last
                return f
            P.step("tensor", mk_sc(), waits=sw, inc="eT")
            c_scm = P.cnt("eT")
            ew = [("eT", c_scm)]
            if gp >= 2:
                ew.append(("eT", c_avm[gp - 2]))
            P.step("scalar", lambda e, psSC=psSC, exb=exb: e.activation(
                exb, psSC, AF.Exp, scale=SM), waits=ew, inc="eS")
            c_exp[gp] = P.cnt("eS")
            mwk = [("eS", c_exp[gp])]
            if gp == 0:
                mwk.append(("dC", c_consts))
            for m in range(2):
                P.step("gpsimd", lambda e, exb=exb, m=m: e.tensor_tensor(
                    exb[:, m * 640:m * 640 + 128], exb[:, m * 640:m * 640 + 128],
                    mfarb, ALU.mult), waits=mwk if m == 0 else ())
                P.step("gpsimd", lambda e, exb=exb, m=m: e.tensor_tensor(
                    exb[:, m * 640 + 512:m * 640 + 640],
                    exb[:, m * 640 + 512:m * 640 + 640], mdiab, ALU.mult),
                    inc="eP" if m == 1 else None)
            c_mask[gp] = P.cnt("eP")
            dnw = [("eP", c_mask[gp])]
            if gp >= 1:
                dnw.append(("eV", c_avd[gp - 1]))

            def mk_dn(qt=qt, exb=exb):
                def f(e):
                    dn = ps[:, 2560:2816]
                    e.ldweights(onesk1b[0:1, :])
                    for j2, nb in enumerate((ncrb0, ncrb1)):
                        e.matmul(dn, onesk1b[0:1, :], nb[0:1, qt * 256:(qt + 1) * 256],
                                 start=(j2 == 0), stop=False, skip_group_check=True)
                    e.ldweights(onesb)
                    last = None
                    for m in range(2):
                        for j in range(5):
                            last = e.matmul(dn[:, m * 128:(m + 1) * 128], onesb,
                                            exb[:, m * 640 + j * 128: m * 640 + (j + 1) * 128],
                                            start=False, stop=(m == 1 and j == 4),
                                            skip_group_check=True)
                    return last
                return f
            P.step("tensor", mk_dn(), waits=dnw, inc="eT")
            c_dnm = P.cnt("eT")
            if DEBUG >= 3 and qt == 4 and pr == 0:
                spare = f32(2720, 256)
                P.step("scalar", lambda e, spare=spare: e.activation(spare, ps[:, 2560:2816], AF.Copy),
                       waits=[("eT", c_dnm)], inc="eS")
                dma("sync", dbg_dn, spare, [("eS", P.cnt("eS"))], sem="dD")
                dma("sync", dbg_ex, exb, [("eP", c_mask[gp])], sem="dD")
            rb = recb(par)
            rcw = [("eT", c_dnm)]
            if gp >= 2:
                rcw.append(("eV", c_avd[gp - 2]))
            P.step("vector", lambda e, rb=rb: e.reciprocal(rb, ps[:, 2560:2816]),
                   waits=rcw, inc="eV")
            c_rec[gp] = P.cnt("eV")
            avw = [("eP", c_mask[gp]), ("eV", c_rec[gp])]
            if gp >= 1:
                avw.append(("eV", c_avd[gp - 1]))

            def mk_av(qt=qt, exb=exb):
                def f(e):
                    av = ps[:, 2816:3072]
                    last = None
                    for j in range(5):
                        vslc = vsb[:, (qt + j) * 128:(qt + j + 1) * 128]
                        e.ldweights(vslc)
                        for m in range(2):
                            last = e.matmul(av[:, m * 128:(m + 1) * 128], vslc,
                                            exb[:, m * 640 + j * 128:m * 640 + (j + 1) * 128],
                                            start=(j == 0 and m == 0), stop=(j == 4),
                                            skip_group_check=True)
                    return last
                return f
            P.step("tensor", mk_av(), waits=avw, inc="eT")
            c_avm[gp] = P.cnt("eT")
            if DEBUG >= 3 and qt == 4 and pr == 0:
                spare2 = f32(2976, 256)
                P.step("scalar", lambda e, spare2=spare2: e.activation(spare2, ps[:, 2816:3072], AF.Copy),
                       waits=[("eT", c_avm[gp])], inc="eS")
                dma("sync", dbg_av, spare2, [("eS", P.cnt("eS"))], sem="dD")
            dvw = [("eT", c_avm[gp]), ("eV", c_rec[gp])]
            if qt > 0 and pr < 2:
                dvw.append(("eT", c_wo_eT_prev))
                if DEBUG >= 3:
                    dvw.append(("dD", P.cnt("dD")))
            P.step("vector", lambda e, pr=pr, rb=rb: e.tensor_tensor(
                avtb[:, pr * 256:(pr + 1) * 256], ps[:, 2816:3072], rb, ALU.mult), waits=dvw, inc="eV")
            c_avd[gp] = P.cnt("eV")
        c_avd_all = c_avd[qt * NPAIR + NPAIR - 1]
        if DEBUG:
            dma("sync", dbg_avt[qt:qt + 1, :].rearrange("r (p f) -> (r p) f", p=128),
                avtb, [("eV", c_avd_all)], sem="dD")

        for dfp in range(2):
            ww = [("eV", c_avd_all), ("dG", c_wo)]
            if qt > 0 and dfp == 0:
                ww.append(("eS", c_stageev_prev))
            if c_ylast:
                ww.append(("eV", c_ylast))

            def mk_wo(dfp=dfp):
                def f(e):
                    last = None
                    for hh in range(H):
                        aslc = avtb[:, hh * 128:(hh + 1) * 128]
                        e.ldweights(aslc)
                        for c in range(2):
                            dfc = dfp * 2 + c
                            last = e.matmul(ps[:, 3072 + c * 512:3072 + (c + 1) * 512], aslc,
                                            wbig[:, hh * 2048 + dfc * 512: hh * 2048 + (dfc + 1) * 512],
                                            start=(hh == 0), stop=(hh == H - 1),
                                            skip_group_check=True)
                    return last
                return f
            P.step("tensor", mk_wo(), waits=ww, inc="eT")
            c_wom = P.cnt("eT")
            P.step("vector", lambda e, dfp=dfp: e.tensor_tensor(
                ytile[:, dfp * 1024:(dfp + 1) * 1024], ps[:, 3072:4096],
                bosb[:, dfp * 1024:(dfp + 1) * 1024], ALU.add),
                waits=[("eT", c_wom), ("dB", c_bo)], inc="eV")
            c_ylast = P.cnt("eV")
        c_wo_eT_prev = P.cnt("eT")

        if DEBUG:
            dma("sync", dbg_y[qt:qt + 1, :].rearrange("r (p f) -> (r p) f", p=128),
                ytile, [("eV", c_ylast)], sem="dD")
        ln_moments(ytile, 2, f32(2720, 24), [("eV", P.cnt("eV"))])
        c_ag = P.cnt("eV")
        P.step("scalar", lambda e: e.activation(rstdb[:, 1:2], mvb[:, 3:4], AF.Ln, bias=EPS),
               waits=[("eV", c_ag)])
        P.step("scalar", lambda e: e.activation(rstdb[:, 1:2], rstdb[:, 1:2], AF.Exp, scale=-0.5),
               waits=[("eS", P.cnt("eS"))], inc="eS")
        c_rs = P.cnt("eS")
        ynw = [("eS", c_rs)]
        if DEBUG >= 3:
            ynw.append(("dD", P.cnt("dD")))
        P.step("vector", lambda e: e.tensor_scalar(
            ytile, ytile, mvb[:, 2:3], rstdb[:, 1:2], ALU.subtract, ALU.mult),
            waits=ynw, inc="eV")
        c_yn = P.cnt("eV")
        spw = [("eV", c_yn), ("dR%d" % (qt % 2), c_xre[qt])]
        if qt >= 1:
            spw.append(("dP", c_x2sp[qt - 1]))
            spw.append(("eV", c_ln2ap[qt - 1]))
        P.step("gpsimd", lambda e, qt=qt: e.tensor_tensor(
            x2tb, ytile, xreb(qt % 2), ALU.add), waits=spw, inc="eP")
        c_x2p.append(P.cnt("eP"))
        c_x2sp.append(dma("sync", x2_d[qt * 128:(qt + 1) * 128, :], x2tb,
                          [("eP", c_x2p[qt])], sem="dP"))
        ln_moments(x2tb, 4, f32(2720, 24),
                   [("eP", c_x2p[qt]), ("eV", P.cnt("eV"))])
        c_ag2 = P.cnt("eV")
        P.step("scalar", lambda e: e.activation(rstdb[:, 2:3], mvb[:, 5:6], AF.Ln, bias=EPS),
               waits=[("eV", c_ag2)])
        P.step("scalar", lambda e: e.activation(rstdb[:, 2:3], rstdb[:, 2:3], AF.Exp, scale=-0.5),
               waits=[("eS", P.cnt("eS"))], inc="eS")
        c_rs2 = P.cnt("eS")
        aw2 = [("eS", c_rs2)]
        if qt >= 1:
            aw2.append(("eT", c_xn2tr_prev))
        P.step("vector", lambda e: e.tensor_scalar(
            xn2tmp, x2tb, mvb[:, 4:5], rstdb[:, 2:3], ALU.subtract, ALU.mult),
            waits=aw2, inc="eV")
        c_ln2ap.append(P.cnt("eV"))
        c_ap2 = P.cnt("eV")
        for g in range(4):
            gid = qt * 4 + g
            reg = ps[:, 3072 + (gid % 2) * 512:3072 + (gid % 2) * 512 + 512]
            tw = [("eV", c_ap2), ("eV", c_ylast)]
            if gid >= 2:
                tw.append(("eS", p45g[gid - 2]))

            def mk_tr3(g=g, reg=reg):
                def f(e):
                    last = None
                    for b in range(4):
                        dt = g * 4 + b
                        e.ldweights(xn2tmp[:, dt * 128:(dt + 1) * 128])
                        last = e.matmul(reg[:, b * 128:(b + 1) * 128],
                                        xn2tmp[:, dt * 128:(dt + 1) * 128], idsb,
                                        start=True, stop=True, skip_group_check=True)
                    return last
                return f
            P.step("tensor", mk_tr3(), waits=tw, inc="eT")
            c_t3 = P.cnt("eT")

            def mk_ev3(g=g, reg=reg, qt=qt):
                def f(e):
                    last = None
                    for b in range(4):
                        dt = g * 4 + b
                        last = e.activation(
                            xn2t[:, dt * S_OWN + qt * 128: dt * S_OWN + (qt + 1) * 128],
                            reg[:, b * 128:(b + 1) * 128], AF.Copy)
                    return last
                return f
            P.step("scalar", mk_ev3(), waits=[("eT", c_t3)], inc="eS")
            p45g.append(P.cnt("eS"))
        c_xn2tr_prev = P.cnt("eT")
        c_stageev_prev = P.cnt("eS")
    fw6 = [("dP", NSTO * 16)]
    if DEBUG:
        fw6.append(("dD", P.cnt("dD")))
    P.step("scalar", None, waits=fw6)
    P.barrier()
    if PHASES <= 3:
        _replay(nc, P)
        return nc

    c_hmm = {}
    c_silu = {}
    c_gev = {}
    c_st1ev = {}
    c_st3ev = {}
    c_tr1 = {}
    c_tr3 = {}
    c_tn = {}
    c_w2sp = {}
    c_bq = {}
    slot = {"i": 0}

    def quant(ht, wsel, dst_tn, extra_b_waits=(), extra_p_waits=()):
        i = slot["i"]
        slot["i"] += 1
        buf = q_wt(i % 2)
        src = {0: w1_d, 1: w3_d, 2: w2t_d}[wsel][ht * 128:(ht + 1) * 128, :]
        lw = [("eV", c_th_ready)]
        if i >= 2:
            lw.append(("eV", c_bq[i - 2]))
        sW = "dW%d" % (i % 2)
        c_d = dma("sync", buf, src, lw, sem=sW)
        thc = {0: 0, 1: 2, 2: 4}[wsel]
        bw = [(sW, c_d)] + list(extra_b_waits)
        if i >= 1:
            bw.append(("eP", P.cnt("eP")))
        P.step("vector", lambda e, buf=buf: e.tensor_scalar(
            q_b1, buf, thsb[:, thc:thc + 1], None, ALU.is_gt), waits=bw)
        P.step("vector", lambda e, buf=buf: e.tensor_scalar(
            q_b2, buf, thsb[:, thc + 1:thc + 2], None, ALU.is_lt), inc="eV")
        c_bq[i] = P.cnt("eV")
        pw = [("eV", c_bq[i])] + list(extra_p_waits)
        P.step("gpsimd", lambda e, dst_tn=dst_tn: e.tensor_tensor(
            dst_tn, q_b2, q_b1, ALU.subtract), waits=pw, inc="eP")
        return P.cnt("eP")

    for ht in range(NHT):
        par = ht % 2
        tn1, tn3, tn2 = q_tn1(par), q_tn3(par), q_tn2(par)
        ep1 = [] if ht < 2 else [("eT", c_tr1[ht - 2])]
        c_tn[(ht, 0)] = quant(ht, 0, tn1, extra_p_waits=ep1)
        ep3 = [] if ht < 2 else [("eT", c_tr3[ht - 2])]
        c_tn[(ht, 1)] = quant(ht, 1, tn3, extra_p_waits=ep3)
        ep2 = [] if ht < 2 else [("dQ", P.cnt("dQ"))]
        c_tn[(ht, 2)] = quant(ht, 2, tn2, extra_p_waits=ep2)
        c_w2sp[ht] = dma("sync", w2q_d[ht:ht + 1, :].rearrange("r (p f) -> (r p) f", p=128),
                         tn2, [("eP", c_tn[(ht, 2)])], sem="dQ")

        if ht == 0:
            P._c_twave = {0: 0, 1: 0}
        ev_cnt = {}
        for wsel, tn, st in ((0, tn1, stage1(par)), (1, tn3, stage3(par))):
            for wv in range(2):
                treg = ps[:, 2048 + wv * 1024:2048 + (wv + 1) * 1024]
                tw = [("eP", c_tn[(ht, wsel)]), ("eS", P._c_twave[wv])]

                def mk_tr(tn=tn, wv=wv, treg=treg):
                    def f(e):
                        last = None
                        for b in range(8):
                            dt = wv * 8 + b
                            e.ldweights(tn[:, dt * 128:(dt + 1) * 128])
                            last = e.matmul(treg[:, b * 128:(b + 1) * 128],
                                            tn[:, dt * 128:(dt + 1) * 128], idsb,
                                            start=True, stop=True, skip_group_check=True)
                        return last
                    return f
                P.step("tensor", mk_tr(), waits=tw, inc="eT")
                c_trw = P.cnt("eT")
                evw = [("eT", c_trw)]
                if ht >= 2:
                    evw.append(("eT", c_hmm[ht - 2]))
                P.step("scalar", lambda e, st=st, wv=wv, treg=treg: e.activation(
                    st[:, wv * 1024:(wv + 1) * 1024], treg, AF.Copy), waits=evw, inc="eS")
                P._c_twave[wv] = P.cnt("eS")
                ev_cnt[(wsel, wv)] = P.cnt("eS")
            if wsel == 0:
                c_tr1[ht] = P.cnt("eT")
                c_st1ev[ht] = P.cnt("eS")
            else:
                c_tr3[ht] = P.cnt("eT")
                c_st3ev[ht] = P.cnt("eS")

        def mk_hpart(st, base, d0, d1):
            def f(e):
                last = None
                for dt in range(d0, d1):
                    lhs = st[:, dt * 128:(dt + 1) * 128]
                    e.ldweights(lhs)
                    for sc in range(2):
                        last = e.matmul(ps[:, base + sc * 512:base + (sc + 1) * 512], lhs,
                                        xn2t[:, dt * S_OWN + sc * 512:dt * S_OWN + (sc + 1) * 512],
                                        start=(dt == 0), stop=(dt == NDT - 1),
                                        skip_group_check=True)
                return last
            return f
        hw1 = [("eS", ev_cnt[(0, 0)])]
        hw3 = [("eS", ev_cnt[(1, 0)])]
        if ht >= 1:
            hw1.append(("eS", c_silu[ht - 1]))
            hw3.append(("eV", c_gev[ht - 1]))
        P.step("tensor", mk_hpart(stage1(par), 0, 0, 8), waits=hw1, inc="eT")
        P.step("tensor", mk_hpart(stage1(par), 0, 8, NDT),
               waits=[("eS", ev_cnt[(0, 1)])], inc="eT")
        c_h1 = P.cnt("eT")
        P.step("tensor", mk_hpart(stage3(par), 1024, 0, 8), waits=hw3, inc="eT")
        P.step("tensor", mk_hpart(stage3(par), 1024, 8, NDT),
               waits=[("eS", ev_cnt[(1, 1)])], inc="eT")
        c_hmm[ht] = P.cnt("eT")
        P.step("scalar", lambda e: e.activation(
            silu_t, ps[:, 0:1024], AF.Silu, scale=ngs[:, 0:1]),
            waits=[("eT", c_h1)], inc="eS")
        c_silu[ht] = P.cnt("eS")
        P.step("vector", lambda e, ht=ht: e.scalar_tensor_tensor(
            gt[:, ht * S_OWN:(ht + 1) * S_OWN], silu_t, ngs[:, 1:2],
            ps[:, 1024:2048], ALU.mult, ALU.mult),
            waits=[("eS", c_silu[ht]), ("eT", c_hmm[ht])], inc="eV")
        c_gev[ht] = P.cnt("eV")
    P.barrier()
    if PHASES <= 4:
        _replay(nc, P)
        return nc

    c_w2mm = {}
    c_oev = 0
    for dfc in range(4):
        for ht in range(NHT):
            par = ht % 2
            sl = w2slb(par)
            lw = [("dQ", NHT * 16)]
            if ht >= 2:
                lw.append(("eT", c_w2mm[(dfc, ht - 2)]))
            elif dfc > 0:
                lw.append(("eT", c_w2mm[(dfc - 1, NHT - 1)]))
            src = w2q_d[ht:ht + 1, :].rearrange("r (p f) -> (r p) f", p=128)[
                :, dfc * 512:(dfc + 1) * 512]
            sL = "dL%d" % par
            c_sl = dma("sync", sl, src, lw, sem=sL)
            mw = [(sL, c_sl)]
            if dfc > 0 and ht == 0:
                mw.append(("eV", c_oev))

            def mk_w2(ht=ht, sl=sl):
                def f(e):
                    last = None
                    for st in range(NSTO):
                        lhs = gt[:, ht * S_OWN + st * 128:ht * S_OWN + (st + 1) * 128]
                        e.ldweights(lhs)
                        last = e.matmul(ps[:, st * 512:(st + 1) * 512], lhs, sl,
                                        start=(ht == 0), stop=(ht == NHT - 1),
                                        skip_group_check=True)
                    return last
                return f
            P.step("tensor", mk_w2(), waits=mw, inc="eT")
            c_w2mm[(dfc, ht)] = P.cnt("eT")
        c_last = P.cnt("eT")
        for st in range(NSTO):
            P.step("vector", lambda e, st=st, dfc=dfc: e.tensor_scalar(
                osb[:, st * 2048 + dfc * 512:st * 2048 + (dfc + 1) * 512],
                ps[:, st * 512:(st + 1) * 512], ngs[:, 2:3], None, ALU.mult),
                waits=[("eT", c_last)] if st == 0 else (),
                inc="eV" if st == NSTO - 1 else None)
        c_oev = P.cnt("eV")
    c_w2ev = c_oev

    c_x3 = []
    for st in range(NSTO):
        rw = [("dP", NSTO * 16)]
        if st >= 2:
            rw.append(("eP", c_x3[st - 2]))
        sE = "dE%d" % (st % 2)
        c_r = dma("sync", x2re(st % 2), x2_d[st * 128:(st + 1) * 128, :], rw, sem=sE)
        ob = osb[:, st * 2048:(st + 1) * 2048]
        ln_moments(ob, 6, f32(3232, 24), [("eV", P.cnt("eV"))])
        c_ag3 = P.cnt("eV")
        P.step("scalar", lambda e: e.activation(rstdb[:, 3:4], mvb[:, 7:8], AF.Ln, bias=EPS),
               waits=[("eV", c_ag3)])
        P.step("scalar", lambda e: e.activation(rstdb[:, 3:4], rstdb[:, 3:4], AF.Exp, scale=-0.5),
               waits=[("eS", P.cnt("eS"))], inc="eS")
        c_rs3 = P.cnt("eS")
        P.step("vector", lambda e, ob=ob: e.tensor_scalar(
            ob, ob, mvb[:, 6:7], rstdb[:, 3:4], ALU.subtract, ALU.mult),
            waits=[("eS", c_rs3)], inc="eV")
        c_yn2 = P.cnt("eV")
        xb = x3tb(st % 2)
        fw = [("eV", c_yn2), (sE, c_r)]
        if st >= 2:
            fw.append(("dO", st * 16))
        P.step("gpsimd", lambda e, xb=xb, ob=ob, st=st: e.tensor_tensor(
            xb, ob, x2re(st % 2), ALU.add), waits=fw, inc="eP")
        c_x3.append(P.cnt("eP"))
        dma("sync", out_d[st * 128:(st + 1) * 128, :], xb, [("eP", c_x3[st])], sem="dO")
    P.step("sync", None, waits=[("dO", NSTO * 16)])
    _replay(nc, P)
    return nc


def _replay(nc, P):
    with nc.Block() as block:
        @block.sync
        def _(e):
            for s in P.steps["sync"]:
                s(e)

        @block.tensor
        def _(e):
            for s in P.steps["tensor"]:
                s(e)

        @block.vector
        def _(e):
            for s in P.steps["vector"]:
                s(e)

        @block.scalar
        def _(e):
            for s in P.steps["scalar"]:
                s(e)

        @block.gpsimd
        def _(e):
            for s in P.steps["gpsimd"]:
                s(e)


def _host_prep(L0_attn_x, L0_attn_Wq, L0_attn_Wk, L0_attn_Wv, L0_attn_Wo,
               L0_attn_bo, L0_ffn_w1, L0_ffn_w3, L0_ffn_w2):
    x = np.ascontiguousarray(np.asarray(L0_attn_x, np.float32).reshape(SEQ, D))
    wq = np.ascontiguousarray(np.asarray(L0_attn_Wq, np.float32))
    wk = np.ascontiguousarray(np.asarray(L0_attn_Wk, np.float32))
    wv = np.ascontiguousarray(np.asarray(L0_attn_Wv, np.float32))
    wo = np.ascontiguousarray(np.asarray(L0_attn_Wo, np.float32))
    bo = np.ascontiguousarray(np.broadcast_to(
        np.asarray(L0_attn_bo, np.float32)[None, :], (128, D)))
    w1 = np.zeros((HIDP, D), np.float32)
    w1[:HID] = np.asarray(L0_ffn_w1, np.float32)
    w3 = np.zeros((HIDP, D), np.float32)
    w3[:HID] = np.asarray(L0_ffn_w3, np.float32)
    w2t = np.zeros((HIDP, D), np.float32)
    w2t[:HID] = np.ascontiguousarray(np.asarray(L0_ffn_w2, np.float32).T)

    inv = np.exp(-np.log(10000.0) * (np.arange(0, ROPE_D, 2, dtype=np.float64) / ROPE_D))
    in_maps = []
    for c in range(NCORES):
        lo = c * S_OWN - WIN
        if c == 0:
            xc = np.concatenate([np.zeros((WIN, D), np.float32), x[0:S_OWN]], 0)
        else:
            xc = np.ascontiguousarray(x[lo:lo + S_TOT])
        pos = np.maximum(np.arange(lo, lo + S_TOT, dtype=np.float64), 0.0)
        th = pos[:, None] * inv[None, :]
        cosT = np.ascontiguousarray(np.tile(np.cos(th).astype(np.float32), (1, 8)))
        sinT = np.ascontiguousarray(np.tile(np.sin(th).astype(np.float32), (1, 8)))
        qabs = np.arange(c * S_OWN, (c + 1) * S_OWN)
        corr = np.maximum(0, WIN - 1 - qabs).astype(np.float64)
        hi = np.minimum(corr, 256.0)
        lo2 = corr - hi
        ncr = np.zeros((2, 2048), np.float32)
        for qt in range(NSTO):
            for m in range(2):
                ncr[0, qt * 256 + m * 128:qt * 256 + (m + 1) * 128] = -hi[qt * 128:(qt + 1) * 128]
                ncr[1, qt * 256 + m * 128:qt * 256 + (m + 1) * 128] = -lo2[qt * 128:(qt + 1) * 128]
        in_maps.append({
            "x": xc, "wq": wq, "wk": wk, "wv": wv, "wo": wo, "bo": bo,
            "ropec": cosT, "ropes": sinT,
            "ncr": ncr.astype(ml_dtypes.bfloat16),
            "w1": w1, "w3": w3, "w2t": w2t,
        })
    return in_maps


NRUN = NCORES


def kernel(**inputs):
    in_maps = _host_prep(**inputs)[:NRUN]
    nc = build_program()
    res = run_bass_kernel_spmd(nc, in_maps, core_ids=list(range(NRUN)))
    if NRUN < NCORES:
        return np.zeros((1, SEQ, D), np.float32)
    out = np.concatenate([res.results[c]["out"] for c in range(NCORES)], 0)
    return out.reshape(1, SEQ, D)
